# version 06
# baseline (speedup 1.0000x reference)
import numpy as np
import concourse.bass as bass
import concourse.mybir as mybir
from concourse.bass_utils import run_bass_kernel_spmd

F32 = mybir.dt.float32
BF16 = mybir.dt.bfloat16
FP8 = mybir.dt.float8e4
AF = mybir.ActivationFunctionType
ALU = mybir.AluOpType

T = 4096
D = 1024
NDC = 8
H = 16
CH = 128
ST0 = 256
NS0 = T // ST0
STF = 512
ST1 = 256
NLEV = 7

DT_SZ = {F32: 4, BF16: 2, FP8: 1}


class Op:
    __slots__ = ("eng", "fn", "reads", "writes", "dma", "idx", "deps", "inc",
                 "cum", "waits", "tag", "lane", "force")

    def __init__(self, eng, fn, reads, writes, dma, idx, tag):
        self.eng, self.fn = eng, fn
        self.reads, self.writes = list(reads), list(writes)
        self.dma, self.idx, self.tag = dma, idx, tag
        self.deps = []
        self.inc = False
        self.cum = 0
        self.waits = []
        self.lane = 0
        self.force = False


class Sched:
    ENGS = ("te", "dve", "act", "gp", "sy")

    def __init__(self):
        self.ops = []
        self.res = {}
        self.last_on = {e: None for e in self.ENGS}
        self.bar = {e: None for e in self.ENGS}
        self.te_key = None
        self.n_drains = 0

    def drain(self, eng):
        last = self.last_on[eng]
        if last is None:
            return
        o = Op(eng, lambda e: None, [], [], False, len(self.ops), "drain")
        o.deps = [last]
        o.force = True
        self.ops.append(o)
        self.n_drains += 1

    def op(self, eng, fn, reads=(), writes=(), dma=False, tag="", mmkey=None):
        if eng == "te" and mmkey is not None:
            if self.te_key is not None and self.te_key != mmkey:
                self.drain("te")
            self.te_key = mmkey
        idx = len(self.ops)
        o = Op(eng, fn, reads, writes, dma, idx, tag)
        deps = set()
        for r in o.reads:
            e = self.res.get(r)
            if e and e["w"] is not None:
                deps.add(e["w"])
        for w in o.writes:
            e = self.res.get(w)
            if e:
                if e["w"] is not None:
                    deps.add(e["w"])
                deps.update(e["r"])
        for e2, i2 in self.bar.items():
            if i2 is not None and e2 != eng:
                deps.add(i2)
        deps.discard(idx)
        o.deps = sorted(deps)
        for r in o.reads:
            self.res.setdefault(r, {"w": None, "r": []})["r"].append(idx)
        for w in o.writes:
            self.res[w] = {"w": idx, "r": []}
        self.ops.append(o)
        self.last_on[eng] = idx
        return idx

    def barrier(self):
        self.drain("te")
        self.te_key = None
        self.bar = dict(self.last_on)

    NLANES = 4

    @staticmethod
    def _semkey(op):
        if op.dma:
            return (op.eng, "d", op.lane)
        return (op.eng, "c")

    def materialize(self, blk, sems):
        ops = self.ops
        def exempt(dp, o):
            if o.force:
                return False
            if dp.eng != o.eng:
                return False
            if dp.dma and o.dma:
                return True
            if dp.dma or o.dma:
                return o.eng != "gp" and not dp.dma
            return o.eng != "gp"
        for o in ops:
            for d in o.deps:
                dp = ops[d]
                if exempt(dp, o):
                    continue
                dp.inc = True
        counters = {}
        qcount = {}
        for o in ops:
            if o.dma:
                qi = qcount.get(o.eng, 0)
                qcount[o.eng] = qi + 1
                o.lane = qi % self.NLANES
                k = self._semkey(o)
                counters[k] = counters.get(k, 0) + 16
                o.cum = counters[k]
                o.inc = True
            elif o.inc:
                k = self._semkey(o)
                counters[k] = counters.get(k, 0) + 1
                o.cum = counters[k]
        marks = {e: {} for e in self.ENGS}
        for o in ops:
            need = {}
            for d in o.deps:
                dp = ops[d]
                if exempt(dp, o):
                    continue
                k = self._semkey(dp)
                if dp.cum > need.get(k, 0):
                    need[k] = dp.cum
            if o.dma and o.cum > 16:
                k = self._semkey(o)
                if o.cum - 16 > need.get(k, 0):
                    need[k] = o.cum - 16
            wm = marks[o.eng]
            for k, v in need.items():
                if v > wm.get(k, 0):
                    o.waits.append((k, v))
                    wm[k] = v

        def run_engine(ename):
            def body(eng):
                for o in ops:
                    if o.eng != ename:
                        continue
                    for (k, v) in o.waits:
                        eng.wait_ge(sems[k], v)
                    if o.tag == "drain":
                        continue
                    inst = o.fn(eng)
                    if o.inc:
                        if inst is None:
                            raise RuntimeError(f"op {o.idx} {o.tag}: no inst for inc")
                        inst.then_inc(sems[self._semkey(o)], 16 if o.dma else 1)
            return body
        blk.tensor(run_engine("te"))
        blk.vector(run_engine("dve"))
        blk.scalar(run_engine("act"))
        blk.gpsimd(run_engine("gp"))
        blk.sync(run_engine("sy"))


class Arena:
    def __init__(self, ap, total_bytes):
        self.ap = ap
        self.total = total_bytes
        self.off = 0
        self.marks = []

    def alloc(self, cols, dt):
        nbytes = cols * DT_SZ[dt]
        pad = (nbytes + 3) & ~3
        assert self.off + pad <= self.total, f"arena overflow {self.off}+{pad}"
        a = self.off
        self.off += pad
        view = self.ap[:, a // 2:(a + nbytes + 1) // 2]
        if dt != BF16:
            view = view.bitcast(dt)
        return view

    def mark(self):
        self.marks.append(self.off)

    def pop(self):
        self.off = self.marks.pop()


def build(phases=("l0a", "l0f", "l1a", "l1f"), ns0=NS0, l0a_parts=("stats", "proj", "beta", "norms", "chunks", "wout"), nchunks=99, chunk_parts=("xinit", "grams", "solve", "uk", "m1", "o", "ds")):
    nc = bass.Bass("TRN2", target_bir_lowering=False, detect_race_conditions=False)
    dram = {}

    def din(name, shape):
        dram[name] = nc.dram_tensor(name, list(shape), F32, kind="ExternalInput").ap()
        return dram[name]

    x_fm = din("x_fm", [D, T])
    wq0 = din("wq0", [D, D]); wk0 = din("wk0", [D, D]); wv0 = din("wv0", [D, D])
    wb0 = din("wb0", [D, H]); wo0 = din("wo0", [D, D])
    wf8 = din("wf8", [D, D])
    wq1 = din("wq1", [D, D]); wk1 = din("wk1", [D, D]); wv1 = din("wv1", [D, D])
    wfb = din("wfb", [2 * D, D]); wo1 = din("wo1", [D, D])
    wr2 = din("wr2", [D, D]); wk2 = din("wk2", [D, D]); wv2 = din("wv2", [D, D])
    pfsrc = {}
    for nm in ("bf0", "bq1", "bk1", "bv1", "bfb", "bo1", "g1", "b1", "g2", "b2",
               "mur", "muk"):
        pfsrc[nm] = din(nm, [D])
    bb0_h = din("bb0v", [H])
    y_fm = nc.dram_tensor("y_fm", [D, T], F32, kind="ExternalOutput").ap()

    sch = Sched()
    OP = sch.op
    ctxs = []

    def sb(name, shape, dt):
        cm = nc.sbuf_tensor(name, list(shape), dt)
        t = cm.__enter__()
        ctxs.append(cm)
        return t

    def psm(name, shape, dt):
        cm = nc.psum_tensor(name, list(shape), dt)
        t = cm.__enter__()
        ctxs.append(cm)
        return t

    xres = sb("xres", [128, NDC * T], BF16)
    idn = sb("idn", [128, 128], BF16)
    maskSUL = sb("maskSUL", [128, 256], BF16)
    maskSU = maskSUL[:, 0:128]
    maskSL = maskSUL[:, 128:256]
    maskIU = sb("maskIU", [128, 128], BF16)
    onesc128 = sb("onesc128", [128, 1], BF16)
    onesc1 = sb("onesc1", [1, 128], BF16)
    onesrow = sb("onesrow", [1, 512], BF16)
    hmap = sb("hmap", [16, NDC * 128], BF16)
    pmap = sb("pmap", [128, NDC * 16], BF16)
    bb0r = sb("bb0r", [1, H], BF16)
    bb0c = sb("bb0c", [H, 2], F32)
    cvals = sb("cvals", [128, 4], F32)
    pf = {nm: sb("pf_" + nm, [128, NDC], F32) for nm in pfsrc}
    prow = {}
    hmapRbox = [None]
    arena_ap = sb("arena", [128, 70500], BF16)
    arena = Arena(arena_ap, 141000)
    pbw = [psm(f"pbw{j}", [128, 1024], F32) for j in range(4)]
    pb = [pbw[i // 2][:, (i % 2) * 512:(i % 2) * 512 + 512] for i in range(8)]

    sems = {}
    semkeys = [("te", "c"), ("dve", "c"), ("act", "c"), ("gp", "c"), ("sy", "c")]
    for q in ("sy", "gp", "act"):
        for ln in range(Sched.NLANES):
            semkeys.append((q, "d", ln))
    for k in semkeys:
        cm = nc.semaphore("sem_" + "_".join(str(x) for x in k))
        sems[k] = cm.__enter__()
        ctxs.append(cm)

    def emit_consts():
        OP("dve", lambda e: e.memset(idn[:], 0.0), writes=["idn"], tag="c_idn0")
        OP("gp", lambda g: g.affine_select(out=idn[:], in_=idn[:],
                                           compare_op=ALU.not_equal, fill=1.0, base=0,
                                           pattern=[[-1, 128]], channel_multiplier=1),
           reads=["idn"], writes=["idn"], tag="c_idn")

        OP("dve", lambda e: e.memset(maskSU[:], -1.0), writes=["maskSU"], tag="c_msu0")
        OP("gp", lambda g: g.affine_select(out=maskSU[:], in_=maskSU[:],
                                           compare_op=ALU.is_gt, fill=0.0, base=0,
                                           pattern=[[1, 128]], channel_multiplier=-1),
           reads=["maskSU"], writes=["maskSU"], tag="c_msu")

        OP("dve", lambda e: e.memset(maskIU[:], 1.0), writes=["maskIU"], tag="c_miu0")
        OP("gp", lambda g: g.affine_select(out=maskIU[:], in_=maskIU[:],
                                           compare_op=ALU.is_ge, fill=0.0, base=0,
                                           pattern=[[1, 128]], channel_multiplier=-1),
           reads=["maskIU"], writes=["maskIU"], tag="c_miu")
        OP("dve", lambda e: e.memset(maskSL[:], -1.0), writes=["maskSL"], tag="c_msl0")
        OP("gp", lambda g: g.affine_select(out=maskSL[:], in_=maskSL[:],
                                           compare_op=ALU.is_gt, fill=0.0, base=0,
                                           pattern=[[-1, 128]], channel_multiplier=1),
           reads=["maskSL"], writes=["maskSL"], tag="c_msl")

        OP("dve", lambda e: e.memset(onesc128[:], 1.0), writes=["onesc128"])
        OP("dve", lambda e: e.memset(onesc1[:], 1.0), writes=["onesc1"])
        OP("dve", lambda e: e.memset(onesrow[:], 1.0), writes=["onesrow"])
        def c_cv(e):
            e.memset(cvals[:, 0:1], 1e-5)
            e.memset(cvals[:, 1:2], 1e-24)
            return e.memset(cvals[:, 2:3], 1e-6)
        OP("dve", c_cv, writes=["cvals"], tag="c_cv")

        OP("dve", lambda e: e.memset(hmap[:], 1.0), writes=["hmap"], tag="c_hmap0")
        for dc in range(NDC):
            for half in range(2):
                sl = hmap[:, dc * 128 + half * 64: dc * 128 + (half + 1) * 64]
                OP("gp", lambda g, sl=sl, dc=dc, half=half: g.affine_select(
                    out=sl, in_=sl, compare_op=ALU.is_equal, fill=0.0,
                    base=-(2 * dc + half), pattern=[[0, 64]], channel_multiplier=1),
                   reads=["hmap"], writes=["hmap"], tag="c_hmap")

        OP("dve", lambda e: e.memset(pmap[:], 1.0), writes=["pmap"], tag="c_pmap0")
        for dc in range(NDC):
            for half in range(2):
                sl = pmap[64 * half:64 * (half + 1), dc * 16:(dc + 1) * 16]
                OP("gp", lambda g, sl=sl, dc=dc, half=half: g.affine_select(
                    out=sl, in_=sl, compare_op=ALU.is_equal, fill=0.0,
                    base=-(2 * dc + half), pattern=[[1, 16]], channel_multiplier=0),
                   reads=["pmap"], writes=["pmap"], tag="c_pmap")

    def emit_param_loads():
        OP("gp", lambda g: g.dma_start(out=xres[:].rearrange("p (c t) -> p c t", c=NDC),
                                       in_=x_fm.rearrange("(c p) t -> p c t", p=128)),
           writes=[f"xres:{gg}" for gg in range(8)], dma=True, tag="ld_x")
        OP("gp", lambda g: g.dma_start(out=bb0r[:],
                                       in_=bb0_h.rearrange("(x h) -> x h", x=1)),
           writes=["bb0r"], dma=True)
        OP("gp", lambda g: g.dma_start(out=bb0c[:, 0:1],
                                       in_=bb0_h.rearrange("(h x) -> h x", x=1),
                                       allow_slow_non_contiguous=True),
           writes=["bb0c"], dma=True)
        OP("dve", lambda e: e.tensor_scalar_mul(bb0c[:, 1:2], bb0c[:, 0:1], -1.0),
           reads=["bb0c"], writes=["bb0c"], tag="bb0n")
        for nm, src in pfsrc.items():
            OP("sy", lambda g, src=src, nm=nm: g.dma_start(
                out=pf[nm][:], in_=src.rearrange("(c p) -> p c", p=128),
                allow_slow_non_contiguous=True),
               writes=["pf_" + nm], dma=True)


    def load_weight(wsb, src, name, kin=D):
        nkc = kin // 128
        OP("gp", lambda g: g.dma_start(
            out=wsb[:].rearrange("p (c o) -> p c o", c=nkc),
            in_=src.rearrange("(c p) o -> p c o", p=128)),
           writes=[name], dma=True, tag="ld_" + name)

    def xres3(t0, st, dc=None):
        if dc is None:
            return xres[:].rearrange("p (c t) -> p c t", c=NDC)[:, :, t0:t0 + st]
        return xres[:, dc * T + t0:dc * T + t0 + st]

    def gkeys(t0, st):
        g0, g1 = t0 // 512, (t0 + st - 1) // 512
        return [f"xres:{g}" for g in range(g0, g1 + 1)]

    def emit_stats(t0, st, src_fn, src_keys, sq, stt, eps, l1a_eps=None, gb=None):
        OP("act", lambda e: e.activation(
            sq[:].rearrange("p (c t) -> p c t", c=NDC)[:, :, 0:st],
            src_fn(), AF.Square),
           reads=src_keys, writes=["sq"], tag="st_sq")
        for dc in range(NDC):
            OP("te", lambda e, dc=dc: e.matmul(
                pb[0][0:1, 0:st], onesc128[:], src_fn(dc),
                start=(dc == 0), stop=(dc == NDC - 1)),
               reads=src_keys + ["onesc128"], writes=["pb0"], tag="st_s1",
               mmkey=(0, 128))
        for dc in range(NDC):
            OP("te", lambda e, dc=dc: e.matmul(
                pb[1][0:1, 0:st], onesc128[:],
                sq[:, dc * (sq.shape[1] // NDC): dc * (sq.shape[1] // NDC) + st],
                start=(dc == 0), stop=(dc == NDC - 1)),
               reads=["sq", "onesc128"], writes=["pb1"], tag="st_s2",
               mmkey=(0, 128))
        mu, va, tmp = stt["mu"], stt["va"], stt["tmp"]
        arow, srow = stt["arow"], stt["srow"]
        arow_bf, srow_bf = stt["arow_bf"], stt["srow_bf"]
        KMU = stt.get("k_mu", "murow")
        KVA = stt.get("k_va", "varow")
        KTMP = stt.get("k_tmp", "tmprow")
        KSR = stt.get("k_srow", "srow")
        KABF = stt.get("k_abf", "arow_bf")
        KSBF = stt.get("k_sbf", "srow_bf")
        OP("dve", lambda e: e.tensor_scalar_mul(mu[0:1, 0:st], pb[0][0:1, 0:st], 1.0 / D),
           reads=["pb0"], writes=[KMU], tag="st_mu")
        OP("dve", lambda e: e.tensor_scalar_mul(va[0:1, 0:st], pb[1][0:1, 0:st], 1.0 / D),
           reads=["pb1"], writes=[KVA], tag="st_ex2")
        OP("dve", lambda e: e.scalar_tensor_tensor(
            tmp[0:1, 0:st], mu[0:1, 0:st], -1.0, mu[0:1, 0:st], ALU.mult, ALU.mult),
           reads=[KMU], writes=[KTMP], tag="st_msq")
        OP("dve", lambda e: e.tensor_add(va[0:1, 0:st], va[0:1, 0:st], tmp[0:1, 0:st]),
           reads=[KVA, KTMP], writes=[KVA], tag="st_var")
        OP("act", lambda e: e.activation(tmp[0:1, 0:st], va[0:1, 0:st], AF.Ln,
                                         bias=cvals[0:1, eps:eps + 1]),
           reads=[KVA, "cvals"], writes=[KTMP], tag="st_ln")
        OP("act", lambda e: e.activation(arow[0:1, 0:st], tmp[0:1, 0:st], AF.Exp, scale=-0.5),
           reads=[KTMP], writes=["arow"], tag="st_rstd")
        if l1a_eps is not None:
            OP("dve", lambda e: e.scalar_tensor_tensor(
                tmp[0:1, 0:st], arow[0:1, 0:st], 1.0, arow[0:1, 0:st], ALU.mult, ALU.mult),
               reads=["arow"], writes=[KTMP], tag="dln_r2")
            OP("dve", lambda e: e.tensor_mul(tmp[0:1, 0:st], tmp[0:1, 0:st], va[0:1, 0:st]),
               reads=[KTMP, KVA], writes=[KTMP], tag="dln_w")
            OP("act", lambda e: e.activation(tmp[0:1, 0:st], tmp[0:1, 0:st], AF.Ln,
                                             bias=cvals[0:1, l1a_eps:l1a_eps + 1]),
               reads=[KTMP, "cvals"], writes=[KTMP], tag="dln_ln")
            OP("act", lambda e: e.activation(tmp[0:1, 0:st], tmp[0:1, 0:st], AF.Exp,
                                             scale=-0.5),
               reads=[KTMP], writes=[KTMP], tag="dln_u")
            OP("dve", lambda e: e.tensor_mul(arow[0:1, 0:st], arow[0:1, 0:st],
                                             tmp[0:1, 0:st]),
               reads=["arow", KTMP], writes=["arow"], tag="dln_a")
        OP("dve", lambda e: e.scalar_tensor_tensor(
            srow[0:1, 0:st], mu[0:1, 0:st], -1.0, arow[0:1, 0:st], ALU.mult, ALU.mult),
           reads=[KMU, "arow"], writes=[KSR], tag="st_srow")
        OP("act", lambda e: e.activation(arow_bf[0:1, 0:st], arow[0:1, 0:st], AF.Identity),
           reads=["arow"], writes=[KABF], tag="st_abf")
        OP("act", lambda e: e.activation(srow_bf[0:1, 0:st], srow[0:1, 0:st], AF.Identity),
           reads=[KSR], writes=[KSBF], tag="st_sbf")
        absc, sbsc = stt["absc"], stt["sbsc"]
        if gb is None:
            OP("te", lambda e: e.matmul(pb[0][:, 0:st], onesc1[:], arow_bf[0:1, 0:st],
                                        start=True, stop=True),
               reads=[KABF, "onesc1"], writes=["pb0"], tag="bc_a", mmkey=(0, 1))
            OP("te", lambda e: e.matmul(pb[1][:, 0:st], onesc1[:], srow_bf[0:1, 0:st],
                                        start=True, stop=True),
               reads=[KSBF, "onesc1"], writes=["pb1"], tag="bc_s", mmkey=(0, 1))
            OP("act", lambda e: e.activation(absc[:, 0:st], pb[0][:, 0:st], AF.Identity),
               reads=["pb0"], writes=["absc"], tag="bc_ae")
            OP("act", lambda e: e.activation(sbsc[:, 0:st], pb[1][:, 0:st], AF.Identity),
               reads=["pb1"], writes=["sbsc"], tag="bc_se")
        else:
            gname, bname = gb
            gcol, bcol = stt["gcolb"], stt["bcolb"]
            for dc in range(NDC):
                OP("te", lambda e, dc=dc: e.matmul(
                    pb[0][:, 0:st], gcol[:, dc:dc + 1], arow_bf[0:1, 0:st],
                    start=True, stop=True),
                   reads=["arow_bf", "gcolb"], writes=["pb0"], tag="bcg_a")
                OP("act", lambda e, dc=dc: e.activation(
                    absc[:, dc * st:(dc + 1) * st], pb[0][:, 0:st], AF.Identity),
                   reads=["pb0"], writes=["absc"], tag="bcg_ae")
                OP("te", lambda e, dc=dc: e.matmul(
                    pb[1][:, 0:st], gcol[:, dc:dc + 1], srow_bf[0:1, 0:st],
                    start=True, stop=False),
                   reads=["srow_bf", "gcolb"], writes=["pb1"], tag="bcg_s")
                OP("te", lambda e, dc=dc: e.matmul(
                    pb[1][:, 0:st], bcol[:, dc:dc + 1], onesrow[0:1, 0:st],
                    start=False, stop=True),
                   reads=["onesrow", "bcolb"], writes=["pb1"], tag="bcg_b")
                OP("act", lambda e, dc=dc: e.activation(
                    sbsc[:, dc * st:(dc + 1) * st], pb[1][:, 0:st], AF.Identity),
                   reads=["pb1"], writes=["sbsc"], tag="bcg_se")

    def emit_apply(st, src_fn, src_keys, stt, out_ap, out_key, lntmp, perdc=False,
                   out_dt_note=None):
        absc, sbsc = stt["absc"], stt["sbsc"]
        if not perdc:
            a_b = absc[:, 0:st].rearrange("p (c t) -> p c t", c=1).broadcast_to([128, NDC, st])
            s_b = sbsc[:, 0:st].rearrange("p (c t) -> p c t", c=1).broadcast_to([128, NDC, st])
        else:
            a_b = absc[:].rearrange("p (c t) -> p c t", c=NDC)
            s_b = sbsc[:].rearrange("p (c t) -> p c t", c=NDC)
        OP("dve", lambda e: e.tensor_tensor(lntmp, src_fn(), a_b, ALU.mult),
           reads=src_keys + ["absc"], writes=["sq"], tag="ap1")
        OP("dve", lambda e: e.tensor_tensor(out_ap, lntmp, s_b, ALU.add),
           reads=["sq", "sbsc"], writes=[out_key], tag="ap2")

    def mkstats(st, big=False):
        d = {}
        d["mu"] = arena.alloc(st, F32)
        d["va"] = arena.alloc(st, F32)
        d["tmp"] = arena.alloc(st, F32)
        d["arow"] = arena.alloc(st, F32)
        d["srow"] = arena.alloc(st, F32)
        d["arow_bf"] = arena.alloc(st, BF16)
        d["srow_bf"] = arena.alloc(st, BF16)
        n = NDC * st if big else st
        d["absc"] = arena.alloc(n, BF16)
        d["sbsc"] = arena.alloc(n, BF16)
        return d

    def emit_l0a():
        arena.mark()
        WQ = arena.alloc(NDC * D, BF16); load_weight(WQ, wq0, "WQ0")
        WK = arena.alloc(NDC * D, BF16); load_weight(WK, wk0, "WK0")
        WV = arena.alloc(NDC * D, BF16); load_weight(WV, wv0, "WV0")
        WO = arena.alloc(NDC * D, BF16); load_weight(WO, wo0, "WO0")
        WB = arena.alloc(NDC * H, BF16); load_weight(WB, wb0, "WB0")
        st = ST0
        xhat = arena.alloc(NDC * st, BF16)
        sq = arena.alloc(NDC * st, BF16)
        qdm = arena.alloc(NDC * st, BF16)
        kdm = arena.alloc(NDC * st, BF16)
        khdm = arena.alloc(NDC * st, BF16)
        kbdm = arena.alloc(NDC * st, BF16)
        ktok = arena.alloc(2 * D, BF16)
        khtok = arena.alloc(2 * D, BF16)
        vtok = arena.alloc(2 * D, BF16)
        odm = kdm
        beta_dm = arena.alloc(st, F32)
        beta_dmb = arena.alloc(st, BF16)
        beta_tok = arena.alloc(2 * 16, F32)
        beta_tokb = arena.alloc(2 * 16, BF16)
        rn_dm = arena.alloc(st, F32)
        rn_dmb = arena.alloc(st, BF16)
        rn_tokb = arena.alloc(2 * 16, BF16)
        brn_dmb = arena.alloc(st, BF16)
        stt = mkstats(st)
        lntmp = sq
        xall = arena.alloc(H * CH, BF16)
        npw = [arena.alloc(1024, BF16) for _ in range(3)] + [sq[:, 0:1024]]
        npow = []
        for c4 in range(4):
            npow += [npw[c4][:, 0:512], npw[c4][:, 512:1024]]
        npkey = [f"npow{i}" for i in range(6)] + ["sq", "sq"]
        ltall = [arena.alloc(H * CH, BF16), None]
        ltall[1] = xhat
        ltkeys = ["ltall0", "xhat"]
        ukdm = ktok[0:64, 0:H * CH]
        m1sb = arena.alloc(H * 64, BF16)
        m1b = arena.alloc(H * 64, BF16)
        oasb = arena.alloc(512, BF16)
        s32 = arena.alloc(H * 64, F32)
        sb16 = arena.alloc(H * 64, BF16)
        OP("dve", lambda e: e.memset(s32[:], 0.0), writes=["s32"], tag="s_init")
        OP("dve", lambda e: e.memset(sb16[:], 0.0), writes=["sb16"], tag="sb_init")

        for s in range(ns0):
            t0 = s * st
            xk = gkeys(t0, st)
            if "stats" not in l0a_parts:
                continue
            emit_stats(t0, st, lambda dc=None, t0=t0: xres3(t0, st, dc), xk, sq, stt, 0)
            emit_apply(st, lambda t0=t0: xres3(t0, st), xk, stt,
                       xhat[:].rearrange("p (c t) -> p c t", c=NDC), "xhat",
                       lntmp[:].rearrange("p (c t) -> p c t", c=NDC))

            def proj_fm(wsb, wkey, out, okey):
                for oc in range(NDC):
                    bk = 2 + (oc % 2)
                    for kc in range(NDC):
                        OP("te", lambda e, oc=oc, kc=kc, bk=bk, wsb=wsb: e.matmul(
                            pb[bk][:, 0:st],
                            wsb[:, kc * D + oc * 128: kc * D + (oc + 1) * 128],
                            xhat[:, kc * st:(kc + 1) * st],
                            start=(kc == 0), stop=(kc == NDC - 1)),
                           reads=[wkey, "xhat"], writes=[f"pb{bk}"], tag=f"pj_{okey}",
                           mmkey=(0, 128))
                    OP("act", lambda e, oc=oc, bk=bk, out=out: e.activation(
                        out[:, oc * st:(oc + 1) * st], pb[bk][:, 0:st], AF.Identity),
                       reads=[f"pb{bk}"], writes=[okey], tag=f"pje_{okey}")
            if "proj" not in l0a_parts:
                continue
            proj_fm(WQ, "WQ0", qdm, "qdm")
            proj_fm(WK, "WK0", kdm, "kdmov")

            def proj_tok(wsb, wkey, out, okey):
                for tt in range(st // 128):
                    for oh in range(2):
                        bk = 2 + ((tt * 2 + oh) % 2)
                        for kc in range(NDC):
                            OP("te", lambda e, tt=tt, oh=oh, kc=kc, bk=bk, wsb=wsb: e.matmul(
                                pb[bk][:, 0:512],
                                xhat[:, kc * st + tt * 128: kc * st + (tt + 1) * 128],
                                wsb[:, kc * D + oh * 512: kc * D + (oh + 1) * 512],
                                start=(kc == 0), stop=(kc == NDC - 1)),
                               reads=[wkey, "xhat"], writes=[f"pb{bk}"], tag=f"pt_{okey}",
                               mmkey=(0, 128))
                        OP("act", lambda e, tt=tt, oh=oh, bk=bk, out=out: e.activation(
                            out[:, tt * D + oh * 512: tt * D + (oh + 1) * 512],
                            pb[bk][:, 0:512], AF.Identity),
                           reads=[f"pb{bk}"], writes=[okey], tag=f"pte_{okey}")
            proj_tok(WK, "WK0", ktok, "ktokov")
            proj_tok(WV, "WV0", vtok, "vtok")

            if "beta" not in l0a_parts:
                continue
            for kc in range(NDC):
                OP("te", lambda e, kc=kc: e.matmul(
                    pb[0][0:H, 0:st], WB[:, kc * H:(kc + 1) * H],
                    xhat[:, kc * st:(kc + 1) * st], start=(kc == 0),
                    stop=(kc == NDC - 1)),
                   reads=["WB0", "xhat"], writes=["pb0"], tag="bdm", mmkey=(0, 128))
            OP("act", lambda e: e.activation(beta_dm[0:H, 0:st], pb[0][0:H, 0:st],
                                             AF.Exp, scale=-1.0, bias=bb0c[0:H, 1:2]),
               reads=["pb0", "bb0c"], writes=["beta_dm"], tag="bdm_e")
            OP("dve", lambda e: e.tensor_scalar_add(beta_dm[0:H, 0:st],
                                                    beta_dm[0:H, 0:st], 1.0),
               reads=["beta_dm"], writes=["beta_dm"], tag="bdm_1")
            OP("act", lambda e: e.activation(beta_dm[0:H, 0:st], beta_dm[0:H, 0:st], AF.Ln),
               reads=["beta_dm"], writes=["beta_dm"], tag="bdm_l")
            OP("act", lambda e: e.activation(beta_dm[0:H, 0:st], beta_dm[0:H, 0:st],
                                             AF.Exp, scale=-1.0),
               reads=["beta_dm"], writes=["beta_dm"], tag="bdm_x")
            OP("act", lambda e: e.activation(beta_dmb[0:H, 0:st], beta_dm[0:H, 0:st],
                                             AF.Identity),
               reads=["beta_dm"], writes=["beta_dmb"], tag="bdm_bf")
            for tt in range(st // 128):
                OP("te", lambda e, tt=tt: e.matmul(
                    pb[1][:, tt * 16:(tt + 1) * 16],
                    beta_dmb[0:H, tt * 128:(tt + 1) * 128], idn[0:H, 0:H],
                    start=True, stop=True),
                   reads=["beta_dmb", "idn"], writes=["pb1"], tag="btok_t",
                   mmkey=(0, 16))
            OP("act", lambda e: e.activation(beta_tok[:, 0:32], pb[1][:, 0:32],
                                             AF.Identity),
               reads=["pb1"], writes=["beta_tok"], tag="btok_e")
            OP("act", lambda e: e.activation(beta_tokb[:, 0:32], pb[1][:, 0:32],
                                             AF.Identity),
               reads=["pb1"], writes=["beta_tokb"], tag="btok_bf")

            if "norms" not in l0a_parts:
                continue
            OP("act", lambda e: e.activation(
                sq[:].rearrange("p (c t) -> p c t", c=NDC),
                kdm[:].rearrange("p (c t) -> p c t", c=NDC), AF.Square),
               reads=["kdmov"], writes=["sq"], tag="kn_sq")
            for dc in range(NDC):
                OP("te", lambda e, dc=dc: e.matmul(
                    pb[0][0:H, 0:st], pmap[:, dc * 16:(dc + 1) * 16],
                    sq[:, dc * st:(dc + 1) * st],
                    start=(dc == 0), stop=(dc == NDC - 1)),
                   reads=["sq", "pmap"], writes=["pb0"], tag="kn_mm", mmkey=(0, 128))
            OP("act", lambda e: e.activation(rn_dm[0:H, 0:st], pb[0][0:H, 0:st],
                                             AF.Ln, bias=cvals[0:H, 1:2]),
               reads=["pb0", "cvals"], writes=["rn_dm"], tag="kn_ln")
            OP("act", lambda e: e.activation(rn_dm[0:H, 0:st], rn_dm[0:H, 0:st],
                                             AF.Exp, scale=-0.5),
               reads=["rn_dm"], writes=["rn_dm"], tag="kn_exp")
            OP("act", lambda e: e.activation(rn_dmb[0:H, 0:st], rn_dm[0:H, 0:st],
                                             AF.Identity),
               reads=["rn_dm"], writes=["rn_dmb"], tag="kn_bf")
            OP("dve", lambda e: e.tensor_mul(brn_dmb[0:H, 0:st], rn_dmb[0:H, 0:st],
                                             beta_dmb[0:H, 0:st]),
               reads=["rn_dmb", "beta_dmb"], writes=["brn_dmb"], tag="kn_brn")
            for tt in range(st // 128):
                OP("te", lambda e, tt=tt: e.matmul(
                    pb[1][:, tt * 16:(tt + 1) * 16],
                    rn_dmb[0:H, tt * 128:(tt + 1) * 128], idn[0:H, 0:H],
                    start=True, stop=True),
                   reads=["rn_dmb", "idn"], writes=["pb1"], tag="rnt_t", mmkey=(0, 16))
            OP("act", lambda e: e.activation(
                rn_tokb[:, 0:32], pb[1][:, 0:32], AF.Identity),
               reads=["pb1"], writes=["rn_tokb"], tag="rnt_e")
            for dc in range(NDC):
                OP("te", lambda e, dc=dc: e.matmul(
                    pb[0][:, 0:st], hmap[:, dc * 128:(dc + 1) * 128], rn_dmb[0:H, 0:st],
                    start=True, stop=True),
                   reads=["rn_dmb", "hmap"], writes=["pb0"], tag="khdm_bc",
                   mmkey=(0, 16))
                OP("dve", lambda e, dc=dc: e.tensor_tensor(
                    khdm[:, dc * st:(dc + 1) * st], kdm[:, dc * st:(dc + 1) * st],
                    pb[0][:, 0:st], ALU.mult),
                   reads=["kdmov", "pb0"], writes=["khdm"], tag="khdm_tt")
                OP("te", lambda e, dc=dc: e.matmul(
                    pb[1][:, 0:st], hmap[:, dc * 128:(dc + 1) * 128], brn_dmb[0:H, 0:st],
                    start=True, stop=True),
                   reads=["brn_dmb", "hmap"], writes=["pb1"], tag="kbdm_bc",
                   mmkey=(0, 16))
                OP("dve", lambda e, dc=dc: e.tensor_tensor(
                    kbdm[:, dc * st:(dc + 1) * st], kdm[:, dc * st:(dc + 1) * st],
                    pb[1][:, 0:st], ALU.mult),
                   reads=["kdmov", "pb1"], writes=["kbdm"], tag="kbdm_tt")
            for tt in range(st // 128):
                rnb = rn_tokb[:, tt * 16:(tt + 1) * 16].rearrange(
                    "p (h x) -> p h x", x=1).broadcast_to([128, 16, 64])
                OP("dve", lambda e, tt=tt, rnb=rnb: e.tensor_tensor(
                    khtok[:, tt * D:(tt + 1) * D].rearrange("p (h d) -> p h d", d=64),
                    ktok[:, tt * D:(tt + 1) * D].rearrange("p (h d) -> p h d", d=64),
                    rnb, ALU.mult),
                   reads=["ktokov", "rn_tokb"], writes=["khtok"], tag="khtok_tt")

            if "chunks" not in l0a_parts:
                continue
            FULLCP = ("xinit", "grams", "solve", "uk", "m1", "o", "ds")
            nch = min(st // CH, nchunks)
            if tuple(chunk_parts) == FULLCP and nch == 2:
                def ec(ci, cp):
                    emit_chunk(s, ci, qdm, kbdm, khdm, khtok, vtok, odm,
                               beta_tokb, xall, npow, npkey, npw, ltall[ci % 2],
                               ltkeys[ci % 2], ukdm, m1sb, m1b, oasb, s32,
                               sb16, st, cp)
                ec(0, ("xinit", "grams", "solve", "uk", "m1"))
                ec(1, ("xinit", "grams", "solve"))
                ec(0, ("o", "ds"))
                ec(1, ("uk", "m1"))
                ec(1, ("o", "ds"))
            else:
                for ci in range(nch):
                    emit_chunk(s, ci, qdm, kbdm, khdm, khtok, vtok, odm,
                               beta_tokb, xall, npow, npkey, npw, ltall[ci % 2],
                               ltkeys[ci % 2], ukdm, m1sb, m1b, oasb, s32,
                               sb16, st, chunk_parts)

            if "wout" not in l0a_parts:
                continue
            for oc in range(NDC):
                bk = 2 + (oc % 2)
                for kc in range(NDC):
                    OP("te", lambda e, oc=oc, kc=kc, bk=bk: e.matmul(
                        pb[bk][:, 0:st],
                        WO[:, kc * D + oc * 128: kc * D + (oc + 1) * 128],
                        odm[:, kc * st:(kc + 1) * st],
                        start=(kc == 0), stop=(kc == NDC - 1)),
                       reads=["WO0", "kdmov"], writes=[f"pb{bk}"], tag="wo",
                       mmkey=(0, 128))
                OP("dve", lambda e, oc=oc, bk=bk, t0=t0: e.tensor_tensor(
                    xres[:, oc * T + t0: oc * T + t0 + st],
                    xres[:, oc * T + t0: oc * T + t0 + st],
                    pb[bk][:, 0:st], ALU.add),
                   reads=[f"pb{bk}"] + xk, writes=xk, tag="res0")
        arena.pop()

    def emit_chunk(s, ci, qdm, kbdm, khdm, khtok, vtok, odm, beta_tokb,
                   xall, npow, npkey, npw, ltc, ltkey, ukdm, m1sb, m1b, oasb, s32, sb16, st,
                   cp=("xinit", "grams", "solve", "uk", "m1", "o", "ds")):
        tc = ci * CH
        if "xinit" in cp:
          OP("act", lambda e: e.activation(
            xall[:].rearrange("p (h w) -> p h w", h=H)[:, :, 0:64],
            vtok[:, ci * D:(ci + 1) * D].rearrange("p (h d) -> p h d", d=64),
            AF.Identity),
             reads=["vtok"], writes=["xall"], tag=f"xiV{s}_{ci}")
          OP("act", lambda e: e.activation(
            xall[:].rearrange("p (h w) -> p h w", h=H)[:, :, 64:128],
            khtok[:, ci * D:(ci + 1) * D].rearrange("p (h d) -> p h d", d=64),
            AF.Identity),
             reads=["khtok"], writes=["xall"], tag=f"xiK{s}_{ci}")
        chains = []
        for par in range(2):
            groups = [[par + 2 * i for i in range(4)],
                      [8 + par + 2 * i for i in range(4)]]
            for gg, hs in enumerate(groups):
                chains.append((2 * par + gg, 64 * par, hs))
        if "grams" in cp:
            for cid, b0, hs in chains:
                for q, h in enumerate(hs):
                    dc = h // 2
                    OP("te", lambda e, q=q, dc=dc, b0=b0: e.matmul(
                        pb[6][:, q * 128:(q + 1) * 128],
                        kbdm[b0:b0 + 64, dc * st + tc: dc * st + tc + CH],
                        khdm[b0:b0 + 64, dc * st + tc: dc * st + tc + CH],
                        start=True, stop=True),
                       reads=["kbdm", "khdm"], writes=["pb6"], tag="grA",
                       mmkey=(b0, 64))
                for q, h in enumerate(hs):
                    dc = h // 2
                    OP("te", lambda e, q=q, dc=dc, b0=b0: e.matmul(
                        pb[7][:, q * 128:(q + 1) * 128],
                        khdm[b0:b0 + 64, dc * st + tc: dc * st + tc + CH],
                        kbdm[b0:b0 + 64, dc * st + tc: dc * st + tc + CH],
                        start=True, stop=True),
                       reads=["kbdm", "khdm"], writes=["pb7"], tag="grAT",
                       mmkey=(b0, 64))
                msul = maskSUL[:].rearrange("p (c u t) -> p c u t", c=2,
                                            u=1).broadcast_to([128, 2, 4, 128])
                OP("dve", lambda e, cid=cid, msul=msul: e.tensor_tensor(
                    npw[cid][:].rearrange("p (c q t) -> p c q t", c=2, q=4),
                    pbw[3][:, 0:1024].rearrange("p (c q t) -> p c q t", c=2, q=4),
                    msul, ALU.mult),
                   reads=["pb6", "pb7", "maskSU", "maskSL"],
                   writes=[npkey[2 * cid], npkey[2 * cid + 1]], tag="grNTe")
                for q, h in enumerate(hs):
                    dc = h // 2
                    OP("te", lambda e, q=q, dc=dc, b0=b0: e.matmul(
                        pb[3][:, q * 128:(q + 1) * 128],
                        kbdm[b0:b0 + 64, dc * st + tc: dc * st + tc + CH],
                        qdm[b0:b0 + 64, dc * st + tc: dc * st + tc + CH],
                        start=True, stop=True),
                       reads=["kbdm", "qdm"], writes=["pb3"], tag="grL",
                       mmkey=(b0, 64))
                mbi = maskIU[:].rearrange("p (c t) -> p c t", c=1).broadcast_to(
                    [128, 4, 128])
                j0, po = hs[0] // 2, (hs[0] % 2) * 128
                OP("dve", lambda e, j0=j0, po=po, mbi=mbi: e.tensor_tensor(
                    ltc[:].rearrange("p (hp x) -> p hp x", x=256)[
                        :, j0:j0 + 4, po:po + 128],
                    pb[3][:, 0:512].rearrange("p (q t) -> p q t", q=4), mbi, ALU.mult),
                   reads=["pb3", "maskIU"], writes=[ltkey], tag="grLe")
        if "solve" in cp:
            for k in range(NLEV):
                for cid, b0, hs in chains:
                    h0 = hs[0]
                    cur, ckey = npow[2 * cid], npkey[2 * cid]
                    curT, ckeyT = npow[2 * cid + 1], npkey[2 * cid + 1]
                    ab = 4 if cid % 2 == 0 else 5
                    nb, tb = (6, 7) if cid % 2 == 0 else (0, 1)
                    sqw = pbw[3] if cid % 2 == 0 else pbw[0]
                    for q, h in enumerate(hs):
                        xsl = xall[:, h * 128:(h + 1) * 128]
                        OP("te", lambda e, q=q, xsl=xsl, cur=cur, ab=ab: e.matmul(
                            pb[ab][:, q * 128:(q + 1) * 128],
                            cur[:, q * 128:(q + 1) * 128], xsl,
                            start=True, stop=True),
                           reads=["xall", ckey], writes=[f"pb{ab}"], tag="apN",
                           mmkey=(0, 128))
                    j0, po = h0 // 2, (h0 % 2) * 128
                    xsl4 = xall[:].rearrange("p (hp x) -> p hp x", x=256)[
                        :, j0:j0 + 4, po:po + 128]
                    OP("dve", lambda e, xsl4=xsl4, ab=ab: e.tensor_tensor(
                        xsl4, xsl4,
                        pb[ab][:, 0:512].rearrange("p (q t) -> p q t", q=4), ALU.add),
                       reads=[f"pb{ab}", "xall"], writes=["xall"], tag="apE")
                    if k < NLEV - 1:
                        for q in range(4):
                            OP("te", lambda e, q=q, cur=cur, curT=curT, nb=nb:
                               e.matmul(
                                pb[nb][:, q * 128:(q + 1) * 128],
                                curT[:, q * 128:(q + 1) * 128],
                                cur[:, q * 128:(q + 1) * 128], start=True, stop=True),
                               reads=[ckey, ckeyT], writes=[f"pb{nb}"], tag="sqN",
                               mmkey=(0, 128))
                            OP("te", lambda e, q=q, cur=cur, curT=curT, tb=tb:
                               e.matmul(
                                pb[tb][:, q * 128:(q + 1) * 128],
                                cur[:, q * 128:(q + 1) * 128],
                                curT[:, q * 128:(q + 1) * 128], start=True, stop=True),
                               reads=[ckey, ckeyT], writes=[f"pb{tb}"], tag="sqT",
                               mmkey=(0, 128))
                        eng = "dve" if cid == 0 else "act"
                        if eng == "dve":
                            OP("dve", lambda e, cid=cid, sqw=sqw: e.tensor_copy(
                                npw[cid][:].rearrange("p (q t) -> p q t", q=8),
                                sqw[:, 0:1024].rearrange("p (q t) -> p q t", q=8)),
                               reads=[f"pb{nb}", f"pb{tb}"],
                               writes=[ckey, ckeyT], tag="sqNTe")
                        else:
                            OP("act", lambda e, cid=cid, sqw=sqw: e.activation(
                                npw[cid][:].rearrange("p (q t) -> p q t", q=8),
                                sqw[:, 0:1024].rearrange("p (q t) -> p q t", q=8),
                                AF.Identity),
                               reads=[f"pb{nb}", f"pb{tb}"],
                               writes=[ckey, ckeyT], tag="sqNTe")
        if "uk" in cp:
          for h in range(H):
            q = h % 4
            OP("te", lambda e, h=h, q=q: e.matmul(
                pb[3][0:64, q * 128:(q + 1) * 128],
                xall[:, h * 128 + 64:(h + 1) * 128], idn[:],
                start=True, stop=True),
               reads=["xall", "idn"], writes=["pb3"], tag="ukT", mmkey=(0, 128))
            if q == 3:
                OP("act", lambda e, h=h: e.activation(
                    ukdm[:, (h - 3) * 128:(h + 1) * 128],
                    pb[3][0:64, 0:512], AF.Identity),
                   reads=["pb3"], writes=["ktokov"], tag="ukTe")
        if "m1" in cp:
          for w in range(2):
            for i in range(8):
                h = w * 8 + i
                OP("te", lambda e, h=h, i=i: e.matmul(
                    pb[2][:, i * 64:(i + 1) * 64],
                    ukdm[:, h * 128:(h + 1) * 128],
                    sb16[0:64, h * 64:(h + 1) * 64],
                    start=True, stop=True),
                   reads=["ktokov", "sb16"], writes=["pb2"], tag="m1", mmkey=(0, 64))
            vsl = xall[:].rearrange("p (h w) -> p h w", h=H)[:, w * 8:(w + 1) * 8, 0:64]
            OP("dve", lambda e, w=w, vsl=vsl: e.tensor_tensor(
                m1sb[:, w * 512:(w + 1) * 512].rearrange("p (h v) -> p h v", h=8),
                vsl, pb[2][:, 0:512].rearrange("p (h v) -> p h v", h=8), ALU.subtract),
               reads=["pb2", "xall"], writes=["m1sb"], tag="m1e")
          bb = beta_tokb[:, ci * 16:(ci + 1) * 16].rearrange(
              "p (h x) -> p h x", x=1).broadcast_to([128, 16, 64])
          OP("dve", lambda e, bb=bb: e.tensor_tensor(
              m1b[:].rearrange("p (h v) -> p h v", h=H),
              m1sb[:].rearrange("p (h v) -> p h v", h=H), bb, ALU.mult),
             reads=["m1sb", "beta_tokb"], writes=["m1b"], tag="m1b")
        if "o" not in cp:
            return
        for w in range(2):
            for par in range(2):
                b0 = 64 * par
                for pr in range(4):
                    h = w * 8 + 2 * pr + par
                    OP("te", lambda e, h=h, b0=b0, pr=pr, dc=h // 2: e.matmul(
                        pb[3][b0:b0 + 64, pr * 128:(pr + 1) * 128],
                        sb16[b0:b0 + 64, h * 64:(h + 1) * 64],
                        qdm[b0:b0 + 64, dc * st + tc: dc * st + tc + CH],
                        start=True, stop=True, tile_position=(b0, b0)),
                       reads=["sb16", "qdm"], writes=["pb3"], tag="oq",
                       mmkey=(b0, 64))
            for i in range(8):
                h = w * 8 + i
                b0 = 64 * (h % 2)
                pr = i // 2
                OP("te", lambda e, h=h, b0=b0, pr=pr: e.matmul(
                    pb[2][b0:b0 + 64, pr * 128:(pr + 1) * 128],
                    m1sb[:, h * 64:(h + 1) * 64],
                    ltc[:, h * 128:(h + 1) * 128],
                    start=True, stop=True, tile_position=(0, b0)),
                   reads=["m1sb", ltkey], writes=["pb2"], tag="ol", mmkey=(0, 128))
            OP("act", lambda e: e.activation(oasb[:], pb[3][:, 0:512], AF.Identity),
               reads=["pb3"], writes=["oasb"], tag="oae")
            oout = odm[:].rearrange("p (c t) -> p c t", c=NDC)[
                :, w * 4:(w + 1) * 4, tc:tc + CH]
            OP("dve", lambda e, oout=oout: e.tensor_tensor(
                oout, pb[2][:, 0:512].rearrange("p (q t) -> p q t", q=4),
                oasb[:].rearrange("p (q t) -> p q t", q=4), ALU.add),
               reads=["pb2", "oasb"], writes=["kdmov"], tag="oe")
        if "ds" in cp:
          for w in range(2):
            for i in range(8):
                h = w * 8 + i
                for dup in range(2):
                    OP("te", lambda e, h=h, i=i, dup=dup: e.matmul(
                        pb[3][64 * dup:64 * dup + 64, i * 64:(i + 1) * 64],
                        khtok[:, ci * D + h * 64: ci * D + (h + 1) * 64],
                        m1b[:, h * 64:(h + 1) * 64],
                        start=True, stop=True, tile_position=(0, 64 * dup)),
                       reads=["khtok", "m1b"], writes=["pb3"], tag="ds",
                       mmkey=(0, 128))
            OP("dve", lambda e, w=w: e.tensor_tensor(
                s32[:, w * 512:(w + 1) * 512], s32[:, w * 512:(w + 1) * 512],
                pb[3][:, 0:512], ALU.add),
               reads=["pb3", "s32"], writes=["s32"], tag="sadd")
            OP("act", lambda e, w=w: e.activation(
                sb16[:, w * 512:(w + 1) * 512], s32[:, w * 512:(w + 1) * 512],
                AF.Identity),
               reads=["s32"], writes=["sb16"], tag="sbf")

    def emit_l0f():
        arena.mark()
        W8 = arena.alloc(NDC * D, FP8)
        nkc = NDC
        OP("gp", lambda g: g.dma_start(
            out=W8[:].rearrange("p (c o) -> p c o", c=nkc),
            in_=wf8.rearrange("(c p) o -> p c o", p=128)),
           writes=["W8"], dma=True, tag="ld_W8")
        st = STF
        xq = arena.alloc(NDC * st, FP8)
        sq = arena.alloc(NDC * st, BF16)
        lntmp = arena.alloc(NDC * st, BF16)
        stt = mkstats(st)
        for s in range(T // st):
            t0 = s * st
            xk = gkeys(t0, st)
            emit_stats(t0, st, lambda dc=None, t0=t0: xres3(t0, st, dc), xk, sq, stt, 0)
            emit_apply(st, lambda t0=t0: xres3(t0, st), xk, stt,
                       xq[:].rearrange("p (c t) -> p c t", c=NDC), "xq",
                       lntmp[:].rearrange("p (c t) -> p c t", c=NDC))
            for oc in range(NDC):
                bk = 2 + (oc % 2)
                for kc in range(NDC):
                    OP("te", lambda e, oc=oc, kc=kc, bk=bk: e.matmul(
                        pb[bk][:, 0:st],
                        W8[:, kc * D + oc * 128: kc * D + (oc + 1) * 128],
                        xq[:, kc * st:(kc + 1) * st],
                        start=(kc == 0), stop=(kc == NDC - 1)),
                       reads=["W8", "xq"], writes=[f"pb{bk}"], tag="f8",
                       mmkey=(0, 128, "f8"))
                OP("dve", lambda e, oc=oc, bk=bk, t0=t0: e.scalar_tensor_tensor(
                    xres[:, oc * T + t0: oc * T + t0 + st],
                    pb[bk][:, 0:st], pf["bf0"][:, oc:oc + 1],
                    xres[:, oc * T + t0: oc * T + t0 + st], ALU.add, ALU.add),
                   reads=[f"pb{bk}", "pf_bf0"] + xk, writes=xk, tag="res1")
        arena.pop()

    def _lp(fn):
        def g(e):
            with nc.allow_low_precision(reason="bf16 reduce (prototype-validated)"):
                return fn(e)
        return g

    def _stat_math(st, stt, eps, l1a_eps=None):
        mu, va, tmp = stt["mu"], stt["va"], stt["tmp"]
        arow, srow = stt["arow"], stt["srow"]
        arow_bf, srow_bf = stt["arow_bf"], stt["srow_bf"]
        KMU = stt.get("k_mu", "murow")
        KVA = stt.get("k_va", "varow")
        KTMP = stt.get("k_tmp", "tmprow")
        KSR = stt.get("k_srow", "srow")
        KABF = stt.get("k_abf", "arow_bf")
        KSBF = stt.get("k_sbf", "srow_bf")
        OP("dve", lambda e: e.tensor_scalar_mul(mu[0:1, 0:st], pb[0][0:1, 0:st], 1.0 / D),
           reads=["pb0"], writes=[KMU], tag="st_mu")
        OP("dve", lambda e: e.tensor_scalar_mul(va[0:1, 0:st], pb[1][0:1, 0:st], 1.0 / D),
           reads=["pb1"], writes=[KVA], tag="st_ex2")
        OP("dve", lambda e: e.scalar_tensor_tensor(
            tmp[0:1, 0:st], mu[0:1, 0:st], -1.0, mu[0:1, 0:st], ALU.mult, ALU.mult),
           reads=[KMU], writes=[KTMP], tag="st_msq")
        OP("dve", lambda e: e.tensor_add(va[0:1, 0:st], va[0:1, 0:st], tmp[0:1, 0:st]),
           reads=[KVA, KTMP], writes=[KVA], tag="st_var")
        OP("act", lambda e: e.activation(tmp[0:1, 0:st], va[0:1, 0:st], AF.Ln,
                                         bias=cvals[0:1, eps:eps + 1]),
           reads=[KVA, "cvals"], writes=[KTMP], tag="st_ln")
        OP("act", lambda e: e.activation(arow[0:1, 0:st], tmp[0:1, 0:st], AF.Exp,
                                         scale=-0.5),
           reads=[KTMP], writes=["arow"], tag="st_rstd")
        if l1a_eps is not None:
            OP("dve", lambda e: e.scalar_tensor_tensor(
                tmp[0:1, 0:st], arow[0:1, 0:st], 1.0, arow[0:1, 0:st],
                ALU.mult, ALU.mult),
               reads=["arow"], writes=[KTMP], tag="dln_r2")
            OP("dve", lambda e: e.tensor_mul(tmp[0:1, 0:st], tmp[0:1, 0:st],
                                             va[0:1, 0:st]),
               reads=[KTMP, KVA], writes=[KTMP], tag="dln_w")
            OP("act", lambda e: e.activation(tmp[0:1, 0:st], tmp[0:1, 0:st], AF.Ln,
                                             bias=cvals[0:1, l1a_eps:l1a_eps + 1]),
               reads=[KTMP, "cvals"], writes=[KTMP], tag="dln_ln")
            OP("act", lambda e: e.activation(tmp[0:1, 0:st], tmp[0:1, 0:st], AF.Exp,
                                             scale=-0.5),
               reads=[KTMP], writes=[KTMP], tag="dln_u")
            OP("dve", lambda e: e.tensor_mul(arow[0:1, 0:st], arow[0:1, 0:st],
                                             tmp[0:1, 0:st]),
               reads=["arow", KTMP], writes=["arow"], tag="dln_a")
        OP("dve", lambda e: e.scalar_tensor_tensor(
            srow[0:1, 0:st], mu[0:1, 0:st], -1.0, arow[0:1, 0:st], ALU.mult, ALU.mult),
           reads=[KMU, "arow"], writes=[KSR], tag="st_srow")
        OP("act", lambda e: e.activation(arow_bf[0:1, 0:st], arow[0:1, 0:st],
                                         AF.Identity),
           reads=["arow"], writes=[KABF], tag="st_abf")
        OP("act", lambda e: e.activation(srow_bf[0:1, 0:st], srow[0:1, 0:st],
                                         AF.Identity),
           reads=[KSR], writes=[KSBF], tag="st_sbf")

    def _bcg(st, stt, gname, bname, pitch=None):
        if pitch is None:
            pitch = st
        arow_bf, srow_bf = stt["arow_bf"], stt["srow_bf"]
        KABF = stt.get("k_abf", "arow_bf")
        KSBF = stt.get("k_sbf", "srow_bf")
        absc, sbsc = stt["absc"], stt["sbsc"]
        grow, brow = prow[gname], prow[bname]
        for dc in range(NDC):
            OP("te", lambda e, dc=dc, grow=grow: e.matmul(
                pb[0][:, 0:st], grow[0:1, dc * 128:(dc + 1) * 128],
                arow_bf[0:1, 0:st], start=True, stop=True),
               reads=[KABF, "pr_" + gname], writes=["pb0"], tag="bcg_a",
               mmkey=(0, 1))
            OP("act", lambda e, dc=dc: e.activation(
                absc[:, dc * pitch:dc * pitch + st], pb[0][:, 0:st], AF.Identity),
               reads=["pb0"], writes=["absc"], tag="bcg_ae")
            OP("te", lambda e, dc=dc, grow=grow: e.matmul(
                pb[1][:, 0:st], grow[0:1, dc * 128:(dc + 1) * 128],
                srow_bf[0:1, 0:st], start=True, stop=False),
               reads=[KSBF, "pr_" + gname], writes=["pb1"], tag="bcg_s",
               mmkey=(0, 1))
            OP("te", lambda e, dc=dc, brow=brow: e.matmul(
                pb[1][:, 0:st], brow[0:1, dc * 128:(dc + 1) * 128],
                onesrow[0:1, 0:st], start=False, stop=True),
               reads=["onesrow", "pr_" + bname], writes=["pb1"], tag="bcg_b",
               mmkey=(0, 1))
            OP("act", lambda e, dc=dc: e.activation(
                sbsc[:, dc * pitch:dc * pitch + st], pb[1][:, 0:st], AF.Identity),
               reads=["pb1"], writes=["sbsc"], tag="bcg_se")

    def emit_l1a():
        arena.mark()
        WQ = arena.alloc(NDC * D, BF16); load_weight(WQ, wq1, "WQ1")
        WK = arena.alloc(NDC * D, BF16); load_weight(WK, wk1, "WK1")
        WV = arena.alloc(NDC * D, BF16); load_weight(WV, wv1, "WV1")
        WFB = arena.alloc(2 * NDC * D, BF16); load_weight(WFB, wfb, "WFB", kin=2 * D)
        WO = arena.alloc(NDC * D, BF16); load_weight(WO, wo1, "WO1")
        for nm in ("g1", "b1", "g2", "b2"):
            prow[nm] = arena.alloc(D, BF16)
            OP("gp", lambda g, nm=nm: g.dma_start(
                out=prow[nm][0:1, :], in_=pfsrc[nm].rearrange("(x d) -> x d", x=1)),
               writes=["pr_" + nm], dma=True)
        hmapR = arena.alloc(NDC * 128, BF16)
        OP("dve", lambda e: e.memset(hmapR[:], 0.0), writes=["hmapR"], tag="c_hmapR0")
        OP("dve", lambda e: e.memset(hmapR[64:80, :], 1.0), writes=["hmapR"],
           tag="c_hmapR1")
        for dc in range(NDC):
            for half in range(2):
                sl = hmapR[64:80, dc * 128 + half * 64: dc * 128 + (half + 1) * 64]
                OP("gp", lambda g, sl=sl, dc=dc, half=half: g.affine_select(
                    out=sl, in_=sl, compare_op=ALU.is_equal, fill=0.0,
                    base=-(2 * dc + half), pattern=[[0, 64]], channel_multiplier=1),
                   reads=["hmapR"], writes=["hmapR"], tag="c_hmapR")
        st = ST1
        W = st + 4
        xw = arena.alloc(NDC * W, BF16)
        vw = arena.alloc(NDC * W, BF16)
        sq = vw
        lntmp = vw
        qw = arena.alloc(NDC * W, BF16)
        kw = arena.alloc(NDC * W, BF16)
        outf, outr, zt = qw, kw, qw
        scsb = arena.alloc(3 * st, F32)
        tmpc = scsb
        stt = {}
        stt["mu"] = arena.alloc(W, F32)
        stt["va"] = arena.alloc(W, F32)
        stt["tmp"] = arena.alloc(W, F32)
        stt["arow"] = arena.alloc(W, F32)
        stt["srow"] = stt["va"]
        mubits = stt["mu"][:].bitcast(BF16)
        stt["arow_bf"] = mubits[:, 0:W]
        stt["srow_bf"] = mubits[:, W:2 * W]
        stt["k_srow"] = "varow"
        stt["k_abf"] = "murow"
        stt["k_sbf"] = "murow"
        stt["absc"] = arena.alloc(NDC * W, BF16)
        stt["sbsc"] = arena.alloc(NDC * W, BF16)
        attnb = stt["sbsc"][:, 0:3 * st]
        mx = stt["tmp"]
        prodsb = mx[:, 0:st].bitcast(BF16)

        KXW = "vw"
        KQ = "qw"
        KK_ = "kw"
        KSC = "scsb"
        KMX = "tmprow"

        def xw3():
            return xw[:].rearrange("p (c t) -> p c t", c=NDC)

        for s in range(T // st):
            t0 = s * st
            first, last = (s == 0), (s == T // st - 1)
            rd0 = max(t0 - 2, 0)
            rd1 = min(t0 + st + 2, T)
            dst0 = rd0 - (t0 - 2)
            rlen = rd1 - rd0
            xk = gkeys(rd0, rlen)

            def srcw(dc=None, rd0=rd0, rlen=rlen):
                if dc is None:
                    return xres[:].rearrange("p (c t) -> p c t", c=NDC)[
                        :, :, rd0:rd0 + rlen]
                return xres[:, dc * T + rd0:dc * T + rd0 + rlen]

            if first:
                OP("dve", lambda e: e.memset(xw3()[:, :, 0:2], 0.0),
                   writes=["xw"], tag="xwpadL")
            if last:
                OP("dve", lambda e: e.memset(xw3()[:, :, W - 2:W], 0.0),
                   writes=["xw"], tag="xwpadR")
            OP("act", lambda e, rlen=rlen, srcw=srcw: e.activation(
                sq[:].rearrange("p (c t) -> p c t", c=NDC)[:, :, 0:rlen],
                srcw(), AF.Square),
               reads=xk, writes=[KXW], tag="l1a_sq")
            for dc in range(NDC):
                OP("te", lambda e, dc=dc, rlen=rlen, srcw=srcw: e.matmul(
                    pb[0][0:1, 0:rlen], onesc128[:], srcw(dc),
                    start=(dc == 0), stop=(dc == NDC - 1)),
                   reads=xk + ["onesc128"], writes=["pb0"], tag="l1a_s1",
                   mmkey=(0, 128))
            for dc in range(NDC):
                OP("te", lambda e, dc=dc, rlen=rlen: e.matmul(
                    pb[1][0:1, 0:rlen], onesc128[:], sq[:, dc * W:dc * W + rlen],
                    start=(dc == 0), stop=(dc == NDC - 1)),
                   reads=[KXW, "onesc128"], writes=["pb1"], tag="l1a_s2",
                   mmkey=(0, 128))
            _stat_math(rlen, stt, 0, l1a_eps=2)
            _bcg(rlen, stt, "g1", "b1", pitch=W)
            a_b = stt["absc"][:].rearrange("p (c t) -> p c t", c=NDC)[:, :, 0:rlen]
            s_b = stt["sbsc"][:].rearrange("p (c t) -> p c t", c=NDC)[:, :, 0:rlen]
            OP("dve", lambda e, rlen=rlen, srcw=srcw, a_b=a_b: e.tensor_tensor(
                lntmp[:].rearrange("p (c t) -> p c t", c=NDC)[:, :, 0:rlen],
                srcw(), a_b, ALU.mult),
               reads=xk + ["absc"], writes=[KXW], tag="l1a_ap1")
            OP("dve", lambda e, rlen=rlen, dst0=dst0, s_b=s_b: e.tensor_tensor(
                xw3()[:, :, dst0:dst0 + rlen],
                lntmp[:].rearrange("p (c t) -> p c t", c=NDC)[:, :, 0:rlen],
                s_b, ALU.add),
               reads=[KXW, "sbsc"], writes=["xw"], tag="l1a_ap2")

            def proj_w(wsb, wkey, out, okey, bias):
                for dc in range(NDC):
                    bank = 1 + (dc % 2)
                    for kc in range(NDC):
                        OP("te", lambda e, kc=kc, dc=dc, wsb=wsb, bank=bank: e.matmul(
                            pb[bank][:, 0:W],
                            wsb[:, kc * D + dc * 128: kc * D + (dc + 1) * 128],
                            xw[:, kc * W:(kc + 1) * W],
                            start=(kc == 0), stop=(kc == NDC - 1)),
                           reads=[wkey, "xw"], writes=[f"pb{bank}"], tag="l1a_p",
                           mmkey=(0, 128))
                    OP("act", lambda e, dc=dc, out=out, bias=bias, bank=bank:
                       e.activation(
                        out[:, dc * W:(dc + 1) * W], pb[bank][:, 0:W], AF.Identity,
                        bias=pf[bias][:, dc:dc + 1]),
                       reads=[f"pb{bank}", "pf_" + bias], writes=[okey], tag="l1a_pe")

            proj_w(WQ, "WQ1", qw, KQ, "bq1")
            proj_w(WK, "WK1", kw, KK_, "bk1")
            for tname, tkey in ((qw, KQ), (kw, KK_)):
                if first:
                    OP("dve", lambda e, tname=tname: e.memset(
                        tname[:].rearrange("p (c t) -> p c t", c=NDC)[:, :, 0:2], 0.0),
                       reads=[tkey], writes=[tkey], tag="padL")
                if last:
                    OP("dve", lambda e, tname=tname: e.memset(
                        tname[:].rearrange("p (c t) -> p c t", c=NDC)[:, :, W - 2:W],
                        0.0),
                       reads=[tkey], writes=[tkey], tag="padR")

            for dc in range(NDC):
                for di, sgn in ((0, 1), (1, -1)):
                    for j in range(3):
                        qs = qw[:, dc * W + 2 - sgn:dc * W + 2 - sgn + st]
                        ks = kw[:, dc * W + 2 - sgn * j:dc * W + 2 - sgn * j + st]
                        OP("dve", lambda e, qs=qs, ks=ks: e.tensor_tensor(
                            prodsb[:, 0:st], qs, ks, ALU.mult),
                           reads=[KQ, KK_], writes=[KMX], tag="l1a_prod")
                        bank = (2 + j) if di == 0 else (5 + j)
                        tp = (0, 0) if di == 0 else (0, 64)
                        r0 = 0 if di == 0 else 64
                        OP("te", lambda e, dc=dc, bank=bank, tp=tp, r0=r0: e.matmul(
                            pb[bank][r0:r0 + 16, 0:st],
                            pmap[:, dc * 16:(dc + 1) * 16], prodsb[:, 0:st],
                            start=(dc == 0), stop=(dc == NDC - 1),
                            tile_position=tp),
                           reads=[KMX, "pmap"], writes=[f"pb{bank}"],
                           tag="l1a_sc", mmkey=(0, 128))
            SC = 1.0 / 8.0
            if s == 0:
                OP("dve", lambda e: e.memset(scsb[0:64, :], 0.0),
                   writes=[KSC], tag="sc_z")
            for j in range(3):
                OP("act", lambda e, j=j: e.activation(
                    scsb[0:16, j * st:(j + 1) * st], pb[2 + j][0:16, 0:st],
                    AF.Identity, scale=SC),
                   reads=[f"pb{2 + j}"], writes=[KSC], tag="sc_eF")
                OP("act", lambda e, j=j: e.activation(
                    scsb[64:80, j * st:(j + 1) * st], pb[5 + j][64:80, 0:st],
                    AF.Identity, scale=SC),
                   reads=[f"pb{5 + j}"], writes=[KSC], tag="sc_eR")
            sc3 = scsb[0:80, 0:3 * st].rearrange("p (j t) -> p t j", j=3)
            OP("dve", lambda e, sc3=sc3: e.tensor_reduce(
                mx[0:80, 0:st], sc3, mybir.AxisListType.X, ALU.max),
               reads=[KSC], writes=[KMX], tag="sm_mx")
            mxb2 = mx[0:80, 0:st].rearrange("p (t x) -> p t x", x=1).broadcast_to(
                [80, st, 3])
            OP("dve", lambda e, sc3=sc3, mxb2=mxb2: e.tensor_tensor(
                sc3, sc3, mxb2, ALU.subtract),
               reads=[KSC, KMX], writes=[KSC], tag="sm_sub")
            OP("act", lambda e: e.activation(scsb[0:80, 0:3 * st],
                                             scsb[0:80, 0:3 * st], AF.Exp),
               reads=[KSC], writes=[KSC], tag="sm_exp")
            OP("dve", lambda e, sc3=sc3: e.tensor_reduce(
                mx[0:80, 0:st], sc3, mybir.AxisListType.X, ALU.add),
               reads=[KSC], writes=[KMX], tag="sm_den")
            OP("act", lambda e: e.activation(mx[0:80, 0:st], mx[0:80, 0:st], AF.Ln),
               reads=[KMX], writes=[KMX], tag="sm_ln")
            OP("act", lambda e: e.activation(mx[0:80, 0:st], mx[0:80, 0:st],
                                             AF.Exp, scale=-1.0),
               reads=[KMX], writes=[KMX], tag="sm_rcp")
            OP("dve", lambda e, sc3=sc3, mxb2=mxb2: e.tensor_tensor(
                attnb[0:80, 0:3 * st].rearrange("p (j t) -> p t j", j=3),
                sc3, mxb2, ALU.mult),
               reads=[KSC, KMX], writes=["sbsc"], tag="sm_attn")

            for dc in range(NDC):
                bank = 1 + (dc % 2)
                for kc in range(NDC):
                    OP("te", lambda e, kc=kc, dc=dc, bank=bank: e.matmul(
                        pb[bank][:, 0:W],
                        WV[:, kc * D + dc * 128: kc * D + (dc + 1) * 128],
                        xw[:, kc * W:(kc + 1) * W],
                        start=(kc == 0), stop=(kc == NDC - 1)),
                       reads=["WV1", "xw"], writes=[f"pb{bank}"], tag="l1a_pv",
                       mmkey=(0, 128))
                OP("act", lambda e, dc=dc, bank=bank: e.activation(
                    vw[:, dc * W:(dc + 1) * W], pb[bank][:, 0:W], AF.Identity,
                    bias=pf["bv1"][:, dc:dc + 1]),
                   reads=[f"pb{bank}", "pf_bv1"], writes=[KXW], tag="l1a_ev")
            if first:
                OP("dve", lambda e: e.memset(
                    vw[:].rearrange("p (c t) -> p c t", c=NDC)[:, :, 0:2], 0.0),
                   reads=[KXW], writes=[KXW], tag="vpadL")
            if last:
                OP("dve", lambda e: e.memset(
                    vw[:].rearrange("p (c t) -> p c t", c=NDC)[:, :, W - 2:W], 0.0),
                   reads=[KXW], writes=[KXW], tag="vpadR")

            for di, sgn, hm, hkey, r0, outt, okey in (
                    (0, 1, hmap, "hmap", 0, outf, KQ),
                    (1, -1, hmapR, "hmapR", 64, outr, KK_)):
                for dc in range(NDC):
                    for j in range(3):
                        bank = 3 if j < 2 else 4
                        col = (j % 2) * st
                        OP("te", lambda e, dc=dc, j=j, hm=hm, r0=r0, bank=bank,
                           col=col: e.matmul(
                            pb[bank][:, col:col + st],
                            hm[r0:r0 + 16, dc * 128:(dc + 1) * 128],
                            attnb[r0:r0 + 16, j * st:(j + 1) * st],
                            start=True, stop=True),
                           reads=["sbsc", hkey], writes=[f"pb{bank}"],
                           tag="l1a_abc", mmkey=(r0, 16))
                    vv = vw[:, dc * W:(dc + 1) * W]
                    OP("dve", lambda e, vv=vv, sgn=sgn: e.tensor_tensor(
                        tmpc[:, 0:3 * st].rearrange("p (t j) -> p t j", j=3)[
                            :, :, 0:2],
                        pb[3][:, 0:2 * st].rearrange("p (j t) -> p t j", j=2),
                        _shiftn(vv, sgn, 0, 2), ALU.mult),
                       reads=["pb3", KXW], writes=[KSC], tag="l1a_cm1")
                    OP("dve", lambda e, vv=vv, sgn=sgn: e.tensor_tensor(
                        tmpc[:, 0:3 * st].rearrange("p (t j) -> p t j", j=3)[
                            :, :, 2:3],
                        pb[4][:, 0:st].rearrange("p (j t) -> p t j", j=1),
                        _shiftn(vv, sgn, 2, 1), ALU.mult),
                       reads=["pb4", KXW], writes=[KSC], tag="l1a_cm2")
                    OP("dve", _lp(lambda e, dc=dc, outt=outt: e.tensor_reduce(
                        outt[:, dc * W + 2:dc * W + 2 + st],
                        tmpc[:, 0:3 * st].rearrange("p (t j) -> p t j", j=3),
                        mybir.AxisListType.X, ALU.add)),
                       reads=[KSC], writes=[okey], tag="l1a_cred")

            for oc in range(NDC):
                bank = 1 + (oc % 2)
                for kc in range(2 * NDC):
                    srct = outf if kc < NDC else outr
                    srk = KQ if kc < NDC else KK_
                    kcl = kc % NDC
                    OP("te", lambda e, oc=oc, kc=kc, kcl=kcl, srct=srct, bank=bank:
                       e.matmul(
                        pb[bank][:, 0:st],
                        WFB[:, kc * D + oc * 128: kc * D + (oc + 1) * 128],
                        srct[:, kcl * W + 2:kcl * W + 2 + st],
                        start=(kc == 0), stop=(kc == 2 * NDC - 1)),
                       reads=["WFB", srk], writes=[f"pb{bank}"], tag="l1a_fb",
                       mmkey=(0, 128))
                OP("dve", lambda e, oc=oc, bank=bank: e.scalar_tensor_tensor(
                    xw[:, oc * W + 2:oc * W + 2 + st], pb[bank][:, 0:st],
                    pf["bfb"][:, oc:oc + 1],
                    xw[:, oc * W + 2:oc * W + 2 + st], ALU.add, ALU.add),
                   reads=[f"pb{bank}", "pf_bfb", "xw"], writes=["xw"], tag="l1a_mid")

            midv = xw3()[:, :, 2:2 + st]
            OP("act", lambda e, midv=midv: e.activation(
                sq[:].rearrange("p (c t) -> p c t", c=NDC)[:, :, 0:st],
                midv, AF.Square),
               reads=["xw"], writes=[KXW], tag="m_sq")
            for dc in range(NDC):
                OP("te", lambda e, dc=dc: e.matmul(
                    pb[0][0:1, 0:st], onesc128[:],
                    xw[:, dc * W + 2:dc * W + 2 + st],
                    start=(dc == 0), stop=(dc == NDC - 1)),
                   reads=["xw", "onesc128"], writes=["pb0"], tag="m_s1",
                   mmkey=(0, 128))
            for dc in range(NDC):
                OP("te", lambda e, dc=dc: e.matmul(
                    pb[1][0:1, 0:st], onesc128[:], sq[:, dc * W:dc * W + st],
                    start=(dc == 0), stop=(dc == NDC - 1)),
                   reads=[KXW, "onesc128"], writes=["pb1"], tag="m_s2",
                   mmkey=(0, 128))
            _stat_math(st, stt, 2)
            _bcg(st, stt, "g2", "b2", pitch=W)
            a_b2 = stt["absc"][:].rearrange("p (c t) -> p c t", c=NDC)[:, :, 0:st]
            s_b2 = stt["sbsc"][:].rearrange("p (c t) -> p c t", c=NDC)[:, :, 0:st]
            OP("dve", lambda e, a_b2=a_b2, midv=midv: e.tensor_tensor(
                lntmp[:].rearrange("p (c t) -> p c t", c=NDC)[:, :, 0:st],
                midv, a_b2, ALU.mult),
               reads=["xw", "absc"], writes=[KXW], tag="z_ap1")
            OP("dve", lambda e, s_b2=s_b2: e.tensor_tensor(
                zt[:].rearrange("p (c t) -> p c t", c=NDC)[:, :, 0:st],
                lntmp[:].rearrange("p (c t) -> p c t", c=NDC)[:, :, 0:st],
                s_b2, ALU.add),
               reads=[KXW, "sbsc"], writes=[KQ], tag="z_ap2")

            OP("act", lambda e: e.add_instruction(mybir.InstLoadActFuncSet(
                name=nc.get_next_instruction_name(), act_func_set_id=10,
                ins=[], outs=[])), tag="tbl10")
            for oc in range(NDC):
                bank = 1 + (oc % 2)
                for kc in range(NDC):
                    OP("te", lambda e, oc=oc, kc=kc, bank=bank: e.matmul(
                        pb[bank][:, 0:st],
                        WO[:, kc * D + oc * 128: kc * D + (oc + 1) * 128],
                        zt[:, kc * W:kc * W + st],
                        start=(kc == 0), stop=(kc == NDC - 1)),
                       reads=["WO1", KQ], writes=[f"pb{bank}"], tag="l1a_wo",
                       mmkey=(0, 128))
                OP("act", lambda e, oc=oc, bank=bank: e.activation(
                    lntmp[:, oc * W:oc * W + st], pb[bank][:, 0:st], AF.Gelu,
                    bias=pf["bo1"][:, oc:oc + 1]),
                   reads=[f"pb{bank}", "pf_bo1"], writes=[KXW], tag="l1a_gel")
            OP("act", lambda e: e.add_instruction(mybir.InstLoadActFuncSet(
                name=nc.get_next_instruction_name(), act_func_set_id=6,
                ins=[], outs=[])), tag="tbl6b")
            for oc in range(NDC):
                OP("dve", lambda e, oc=oc, t0=t0: e.tensor_tensor(
                    xres[:, oc * T + t0: oc * T + t0 + st],
                    xres[:, oc * T + t0: oc * T + t0 + st],
                    lntmp[:, oc * W:oc * W + st], ALU.add),
                   reads=[KXW] + xk, writes=xk, tag="l1a_res")
        arena.pop()

    def _shiftn(vv, sgn, j0, nj):
        base = vv[:, 2 - sgn * j0:2 - sgn * j0 + ST1]
        return bass.AP(base.tensor, base.offset,
                       [list(base.ap[0]), [1, ST1], [-sgn, nj]])

    def emit_l1f():
        arena.mark()
        WR = arena.alloc(NDC * D, BF16); load_weight(WR, wr2, "WR2")
        WK2 = arena.alloc(NDC * D, BF16); load_weight(WK2, wk2, "WK2")
        WV2 = arena.alloc(NDC * D, BF16); load_weight(WV2, wv2, "WV2")
        st = STF
        Wd = st + 1
        xh = arena.alloc(NDC * Wd, BF16)
        sq = arena.alloc(NDC * st, BF16)
        lntmp = arena.alloc(NDC * st, BF16)
        xr = arena.alloc(NDC * st, BF16)
        xkk = arena.alloc(NDC * st, BF16)
        rt = arena.alloc(NDC * st, BF16)
        kact = arena.alloc(NDC * st, BF16)
        ytile = arena.alloc(NDC * st, F32)
        stt = mkstats(st)
        OP("dve", lambda e: e.memset(
            xh[:].rearrange("p (c t) -> p c t", c=NDC)[:, :, 0:1], 0.0),
           writes=["xh"], tag="xh0")
        for s in range(T // st):
            t0 = s * st
            xk = gkeys(t0, st)
            emit_stats(t0, st, lambda dc=None, t0=t0: xres3(t0, st, dc), xk, sq, stt, 0)
            a_b = stt["absc"][:, 0:st].rearrange("p (c t) -> p c t", c=1).broadcast_to(
                [128, NDC, st])
            s_b = stt["sbsc"][:, 0:st].rearrange("p (c t) -> p c t", c=1).broadcast_to(
                [128, NDC, st])
            OP("dve", lambda e, t0=t0: e.tensor_tensor(
                lntmp[:].rearrange("p (c t) -> p c t", c=NDC),
                xres3(t0, st), a_b, ALU.mult),
               reads=xk + ["absc"], writes=["lntmp"], tag="f_ap1")
            OP("dve", lambda e: e.tensor_tensor(
                xh[:].rearrange("p (c t) -> p c t", c=NDC)[:, :, 1:Wd],
                lntmp[:].rearrange("p (c t) -> p c t", c=NDC), s_b, ALU.add),
               reads=["lntmp", "sbsc"], writes=["xh"], tag="f_ap2")
            mur3 = pf["mur"][:].rearrange("p (c x) -> p c x", x=1).broadcast_to(
                [128, NDC, st])
            muk3 = pf["muk"][:].rearrange("p (c x) -> p c x", x=1).broadcast_to(
                [128, NDC, st])
            xcur = xh[:].rearrange("p (c t) -> p c t", c=NDC)[:, :, 1:Wd]
            xprv = xh[:].rearrange("p (c t) -> p c t", c=NDC)[:, :, 0:Wd - 1]
            OP("dve", lambda e, mur3=mur3: e.tensor_tensor(
                lntmp[:].rearrange("p (c t) -> p c t", c=NDC), xcur, mur3, ALU.mult),
               reads=["xh", "pf_mur"], writes=["lntmp"], tag="f_xr1")
            OP("dve", lambda e, mur3=mur3: e.scalar_tensor_tensor(
                xr[:].rearrange("p (c t) -> p c t", c=NDC), xprv, 1.0,
                lntmp[:].rearrange("p (c t) -> p c t", c=NDC), ALU.mult, ALU.add)
               if False else
               e.tensor_tensor(sq[:].rearrange("p (c t) -> p c t", c=NDC),
                               xprv, mur3, ALU.mult),
               reads=["xh", "pf_mur"], writes=["sq"], tag="f_xr2")
            OP("dve", lambda e: e.tensor_tensor(
                xr[:].rearrange("p (c t) -> p c t", c=NDC),
                lntmp[:].rearrange("p (c t) -> p c t", c=NDC),
                sq[:].rearrange("p (c t) -> p c t", c=NDC), ALU.subtract)
               if False else None, tag="noop") if False else None
            OP("dve", lambda e: e.tensor_tensor(
                xr[:].rearrange("p (c t) -> p c t", c=NDC),
                lntmp[:].rearrange("p (c t) -> p c t", c=NDC), xprv, ALU.add),
               reads=["lntmp", "xh"], writes=["xr"], tag="f_xr3")
            OP("dve", lambda e: e.tensor_tensor(
                xr[:].rearrange("p (c t) -> p c t", c=NDC),
                xr[:].rearrange("p (c t) -> p c t", c=NDC),
                sq[:].rearrange("p (c t) -> p c t", c=NDC), ALU.subtract),
               reads=["xr", "sq"], writes=["xr"], tag="f_xr4")
            OP("dve", lambda e, muk3=muk3: e.tensor_tensor(
                lntmp[:].rearrange("p (c t) -> p c t", c=NDC), xcur, muk3, ALU.mult),
               reads=["xh", "pf_muk"], writes=["lntmp"], tag="f_xk1")
            OP("dve", lambda e, muk3=muk3: e.tensor_tensor(
                sq[:].rearrange("p (c t) -> p c t", c=NDC), xprv, muk3, ALU.mult),
               reads=["xh", "pf_muk"], writes=["sq"], tag="f_xk2")
            OP("dve", lambda e: e.tensor_tensor(
                xkk[:].rearrange("p (c t) -> p c t", c=NDC),
                lntmp[:].rearrange("p (c t) -> p c t", c=NDC), xprv, ALU.add),
               reads=["lntmp", "xh"], writes=["xkk"], tag="f_xk3")
            OP("dve", lambda e: e.tensor_tensor(
                xkk[:].rearrange("p (c t) -> p c t", c=NDC),
                xkk[:].rearrange("p (c t) -> p c t", c=NDC),
                sq[:].rearrange("p (c t) -> p c t", c=NDC), ALU.subtract),
               reads=["xkk", "sq"], writes=["xkk"], tag="f_xk4")
            OP("dve", lambda e: e.tensor_copy(
                xh[:].rearrange("p (c t) -> p c t", c=NDC)[:, :, 0:1],
                xh[:].rearrange("p (c t) -> p c t", c=NDC)[:, :, Wd - 1:Wd]),
               reads=["xh"], writes=["xh"], tag="f_carry")
            for oc in range(NDC):
                bank = 1 + (oc % 2)
                for kc in range(NDC):
                    OP("te", lambda e, oc=oc, kc=kc, bank=bank: e.matmul(
                        pb[bank][:, 0:st],
                        WR[:, kc * D + oc * 128: kc * D + (oc + 1) * 128],
                        xr[:, kc * st:(kc + 1) * st],
                        start=(kc == 0), stop=(kc == NDC - 1)),
                       reads=["WR2", "xr"], writes=[f"pb{bank}"], tag="f_r",
                       mmkey=(0, 128))
                OP("act", lambda e, oc=oc, bank=bank: e.activation(
                    rt[:, oc * st:(oc + 1) * st], pb[bank][:, 0:st], AF.Exp,
                    scale=-1.0),
                   reads=[f"pb{bank}"], writes=["rt"], tag="f_re")
            OP("dve", lambda e: e.tensor_scalar_add(rt[:], rt[:], 1.0),
               reads=["rt"], writes=["rt"], tag="f_r1")
            OP("act", lambda e: e.activation(rt[:], rt[:], AF.Ln),
               reads=["rt"], writes=["rt"], tag="f_rl")
            OP("act", lambda e: e.activation(rt[:], rt[:], AF.Exp, scale=-1.0),
               reads=["rt"], writes=["rt"], tag="f_rx")
            for oc in range(NDC):
                bank = 1 + (oc % 2)
                for kc in range(NDC):
                    OP("te", lambda e, oc=oc, kc=kc, bank=bank: e.matmul(
                        pb[bank][:, 0:st],
                        WK2[:, kc * D + oc * 128: kc * D + (oc + 1) * 128],
                        xkk[:, kc * st:(kc + 1) * st],
                        start=(kc == 0), stop=(kc == NDC - 1)),
                       reads=["WK2", "xkk"], writes=[f"pb{bank}"], tag="f_k",
                       mmkey=(0, 128))
                OP("dve", lambda e, oc=oc, bank=bank: e.tensor_scalar_max(
                    sq[:, oc * st:(oc + 1) * st], pb[bank][:, 0:st], 0.0),
                   reads=[f"pb{bank}"], writes=["sq"], tag="f_km")
            OP("act", lambda e: e.activation(kact[:], sq[:, 0:NDC * st], AF.Square),
               reads=["sq"], writes=["kact"], tag="f_k2")
            for oc in range(NDC):
                bank = 1 + (oc % 2)
                for kc in range(NDC):
                    OP("te", lambda e, oc=oc, kc=kc, bank=bank: e.matmul(
                        pb[bank][:, 0:st],
                        WV2[:, kc * D + oc * 128: kc * D + (oc + 1) * 128],
                        kact[:, kc * st:(kc + 1) * st],
                        start=(kc == 0), stop=(kc == NDC - 1)),
                       reads=["WV2", "kact"], writes=[f"pb{bank}"], tag="f_v",
                       mmkey=(0, 128))
                OP("dve", lambda e, oc=oc, bank=bank: e.tensor_tensor(
                    lntmp[:, oc * st:(oc + 1) * st], pb[bank][:, 0:st],
                    rt[:, oc * st:(oc + 1) * st], ALU.mult),
                   reads=[f"pb{bank}", "rt"], writes=["lntmp"], tag="f_vm")
                OP("dve", lambda e, oc=oc, t0=t0: e.tensor_tensor(
                    ytile[:, oc * st:(oc + 1) * st],
                    xres[:, oc * T + t0:oc * T + t0 + st],
                    lntmp[:, oc * st:(oc + 1) * st], ALU.add),
                   reads=["lntmp"] + xk, writes=["ytile"], tag="f_y")
            OP("sy", lambda e, t0=t0: e.dma_start(
                out=y_fm.rearrange("(c p) t -> p c t", p=128)[:, :, t0:t0 + st],
                in_=ytile[:].rearrange("p (c t) -> p c t", c=NDC)),
               reads=["ytile"], writes=[f"y:{t0 // 512}"], dma=True, tag="f_yd")
        arena.pop()

    def emit_out():
        arena.mark()
        ytile = arena.alloc(NDC * 512, F32)
        for s in range(8):
            t0 = s * 512
            OP("act", lambda e, t0=t0: e.activation(
                ytile[:].rearrange("p (c t) -> p c t", c=NDC),
                xres[:].rearrange("p (c t) -> p c t", c=NDC)[:, :, t0:t0 + 512],
                AF.Identity),
               reads=[f"xres:{t0 // 512}"], writes=["ytile"], tag="yt")
            OP("sy", lambda e, t0=t0: e.dma_start(
                out=y_fm.rearrange("(c p) t -> p c t", p=128)[:, :, t0:t0 + 512],
                in_=ytile[:].rearrange("p (c t) -> p c t", c=NDC)),
               reads=["ytile"], writes=[f"y:{t0 // 512}"], dma=True, tag="yd")
        arena.pop()

    OP("act", lambda e: e.add_instruction(mybir.InstLoadActFuncSet(
        name=nc.get_next_instruction_name(), act_func_set_id=6, ins=[], outs=[])),
       tag="tbl6")
    emit_consts()
    emit_param_loads()
    if "l0a" in phases:
        emit_l0a()
        sch.barrier()
    if "l0f" in phases:
        emit_l0f()
        sch.barrier()
    if "l1a" in phases:
        emit_l1a()
        sch.barrier()
    if "l1f" in phases:
        emit_l1f()
        sch.barrier()
    if "l1f" not in phases:
        emit_out()
    OP("sy", lambda e: e.nop(), reads=[f"y:{s}" for s in range(8)], tag="fin")

    with nc.Block() as blk:
        sch.materialize(blk, sems)
    for cm in reversed(ctxs):
        cm.__exit__(None, None, None)
    return nc


def host_inputs(inputs):
    shared = dict(
        wq0=np.ascontiguousarray(np.asarray(inputs["L0_attn_Wq"], np.float32)),
        wk0=np.ascontiguousarray(np.asarray(inputs["L0_attn_Wk"], np.float32)),
        wv0=np.ascontiguousarray(np.asarray(inputs["L0_attn_Wv"], np.float32)),
        wb0=np.ascontiguousarray(np.asarray(inputs["L0_attn_Wbeta"], np.float32)),
        bb0v=np.ascontiguousarray(np.asarray(inputs["L0_attn_b_beta"], np.float32)),
        wo0=np.ascontiguousarray(np.asarray(inputs["L0_attn_Wout"], np.float32)),
        wf8=np.ascontiguousarray(np.asarray(inputs["L0_ffn_weight"], np.float32).T),
        bf0=np.ascontiguousarray(np.asarray(inputs["L0_ffn_bias"], np.float32)),
        wq1=np.ascontiguousarray(np.asarray(inputs["L1_attn_Wq"], np.float32)),
        bq1=np.ascontiguousarray(np.asarray(inputs["L1_attn_bq"], np.float32)),
        wk1=np.ascontiguousarray(np.asarray(inputs["L1_attn_Wk"], np.float32)),
        bk1=np.ascontiguousarray(np.asarray(inputs["L1_attn_bk"], np.float32)),
        wv1=np.ascontiguousarray(np.asarray(inputs["L1_attn_Wv"], np.float32)),
        bv1=np.ascontiguousarray(np.asarray(inputs["L1_attn_bv"], np.float32)),
        wfb=np.ascontiguousarray(np.asarray(inputs["L1_attn_Wfb"], np.float32)),
        bfb=np.ascontiguousarray(np.asarray(inputs["L1_attn_bfb"], np.float32)),
        wo1=np.ascontiguousarray(np.asarray(inputs["L1_attn_Wo"], np.float32)),
        bo1=np.ascontiguousarray(np.asarray(inputs["L1_attn_bo"], np.float32)),
        g1=np.ascontiguousarray(np.asarray(inputs["L1_attn_g1"], np.float32)),
        b1=np.ascontiguousarray(np.asarray(inputs["L1_attn_b1"], np.float32)),
        g2=np.ascontiguousarray(np.asarray(inputs["L1_attn_g2"], np.float32)),
        b2=np.ascontiguousarray(np.asarray(inputs["L1_attn_b2"], np.float32)),
        wr2=np.ascontiguousarray(np.asarray(inputs["L1_ffn_W_r"], np.float32).T),
        wk2=np.ascontiguousarray(np.asarray(inputs["L1_ffn_W_k"], np.float32).T),
        wv2=np.ascontiguousarray(np.asarray(inputs["L1_ffn_W_v"], np.float32).T),
        mur=np.ascontiguousarray(np.asarray(inputs["L1_ffn_mu_r"], np.float32)),
        muk=np.ascontiguousarray(np.asarray(inputs["L1_ffn_mu_k"], np.float32)),
    )
    xs = np.asarray(inputs["L0_attn_x"], np.float32)
    in_maps = []
    for b in range(8):
        m = dict(shared)
        m["x_fm"] = np.ascontiguousarray(xs[b].T)
        in_maps.append(m)
    return in_maps


def kernel(**inputs):
    in_maps = host_inputs(inputs)
    nc = build()
    res = run_bass_kernel_spmd(nc, in_maps, core_ids=list(range(8)))
    out = np.stack([np.ascontiguousarray(res.results[b]["y_fm"].T) for b in range(8)])
    return out.astype(np.float32)


# version 08
# speedup vs baseline: 1.0578x; 1.0578x over previous
import numpy as np
import concourse.bass as bass
import concourse.mybir as mybir
from concourse.bass_utils import run_bass_kernel_spmd

F32 = mybir.dt.float32
BF16 = mybir.dt.bfloat16
FP8 = mybir.dt.float8e4
AF = mybir.ActivationFunctionType
ALU = mybir.AluOpType

T = 4096
D = 1024
NDC = 8
H = 16
CH = 128
ST0 = 256
NS0 = T // ST0
STF = 512
ST1 = 256
NLEV = 7

DT_SZ = {F32: 4, BF16: 2, FP8: 1}


class Op:
    __slots__ = ("eng", "fn", "reads", "writes", "dma", "idx", "deps", "inc",
                 "cum", "waits", "tag", "lane", "force")

    def __init__(self, eng, fn, reads, writes, dma, idx, tag):
        self.eng, self.fn = eng, fn
        self.reads, self.writes = list(reads), list(writes)
        self.dma, self.idx, self.tag = dma, idx, tag
        self.deps = []
        self.inc = False
        self.cum = 0
        self.waits = []
        self.lane = 0
        self.force = False


class Sched:
    ENGS = ("te", "dve", "act", "gp", "sy")

    def __init__(self):
        self.ops = []
        self.res = {}
        self.last_on = {e: None for e in self.ENGS}
        self.bar = {e: None for e in self.ENGS}
        self.te_key = None
        self.n_drains = 0

    def drain(self, eng):
        last = self.last_on[eng]
        if last is None:
            return
        o = Op(eng, lambda e: None, [], [], False, len(self.ops), "drain")
        o.deps = [last]
        o.force = True
        self.ops.append(o)
        self.n_drains += 1

    def op(self, eng, fn, reads=(), writes=(), dma=False, tag="", mmkey=None):
        if eng == "te" and mmkey is not None:
            if self.te_key is not None and self.te_key != mmkey:
                self.drain("te")
            self.te_key = mmkey
        idx = len(self.ops)
        o = Op(eng, fn, reads, writes, dma, idx, tag)
        deps = set()
        for r in o.reads:
            e = self.res.get(r)
            if e and e["w"] is not None:
                deps.add(e["w"])
        for w in o.writes:
            e = self.res.get(w)
            if e:
                if e["w"] is not None:
                    deps.add(e["w"])
                deps.update(e["r"])
        for e2, i2 in self.bar.items():
            if i2 is not None and e2 != eng:
                deps.add(i2)
        deps.discard(idx)
        o.deps = sorted(deps)
        for r in o.reads:
            self.res.setdefault(r, {"w": None, "r": []})["r"].append(idx)
        for w in o.writes:
            self.res[w] = {"w": idx, "r": []}
        self.ops.append(o)
        self.last_on[eng] = idx
        return idx

    def barrier(self):
        self.drain("te")
        self.te_key = None
        self.bar = dict(self.last_on)

    NLANES = 4

    @staticmethod
    def _semkey(op):
        if op.dma:
            return (op.eng, "d", op.lane)
        return (op.eng, "c")

    def materialize(self, blk, sems):
        ops = self.ops
        def exempt(dp, o):
            if o.force:
                return False
            if dp.eng != o.eng:
                return False
            if dp.dma and o.dma:
                return True
            if dp.dma or o.dma:
                return o.eng != "gp" and not dp.dma
            return o.eng != "gp"
        for o in ops:
            for d in o.deps:
                dp = ops[d]
                if exempt(dp, o):
                    continue
                dp.inc = True
        counters = {}
        qcount = {}
        for o in ops:
            if o.dma:
                qi = qcount.get(o.eng, 0)
                qcount[o.eng] = qi + 1
                o.lane = qi % self.NLANES
                k = self._semkey(o)
                counters[k] = counters.get(k, 0) + 16
                o.cum = counters[k]
                o.inc = True
            elif o.inc:
                k = self._semkey(o)
                counters[k] = counters.get(k, 0) + 1
                o.cum = counters[k]
        marks = {e: {} for e in self.ENGS}
        for o in ops:
            need = {}
            for d in o.deps:
                dp = ops[d]
                if exempt(dp, o):
                    continue
                k = self._semkey(dp)
                if dp.cum > need.get(k, 0):
                    need[k] = dp.cum
            if o.dma and o.cum > 16:
                k = self._semkey(o)
                if o.cum - 16 > need.get(k, 0):
                    need[k] = o.cum - 16
            wm = marks[o.eng]
            for k, v in need.items():
                if v > wm.get(k, 0):
                    o.waits.append((k, v))
                    wm[k] = v

        def run_engine(ename):
            def body(eng):
                for o in ops:
                    if o.eng != ename:
                        continue
                    for (k, v) in o.waits:
                        eng.wait_ge(sems[k], v)
                    if o.tag == "drain":
                        continue
                    inst = o.fn(eng)
                    if o.inc:
                        if inst is None:
                            raise RuntimeError(f"op {o.idx} {o.tag}: no inst for inc")
                        inst.then_inc(sems[self._semkey(o)], 16 if o.dma else 1)
            return body
        blk.tensor(run_engine("te"))
        blk.vector(run_engine("dve"))
        blk.scalar(run_engine("act"))
        blk.gpsimd(run_engine("gp"))
        blk.sync(run_engine("sy"))


class Arena:
    def __init__(self, ap, total_bytes):
        self.ap = ap
        self.total = total_bytes
        self.off = 0
        self.marks = []

    def alloc(self, cols, dt):
        nbytes = cols * DT_SZ[dt]
        pad = (nbytes + 3) & ~3
        assert self.off + pad <= self.total, f"arena overflow {self.off}+{pad}"
        a = self.off
        self.off += pad
        view = self.ap[:, a // 2:(a + nbytes + 1) // 2]
        if dt != BF16:
            view = view.bitcast(dt)
        return view

    def mark(self):
        self.marks.append(self.off)

    def pop(self):
        self.off = self.marks.pop()


def build(phases=("l0a", "l0f", "l1a", "l1f"), ns0=NS0, l0a_parts=("stats", "proj", "beta", "norms", "chunks", "wout"), nchunks=99, chunk_parts=("xinit", "grams", "solve", "uk", "m1", "o", "ds")):
    nc = bass.Bass("TRN2", target_bir_lowering=False, detect_race_conditions=False)
    dram = {}

    def din(name, shape):
        dram[name] = nc.dram_tensor(name, list(shape), F32, kind="ExternalInput").ap()
        return dram[name]

    x_fm = din("x_fm", [D, T])
    wq0 = din("wq0", [D, D]); wk0 = din("wk0", [D, D]); wv0 = din("wv0", [D, D])
    wb0 = din("wb0", [D, H]); wo0 = din("wo0", [D, D])
    wf8 = din("wf8", [D, D])
    wq1 = din("wq1", [D, D]); wk1 = din("wk1", [D, D]); wv1 = din("wv1", [D, D])
    wfb = din("wfb", [2 * D, D]); wo1 = din("wo1", [D, D])
    wr2 = din("wr2", [D, D]); wk2 = din("wk2", [D, D]); wv2 = din("wv2", [D, D])
    pfsrc = {}
    for nm in ("bf0", "bq1", "bk1", "bv1", "bfb", "bo1", "g1", "b1", "g2", "b2",
               "mur", "muk"):
        pfsrc[nm] = din(nm, [D])
    bb0_h = din("bb0v", [H])
    y_fm = nc.dram_tensor("y_fm", [D, T], F32, kind="ExternalOutput").ap()

    sch = Sched()
    OP = sch.op
    ctxs = []

    def sb(name, shape, dt):
        cm = nc.sbuf_tensor(name, list(shape), dt)
        t = cm.__enter__()
        ctxs.append(cm)
        return t

    def psm(name, shape, dt):
        cm = nc.psum_tensor(name, list(shape), dt)
        t = cm.__enter__()
        ctxs.append(cm)
        return t

    xres = sb("xres", [128, NDC * T], BF16)
    idn = sb("idn", [128, 128], BF16)
    maskSUL = sb("maskSUL", [128, 256], BF16)
    maskSU = maskSUL[:, 0:128]
    maskSL = maskSUL[:, 128:256]
    maskIU = sb("maskIU", [128, 128], BF16)
    onesc128 = sb("onesc128", [128, 1], BF16)
    onesc1 = sb("onesc1", [1, 128], BF16)
    onesrow = sb("onesrow", [1, 512], BF16)
    hmap = sb("hmap", [16, NDC * 128], BF16)
    pmap = sb("pmap", [128, NDC * 16], BF16)
    bb0r = sb("bb0r", [1, H], BF16)
    bb0c = sb("bb0c", [H, 2], F32)
    cvals = sb("cvals", [128, 4], F32)
    pf = {nm: sb("pf_" + nm, [128, NDC], F32) for nm in pfsrc}
    prow = {}
    hmapRbox = [None]
    arena_ap = sb("arena", [128, 70500], BF16)
    arena = Arena(arena_ap, 141000)
    pbw = [psm(f"pbw{j}", [128, 1024], F32) for j in range(4)]
    pb = [pbw[i // 2][:, (i % 2) * 512:(i % 2) * 512 + 512] for i in range(8)]

    sems = {}
    semkeys = [("te", "c"), ("dve", "c"), ("act", "c"), ("gp", "c"), ("sy", "c")]
    for q in ("sy", "gp", "act"):
        for ln in range(Sched.NLANES):
            semkeys.append((q, "d", ln))
    for k in semkeys:
        cm = nc.semaphore("sem_" + "_".join(str(x) for x in k))
        sems[k] = cm.__enter__()
        ctxs.append(cm)

    def emit_consts():
        OP("dve", lambda e: e.memset(idn[:], 0.0), writes=["idn"], tag="c_idn0")
        OP("gp", lambda g: g.affine_select(out=idn[:], in_=idn[:],
                                           compare_op=ALU.not_equal, fill=1.0, base=0,
                                           pattern=[[-1, 128]], channel_multiplier=1),
           reads=["idn"], writes=["idn"], tag="c_idn")

        OP("dve", lambda e: e.memset(maskSU[:], -1.0), writes=["maskSU"], tag="c_msu0")
        OP("gp", lambda g: g.affine_select(out=maskSU[:], in_=maskSU[:],
                                           compare_op=ALU.is_gt, fill=0.0, base=0,
                                           pattern=[[1, 128]], channel_multiplier=-1),
           reads=["maskSU"], writes=["maskSU"], tag="c_msu")

        OP("dve", lambda e: e.memset(maskIU[:], 1.0), writes=["maskIU"], tag="c_miu0")
        OP("gp", lambda g: g.affine_select(out=maskIU[:], in_=maskIU[:],
                                           compare_op=ALU.is_ge, fill=0.0, base=0,
                                           pattern=[[1, 128]], channel_multiplier=-1),
           reads=["maskIU"], writes=["maskIU"], tag="c_miu")
        OP("dve", lambda e: e.memset(maskSL[:], -1.0), writes=["maskSL"], tag="c_msl0")
        OP("gp", lambda g: g.affine_select(out=maskSL[:], in_=maskSL[:],
                                           compare_op=ALU.is_gt, fill=0.0, base=0,
                                           pattern=[[-1, 128]], channel_multiplier=1),
           reads=["maskSL"], writes=["maskSL"], tag="c_msl")

        OP("dve", lambda e: e.memset(onesc128[:], 1.0), writes=["onesc128"])
        OP("dve", lambda e: e.memset(onesc1[:], 1.0), writes=["onesc1"])
        OP("dve", lambda e: e.memset(onesrow[:], 1.0), writes=["onesrow"])
        def c_cv(e):
            e.memset(cvals[:, 0:1], 1e-5)
            e.memset(cvals[:, 1:2], 1e-24)
            return e.memset(cvals[:, 2:3], 1e-6)
        OP("dve", c_cv, writes=["cvals"], tag="c_cv")

        OP("dve", lambda e: e.memset(hmap[:], 1.0), writes=["hmap"], tag="c_hmap0")
        for dc in range(NDC):
            for half in range(2):
                sl = hmap[:, dc * 128 + half * 64: dc * 128 + (half + 1) * 64]
                OP("gp", lambda g, sl=sl, dc=dc, half=half: g.affine_select(
                    out=sl, in_=sl, compare_op=ALU.is_equal, fill=0.0,
                    base=-(2 * dc + half), pattern=[[0, 64]], channel_multiplier=1),
                   reads=["hmap"], writes=["hmap"], tag="c_hmap")

        OP("dve", lambda e: e.memset(pmap[:], 1.0), writes=["pmap"], tag="c_pmap0")
        for dc in range(NDC):
            for half in range(2):
                sl = pmap[64 * half:64 * (half + 1), dc * 16:(dc + 1) * 16]
                OP("gp", lambda g, sl=sl, dc=dc, half=half: g.affine_select(
                    out=sl, in_=sl, compare_op=ALU.is_equal, fill=0.0,
                    base=-(2 * dc + half), pattern=[[1, 16]], channel_multiplier=0),
                   reads=["pmap"], writes=["pmap"], tag="c_pmap")

    def emit_param_loads():
        OP("gp", lambda g: g.dma_start(out=xres[:].rearrange("p (c t) -> p c t", c=NDC),
                                       in_=x_fm.rearrange("(c p) t -> p c t", p=128)),
           writes=[f"xres:{gg}" for gg in range(8)], dma=True, tag="ld_x")
        OP("gp", lambda g: g.dma_start(out=bb0r[:],
                                       in_=bb0_h.rearrange("(x h) -> x h", x=1)),
           writes=["bb0r"], dma=True)
        OP("gp", lambda g: g.dma_start(out=bb0c[:, 0:1],
                                       in_=bb0_h.rearrange("(h x) -> h x", x=1),
                                       allow_slow_non_contiguous=True),
           writes=["bb0c"], dma=True)
        OP("dve", lambda e: e.tensor_scalar_mul(bb0c[:, 1:2], bb0c[:, 0:1], -1.0),
           reads=["bb0c"], writes=["bb0c"], tag="bb0n")
        for nm, src in pfsrc.items():
            OP("sy", lambda g, src=src, nm=nm: g.dma_start(
                out=pf[nm][:], in_=src.rearrange("(c p) -> p c", p=128),
                allow_slow_non_contiguous=True),
               writes=["pf_" + nm], dma=True)


    def load_weight(wsb, src, name, kin=D):
        nkc = kin // 128
        OP("gp", lambda g: g.dma_start(
            out=wsb[:].rearrange("p (c o) -> p c o", c=nkc),
            in_=src.rearrange("(c p) o -> p c o", p=128)),
           writes=[name], dma=True, tag="ld_" + name)

    def xres3(t0, st, dc=None):
        if dc is None:
            return xres[:].rearrange("p (c t) -> p c t", c=NDC)[:, :, t0:t0 + st]
        return xres[:, dc * T + t0:dc * T + t0 + st]

    def gkeys(t0, st):
        g0, g1 = t0 // 512, (t0 + st - 1) // 512
        return [f"xres:{g}" for g in range(g0, g1 + 1)]

    def emit_stats(t0, st, src_fn, src_keys, sq, stt, eps, l1a_eps=None, gb=None):
        OP("act", lambda e: e.activation(
            sq[:].rearrange("p (c t) -> p c t", c=NDC)[:, :, 0:st],
            src_fn(), AF.Square),
           reads=src_keys, writes=["sq"], tag="st_sq")
        for dc in range(NDC):
            OP("te", lambda e, dc=dc: e.matmul(
                pb[0][0:1, 0:st], onesc128[:], src_fn(dc),
                start=(dc == 0), stop=(dc == NDC - 1)),
               reads=src_keys + ["onesc128"], writes=["pb0"], tag="st_s1",
               mmkey=(0, 128))
        for dc in range(NDC):
            OP("te", lambda e, dc=dc: e.matmul(
                pb[1][0:1, 0:st], onesc128[:],
                sq[:, dc * (sq.shape[1] // NDC): dc * (sq.shape[1] // NDC) + st],
                start=(dc == 0), stop=(dc == NDC - 1)),
               reads=["sq", "onesc128"], writes=["pb1"], tag="st_s2",
               mmkey=(0, 128))
        mu, va, tmp = stt["mu"], stt["va"], stt["tmp"]
        arow, srow = stt["arow"], stt["srow"]
        arow_bf, srow_bf = stt["arow_bf"], stt["srow_bf"]
        KMU = stt.get("k_mu", "murow")
        KVA = stt.get("k_va", "varow")
        KTMP = stt.get("k_tmp", "tmprow")
        KSR = stt.get("k_srow", "srow")
        KABF = stt.get("k_abf", "arow_bf")
        KSBF = stt.get("k_sbf", "srow_bf")
        OP("dve", lambda e: e.tensor_scalar_mul(mu[0:1, 0:st], pb[0][0:1, 0:st], 1.0 / D),
           reads=["pb0"], writes=[KMU], tag="st_mu")
        OP("dve", lambda e: e.tensor_scalar_mul(va[0:1, 0:st], pb[1][0:1, 0:st], 1.0 / D),
           reads=["pb1"], writes=[KVA], tag="st_ex2")
        OP("dve", lambda e: e.scalar_tensor_tensor(
            tmp[0:1, 0:st], mu[0:1, 0:st], -1.0, mu[0:1, 0:st], ALU.mult, ALU.mult),
           reads=[KMU], writes=[KTMP], tag="st_msq")
        OP("dve", lambda e: e.tensor_add(va[0:1, 0:st], va[0:1, 0:st], tmp[0:1, 0:st]),
           reads=[KVA, KTMP], writes=[KVA], tag="st_var")
        OP("act", lambda e: e.activation(tmp[0:1, 0:st], va[0:1, 0:st], AF.Ln,
                                         bias=cvals[0:1, eps:eps + 1]),
           reads=[KVA, "cvals"], writes=[KTMP], tag="st_ln")
        OP("act", lambda e: e.activation(arow[0:1, 0:st], tmp[0:1, 0:st], AF.Exp, scale=-0.5),
           reads=[KTMP], writes=["arow"], tag="st_rstd")
        if l1a_eps is not None:
            OP("dve", lambda e: e.scalar_tensor_tensor(
                tmp[0:1, 0:st], arow[0:1, 0:st], 1.0, arow[0:1, 0:st], ALU.mult, ALU.mult),
               reads=["arow"], writes=[KTMP], tag="dln_r2")
            OP("dve", lambda e: e.tensor_mul(tmp[0:1, 0:st], tmp[0:1, 0:st], va[0:1, 0:st]),
               reads=[KTMP, KVA], writes=[KTMP], tag="dln_w")
            OP("act", lambda e: e.activation(tmp[0:1, 0:st], tmp[0:1, 0:st], AF.Ln,
                                             bias=cvals[0:1, l1a_eps:l1a_eps + 1]),
               reads=[KTMP, "cvals"], writes=[KTMP], tag="dln_ln")
            OP("act", lambda e: e.activation(tmp[0:1, 0:st], tmp[0:1, 0:st], AF.Exp,
                                             scale=-0.5),
               reads=[KTMP], writes=[KTMP], tag="dln_u")
            OP("dve", lambda e: e.tensor_mul(arow[0:1, 0:st], arow[0:1, 0:st],
                                             tmp[0:1, 0:st]),
               reads=["arow", KTMP], writes=["arow"], tag="dln_a")
        OP("dve", lambda e: e.scalar_tensor_tensor(
            srow[0:1, 0:st], mu[0:1, 0:st], -1.0, arow[0:1, 0:st], ALU.mult, ALU.mult),
           reads=[KMU, "arow"], writes=[KSR], tag="st_srow")
        OP("act", lambda e: e.activation(arow_bf[0:1, 0:st], arow[0:1, 0:st], AF.Identity),
           reads=["arow"], writes=[KABF], tag="st_abf")
        OP("act", lambda e: e.activation(srow_bf[0:1, 0:st], srow[0:1, 0:st], AF.Identity),
           reads=[KSR], writes=[KSBF], tag="st_sbf")
        absc, sbsc = stt["absc"], stt["sbsc"]
        if gb is None:
            OP("te", lambda e: e.matmul(pb[0][:, 0:st], onesc1[:], arow_bf[0:1, 0:st],
                                        start=True, stop=True),
               reads=[KABF, "onesc1"], writes=["pb0"], tag="bc_a", mmkey=(0, 1))
            OP("te", lambda e: e.matmul(pb[1][:, 0:st], onesc1[:], srow_bf[0:1, 0:st],
                                        start=True, stop=True),
               reads=[KSBF, "onesc1"], writes=["pb1"], tag="bc_s", mmkey=(0, 1))
            OP("act", lambda e: e.activation(absc[:, 0:st], pb[0][:, 0:st], AF.Identity),
               reads=["pb0"], writes=["absc"], tag="bc_ae")
            OP("act", lambda e: e.activation(sbsc[:, 0:st], pb[1][:, 0:st], AF.Identity),
               reads=["pb1"], writes=["sbsc"], tag="bc_se")
        else:
            gname, bname = gb
            gcol, bcol = stt["gcolb"], stt["bcolb"]
            for dc in range(NDC):
                OP("te", lambda e, dc=dc: e.matmul(
                    pb[0][:, 0:st], gcol[:, dc:dc + 1], arow_bf[0:1, 0:st],
                    start=True, stop=True),
                   reads=["arow_bf", "gcolb"], writes=["pb0"], tag="bcg_a")
                OP("act", lambda e, dc=dc: e.activation(
                    absc[:, dc * st:(dc + 1) * st], pb[0][:, 0:st], AF.Identity),
                   reads=["pb0"], writes=["absc"], tag="bcg_ae")
                OP("te", lambda e, dc=dc: e.matmul(
                    pb[1][:, 0:st], gcol[:, dc:dc + 1], srow_bf[0:1, 0:st],
                    start=True, stop=False),
                   reads=["srow_bf", "gcolb"], writes=["pb1"], tag="bcg_s")
                OP("te", lambda e, dc=dc: e.matmul(
                    pb[1][:, 0:st], bcol[:, dc:dc + 1], onesrow[0:1, 0:st],
                    start=False, stop=True),
                   reads=["onesrow", "bcolb"], writes=["pb1"], tag="bcg_b")
                OP("act", lambda e, dc=dc: e.activation(
                    sbsc[:, dc * st:(dc + 1) * st], pb[1][:, 0:st], AF.Identity),
                   reads=["pb1"], writes=["sbsc"], tag="bcg_se")

    def emit_apply(st, src_fn, src_keys, stt, out_ap, out_key, lntmp, perdc=False,
                   out_dt_note=None):
        absc, sbsc = stt["absc"], stt["sbsc"]
        if not perdc:
            a_b = absc[:, 0:st].rearrange("p (c t) -> p c t", c=1).broadcast_to([128, NDC, st])
            s_b = sbsc[:, 0:st].rearrange("p (c t) -> p c t", c=1).broadcast_to([128, NDC, st])
        else:
            a_b = absc[:].rearrange("p (c t) -> p c t", c=NDC)
            s_b = sbsc[:].rearrange("p (c t) -> p c t", c=NDC)
        OP("dve", lambda e: e.tensor_tensor(lntmp, src_fn(), a_b, ALU.mult),
           reads=src_keys + ["absc"], writes=["sq"], tag="ap1")
        OP("dve", lambda e: e.tensor_tensor(out_ap, lntmp, s_b, ALU.add),
           reads=["sq", "sbsc"], writes=[out_key], tag="ap2")

    def mkstats(st, big=False):
        d = {}
        d["mu"] = arena.alloc(st, F32)
        d["va"] = arena.alloc(st, F32)
        d["tmp"] = arena.alloc(st, F32)
        d["arow"] = arena.alloc(st, F32)
        d["srow"] = arena.alloc(st, F32)
        d["arow_bf"] = arena.alloc(st, BF16)
        d["srow_bf"] = arena.alloc(st, BF16)
        n = NDC * st if big else st
        d["absc"] = arena.alloc(n, BF16)
        d["sbsc"] = arena.alloc(n, BF16)
        return d

    def emit_l0a():
        arena.mark()
        WQ = arena.alloc(NDC * D, BF16); load_weight(WQ, wq0, "WQ0")
        WK = arena.alloc(NDC * D, BF16); load_weight(WK, wk0, "WK0")
        WV = arena.alloc(NDC * D, BF16); load_weight(WV, wv0, "WV0")
        WO = arena.alloc(NDC * D, BF16); load_weight(WO, wo0, "WO0")
        WB = arena.alloc(NDC * H, BF16); load_weight(WB, wb0, "WB0")
        st = ST0
        xhat = arena.alloc(NDC * st, BF16)
        sq = arena.alloc(NDC * st, BF16)
        qdm = arena.alloc(NDC * st, BF16)
        kdm = arena.alloc(NDC * st, BF16)
        khdm = arena.alloc(NDC * st, BF16)
        kbdm = arena.alloc(NDC * st, BF16)
        ktok = arena.alloc(2 * D, BF16)
        khtok = arena.alloc(2 * D, BF16)
        vtok = arena.alloc(2 * D, BF16)
        odm = kdm
        beta_dm = arena.alloc(st, F32)
        beta_dmb = arena.alloc(st, BF16)
        beta_tok = arena.alloc(2 * 16, F32)
        beta_tokb = arena.alloc(2 * 16, BF16)
        rn_dm = arena.alloc(st, F32)
        rn_dmb = arena.alloc(st, BF16)
        rn_tokb = arena.alloc(2 * 16, BF16)
        brn_dmb = arena.alloc(st, BF16)
        stt = mkstats(st)
        lntmp = sq
        xall = arena.alloc(H * CH, BF16)
        npw = [arena.alloc(1024, BF16) for _ in range(3)] + [sq[:, 0:1024]]
        npow = []
        for c4 in range(4):
            npow += [npw[c4][:, 0:512], npw[c4][:, 512:1024]]
        npkey = [f"npow{i}" for i in range(6)] + ["sq", "sq"]
        ltall = [arena.alloc(H * CH, BF16), None]
        ltall[1] = xhat
        ltkeys = ["ltall0", "xhat"]
        ukdm = ktok[0:64, 0:H * CH]
        m1sb = arena.alloc(H * 64, BF16)
        m1b = arena.alloc(H * 64, BF16)
        oasb = arena.alloc(512, BF16)
        s32 = arena.alloc(H * 64, F32)
        sb16 = arena.alloc(H * 64, BF16)
        OP("dve", lambda e: e.memset(s32[:], 0.0), writes=["s32"], tag="s_init")
        OP("dve", lambda e: e.memset(sb16[:], 0.0), writes=["sb16"], tag="sb_init")

        for s in range(ns0):
            t0 = s * st
            xk = gkeys(t0, st)
            if "stats" not in l0a_parts:
                continue
            emit_stats(t0, st, lambda dc=None, t0=t0: xres3(t0, st, dc), xk, sq, stt, 0)
            emit_apply(st, lambda t0=t0: xres3(t0, st), xk, stt,
                       xhat[:].rearrange("p (c t) -> p c t", c=NDC), "xhat",
                       lntmp[:].rearrange("p (c t) -> p c t", c=NDC))

            def proj_fm(wsb, wkey, out, okey):
                for oc in range(NDC):
                    bk = 2 + (oc % 2)
                    for kc in range(NDC):
                        OP("te", lambda e, oc=oc, kc=kc, bk=bk, wsb=wsb: e.matmul(
                            pb[bk][:, 0:st],
                            wsb[:, kc * D + oc * 128: kc * D + (oc + 1) * 128],
                            xhat[:, kc * st:(kc + 1) * st],
                            start=(kc == 0), stop=(kc == NDC - 1)),
                           reads=[wkey, "xhat"], writes=[f"pb{bk}"], tag=f"pj_{okey}",
                           mmkey=(0, 128))
                    OP("act", lambda e, oc=oc, bk=bk, out=out: e.activation(
                        out[:, oc * st:(oc + 1) * st], pb[bk][:, 0:st], AF.Identity),
                       reads=[f"pb{bk}"], writes=[okey], tag=f"pje_{okey}")
            if "proj" not in l0a_parts:
                continue
            proj_fm(WQ, "WQ0", qdm, "qdm")
            proj_fm(WK, "WK0", kdm, "kdmov")

            def proj_tok(wsb, wkey, out, okey):
                for tt in range(st // 128):
                    for oh in range(2):
                        bk = 2 + ((tt * 2 + oh) % 2)
                        for kc in range(NDC):
                            OP("te", lambda e, tt=tt, oh=oh, kc=kc, bk=bk, wsb=wsb: e.matmul(
                                pb[bk][:, 0:512],
                                xhat[:, kc * st + tt * 128: kc * st + (tt + 1) * 128],
                                wsb[:, kc * D + oh * 512: kc * D + (oh + 1) * 512],
                                start=(kc == 0), stop=(kc == NDC - 1)),
                               reads=[wkey, "xhat"], writes=[f"pb{bk}"], tag=f"pt_{okey}",
                               mmkey=(0, 128))
                        OP("act", lambda e, tt=tt, oh=oh, bk=bk, out=out: e.activation(
                            out[:, tt * D + oh * 512: tt * D + (oh + 1) * 512],
                            pb[bk][:, 0:512], AF.Identity),
                           reads=[f"pb{bk}"], writes=[okey], tag=f"pte_{okey}")
            proj_tok(WK, "WK0", ktok, "ktokov")
            proj_tok(WV, "WV0", vtok, "vtok")

            if "beta" not in l0a_parts:
                continue
            for kc in range(NDC):
                OP("te", lambda e, kc=kc: e.matmul(
                    pb[0][0:H, 0:st], WB[:, kc * H:(kc + 1) * H],
                    xhat[:, kc * st:(kc + 1) * st], start=(kc == 0),
                    stop=(kc == NDC - 1)),
                   reads=["WB0", "xhat"], writes=["pb0"], tag="bdm", mmkey=(0, 128))
            OP("act", lambda e: e.activation(beta_dm[0:H, 0:st], pb[0][0:H, 0:st],
                                             AF.Exp, scale=-1.0, bias=bb0c[0:H, 1:2]),
               reads=["pb0", "bb0c"], writes=["beta_dm"], tag="bdm_e")
            OP("dve", lambda e: e.tensor_scalar_add(beta_dm[0:H, 0:st],
                                                    beta_dm[0:H, 0:st], 1.0),
               reads=["beta_dm"], writes=["beta_dm"], tag="bdm_1")
            OP("act", lambda e: e.activation(beta_dm[0:H, 0:st], beta_dm[0:H, 0:st], AF.Ln),
               reads=["beta_dm"], writes=["beta_dm"], tag="bdm_l")
            OP("act", lambda e: e.activation(beta_dm[0:H, 0:st], beta_dm[0:H, 0:st],
                                             AF.Exp, scale=-1.0),
               reads=["beta_dm"], writes=["beta_dm"], tag="bdm_x")
            OP("act", lambda e: e.activation(beta_dmb[0:H, 0:st], beta_dm[0:H, 0:st],
                                             AF.Identity),
               reads=["beta_dm"], writes=["beta_dmb"], tag="bdm_bf")
            for tt in range(st // 128):
                OP("te", lambda e, tt=tt: e.matmul(
                    pb[1][:, tt * 16:(tt + 1) * 16],
                    beta_dmb[0:H, tt * 128:(tt + 1) * 128], idn[0:H, 0:H],
                    start=True, stop=True),
                   reads=["beta_dmb", "idn"], writes=["pb1"], tag="btok_t",
                   mmkey=(0, 16))
            OP("act", lambda e: e.activation(beta_tok[:, 0:32], pb[1][:, 0:32],
                                             AF.Identity),
               reads=["pb1"], writes=["beta_tok"], tag="btok_e")
            OP("act", lambda e: e.activation(beta_tokb[:, 0:32], pb[1][:, 0:32],
                                             AF.Identity),
               reads=["pb1"], writes=["beta_tokb"], tag="btok_bf")

            if "norms" not in l0a_parts:
                continue
            OP("act", lambda e: e.activation(
                sq[:].rearrange("p (c t) -> p c t", c=NDC),
                kdm[:].rearrange("p (c t) -> p c t", c=NDC), AF.Square),
               reads=["kdmov"], writes=["sq"], tag="kn_sq")
            for dc in range(NDC):
                OP("te", lambda e, dc=dc: e.matmul(
                    pb[0][0:H, 0:st], pmap[:, dc * 16:(dc + 1) * 16],
                    sq[:, dc * st:(dc + 1) * st],
                    start=(dc == 0), stop=(dc == NDC - 1)),
                   reads=["sq", "pmap"], writes=["pb0"], tag="kn_mm", mmkey=(0, 128))
            OP("act", lambda e: e.activation(rn_dm[0:H, 0:st], pb[0][0:H, 0:st],
                                             AF.Ln, bias=cvals[0:H, 1:2]),
               reads=["pb0", "cvals"], writes=["rn_dm"], tag="kn_ln")
            OP("act", lambda e: e.activation(rn_dm[0:H, 0:st], rn_dm[0:H, 0:st],
                                             AF.Exp, scale=-0.5),
               reads=["rn_dm"], writes=["rn_dm"], tag="kn_exp")
            OP("act", lambda e: e.activation(rn_dmb[0:H, 0:st], rn_dm[0:H, 0:st],
                                             AF.Identity),
               reads=["rn_dm"], writes=["rn_dmb"], tag="kn_bf")
            OP("dve", lambda e: e.tensor_mul(brn_dmb[0:H, 0:st], rn_dmb[0:H, 0:st],
                                             beta_dmb[0:H, 0:st]),
               reads=["rn_dmb", "beta_dmb"], writes=["brn_dmb"], tag="kn_brn")
            for tt in range(st // 128):
                OP("te", lambda e, tt=tt: e.matmul(
                    pb[1][:, tt * 16:(tt + 1) * 16],
                    rn_dmb[0:H, tt * 128:(tt + 1) * 128], idn[0:H, 0:H],
                    start=True, stop=True),
                   reads=["rn_dmb", "idn"], writes=["pb1"], tag="rnt_t", mmkey=(0, 16))
            OP("act", lambda e: e.activation(
                rn_tokb[:, 0:32], pb[1][:, 0:32], AF.Identity),
               reads=["pb1"], writes=["rn_tokb"], tag="rnt_e")
            for dc in range(NDC):
                OP("te", lambda e, dc=dc: e.matmul(
                    pb[0][:, 0:st], hmap[:, dc * 128:(dc + 1) * 128], rn_dmb[0:H, 0:st],
                    start=True, stop=True),
                   reads=["rn_dmb", "hmap"], writes=["pb0"], tag="khdm_bc",
                   mmkey=(0, 16))
                OP("dve", lambda e, dc=dc: e.tensor_tensor(
                    khdm[:, dc * st:(dc + 1) * st], kdm[:, dc * st:(dc + 1) * st],
                    pb[0][:, 0:st], ALU.mult),
                   reads=["kdmov", "pb0"], writes=["khdm"], tag="khdm_tt")
                OP("te", lambda e, dc=dc: e.matmul(
                    pb[1][:, 0:st], hmap[:, dc * 128:(dc + 1) * 128], brn_dmb[0:H, 0:st],
                    start=True, stop=True),
                   reads=["brn_dmb", "hmap"], writes=["pb1"], tag="kbdm_bc",
                   mmkey=(0, 16))
                OP("dve", lambda e, dc=dc: e.tensor_tensor(
                    kbdm[:, dc * st:(dc + 1) * st], kdm[:, dc * st:(dc + 1) * st],
                    pb[1][:, 0:st], ALU.mult),
                   reads=["kdmov", "pb1"], writes=["kbdm"], tag="kbdm_tt")
            for tt in range(st // 128):
                rnb = rn_tokb[:, tt * 16:(tt + 1) * 16].rearrange(
                    "p (h x) -> p h x", x=1).broadcast_to([128, 16, 64])
                OP("dve", lambda e, tt=tt, rnb=rnb: e.tensor_tensor(
                    khtok[:, tt * D:(tt + 1) * D].rearrange("p (h d) -> p h d", d=64),
                    ktok[:, tt * D:(tt + 1) * D].rearrange("p (h d) -> p h d", d=64),
                    rnb, ALU.mult),
                   reads=["ktokov", "rn_tokb"], writes=["khtok"], tag="khtok_tt")

            if "chunks" not in l0a_parts:
                continue
            FULLCP = ("xinit", "grams", "solve", "uk", "m1", "o", "ds")
            nch = min(st // CH, nchunks)
            if tuple(chunk_parts) == FULLCP and nch == 2:
                def ec(ci, cp):
                    emit_chunk(s, ci, qdm, kbdm, khdm, khtok, vtok, odm,
                               beta_tokb, xall, npow, npkey, npw, ltall[ci % 2],
                               ltkeys[ci % 2], ukdm, m1sb, m1b, oasb, s32,
                               sb16, st, cp)
                ec(0, ("xinit", "grams", "solve", "uk", "m1"))
                ec(1, ("xinit", "grams", "solve"))
                ec(0, ("o", "ds"))
                ec(1, ("uk", "m1"))
                ec(1, ("o", "ds"))
            else:
                for ci in range(nch):
                    emit_chunk(s, ci, qdm, kbdm, khdm, khtok, vtok, odm,
                               beta_tokb, xall, npow, npkey, npw, ltall[ci % 2],
                               ltkeys[ci % 2], ukdm, m1sb, m1b, oasb, s32,
                               sb16, st, chunk_parts)

            if "wout" not in l0a_parts:
                continue
            for oc in range(NDC):
                bk = 2 + (oc % 2)
                for kc in range(NDC):
                    OP("te", lambda e, oc=oc, kc=kc, bk=bk: e.matmul(
                        pb[bk][:, 0:st],
                        WO[:, kc * D + oc * 128: kc * D + (oc + 1) * 128],
                        odm[:, kc * st:(kc + 1) * st],
                        start=(kc == 0), stop=(kc == NDC - 1)),
                       reads=["WO0", "kdmov"], writes=[f"pb{bk}"], tag="wo",
                       mmkey=(0, 128))
                OP("dve", lambda e, oc=oc, bk=bk, t0=t0: e.tensor_tensor(
                    xres[:, oc * T + t0: oc * T + t0 + st],
                    xres[:, oc * T + t0: oc * T + t0 + st],
                    pb[bk][:, 0:st], ALU.add),
                   reads=[f"pb{bk}"] + xk, writes=xk, tag="res0")
        arena.pop()

    def emit_chunk(s, ci, qdm, kbdm, khdm, khtok, vtok, odm, beta_tokb,
                   xall, npow, npkey, npw, ltc, ltkey, ukdm, m1sb, m1b, oasb, s32, sb16, st,
                   cp=("xinit", "grams", "solve", "uk", "m1", "o", "ds")):
        tc = ci * CH
        if "xinit" in cp:
          OP("act", lambda e: e.activation(
            xall[:].rearrange("p (h w) -> p h w", h=H)[:, :, 0:64],
            vtok[:, ci * D:(ci + 1) * D].rearrange("p (h d) -> p h d", d=64),
            AF.Identity),
             reads=["vtok"], writes=["xall"], tag=f"xiV{s}_{ci}")
          OP("act", lambda e: e.activation(
            xall[:].rearrange("p (h w) -> p h w", h=H)[:, :, 64:128],
            khtok[:, ci * D:(ci + 1) * D].rearrange("p (h d) -> p h d", d=64),
            AF.Identity),
             reads=["khtok"], writes=["xall"], tag=f"xiK{s}_{ci}")
        chains = []
        for par in range(2):
            groups = [[par + 2 * i for i in range(4)],
                      [8 + par + 2 * i for i in range(4)]]
            for gg, hs in enumerate(groups):
                chains.append((2 * par + gg, 64 * par, hs))
        if "grams" in cp:
            for cid, b0, hs in chains:
                for q, h in enumerate(hs):
                    dc = h // 2
                    OP("te", lambda e, q=q, dc=dc, b0=b0: e.matmul(
                        pb[6][:, q * 128:(q + 1) * 128],
                        kbdm[b0:b0 + 64, dc * st + tc: dc * st + tc + CH],
                        khdm[b0:b0 + 64, dc * st + tc: dc * st + tc + CH],
                        start=True, stop=True),
                       reads=["kbdm", "khdm"], writes=["pb6"], tag="grA",
                       mmkey=(b0, 64))
                for q, h in enumerate(hs):
                    dc = h // 2
                    OP("te", lambda e, q=q, dc=dc, b0=b0: e.matmul(
                        pb[7][:, q * 128:(q + 1) * 128],
                        khdm[b0:b0 + 64, dc * st + tc: dc * st + tc + CH],
                        kbdm[b0:b0 + 64, dc * st + tc: dc * st + tc + CH],
                        start=True, stop=True),
                       reads=["kbdm", "khdm"], writes=["pb7"], tag="grAT",
                       mmkey=(b0, 64))
                msul = maskSUL[:].rearrange("p (c u t) -> p c u t", c=2,
                                            u=1).broadcast_to([128, 2, 4, 128])
                OP("dve", lambda e, cid=cid, msul=msul: e.tensor_tensor(
                    npw[cid][:].rearrange("p (c q t) -> p c q t", c=2, q=4),
                    pbw[3][:, 0:1024].rearrange("p (c q t) -> p c q t", c=2, q=4),
                    msul, ALU.mult),
                   reads=["pb6", "pb7", "maskSU", "maskSL"],
                   writes=[npkey[2 * cid], npkey[2 * cid + 1]], tag="grNTe")
                for q, h in enumerate(hs):
                    dc = h // 2
                    OP("te", lambda e, q=q, dc=dc, b0=b0: e.matmul(
                        pb[3][:, q * 128:(q + 1) * 128],
                        kbdm[b0:b0 + 64, dc * st + tc: dc * st + tc + CH],
                        qdm[b0:b0 + 64, dc * st + tc: dc * st + tc + CH],
                        start=True, stop=True),
                       reads=["kbdm", "qdm"], writes=["pb3"], tag="grL",
                       mmkey=(b0, 64))
                mbi = maskIU[:].rearrange("p (c t) -> p c t", c=1).broadcast_to(
                    [128, 4, 128])
                j0, po = hs[0] // 2, (hs[0] % 2) * 128
                OP("dve", lambda e, j0=j0, po=po, mbi=mbi: e.tensor_tensor(
                    ltc[:].rearrange("p (hp x) -> p hp x", x=256)[
                        :, j0:j0 + 4, po:po + 128],
                    pb[3][:, 0:512].rearrange("p (q t) -> p q t", q=4), mbi, ALU.mult),
                   reads=["pb3", "maskIU"], writes=[ltkey], tag="grLe")
        if "solve" in cp:
            for k in range(NLEV):
                for cid, b0, hs in chains:
                    h0 = hs[0]
                    cur, ckey = npow[2 * cid], npkey[2 * cid]
                    curT, ckeyT = npow[2 * cid + 1], npkey[2 * cid + 1]
                    ab = 4 if cid % 2 == 0 else 5
                    nb, tb = (6, 7) if cid % 2 == 0 else (0, 1)
                    sqw = pbw[3] if cid % 2 == 0 else pbw[0]
                    for q, h in enumerate(hs):
                        xsl = xall[:, h * 128:(h + 1) * 128]
                        OP("te", lambda e, q=q, xsl=xsl, cur=cur, ab=ab: e.matmul(
                            pb[ab][:, q * 128:(q + 1) * 128],
                            cur[:, q * 128:(q + 1) * 128], xsl,
                            start=True, stop=True),
                           reads=["xall", ckey], writes=[f"pb{ab}"], tag="apN",
                           mmkey=(0, 128))
                    j0, po = h0 // 2, (h0 % 2) * 128
                    xsl4 = xall[:].rearrange("p (hp x) -> p hp x", x=256)[
                        :, j0:j0 + 4, po:po + 128]
                    OP("dve", lambda e, xsl4=xsl4, ab=ab: e.tensor_tensor(
                        xsl4, xsl4,
                        pb[ab][:, 0:512].rearrange("p (q t) -> p q t", q=4), ALU.add),
                       reads=[f"pb{ab}", "xall"], writes=["xall"], tag="apE")
                    if k < NLEV - 1:
                        lastsq = (k == NLEV - 2)
                        for q in range(4):
                            OP("te", lambda e, q=q, cur=cur, curT=curT, nb=nb:
                               e.matmul(
                                pb[nb][:, q * 128:(q + 1) * 128],
                                curT[:, q * 128:(q + 1) * 128],
                                cur[:, q * 128:(q + 1) * 128], start=True, stop=True),
                               reads=[ckey, ckeyT], writes=[f"pb{nb}"], tag="sqN",
                               mmkey=(0, 128))
                            if not lastsq:
                                OP("te", lambda e, q=q, cur=cur, curT=curT, tb=tb:
                                   e.matmul(
                                    pb[tb][:, q * 128:(q + 1) * 128],
                                    cur[:, q * 128:(q + 1) * 128],
                                    curT[:, q * 128:(q + 1) * 128],
                                    start=True, stop=True),
                                   reads=[ckey, ckeyT], writes=[f"pb{tb}"],
                                   tag="sqT", mmkey=(0, 128))
                        eng = "dve" if cid == 0 else "act"
                        nq = 4 if lastsq else 8
                        dst = (npow[2 * cid] if lastsq else npw[cid])
                        wk = [ckey] if lastsq else [ckey, ckeyT]
                        rk = [f"pb{nb}"] if lastsq else [f"pb{nb}", f"pb{tb}"]
                        src = (pb[nb] if lastsq else sqw[:, 0:1024])
                        if eng == "dve":
                            OP("dve", lambda e, dst=dst, src=src, nq=nq:
                               e.tensor_copy(
                                dst[:].rearrange("p (q t) -> p q t", q=nq),
                                src[:].rearrange("p (q t) -> p q t", q=nq)),
                               reads=rk, writes=wk, tag="sqNTe")
                        else:
                            OP("act", lambda e, dst=dst, src=src, nq=nq:
                               e.activation(
                                dst[:].rearrange("p (q t) -> p q t", q=nq),
                                src[:].rearrange("p (q t) -> p q t", q=nq),
                                AF.Identity),
                               reads=rk, writes=wk, tag="sqNTe")
        if "uk" in cp:
          for h in range(H):
            q = h % 4
            OP("te", lambda e, h=h, q=q: e.matmul(
                pb[3][0:64, q * 128:(q + 1) * 128],
                xall[:, h * 128 + 64:(h + 1) * 128], idn[:],
                start=True, stop=True),
               reads=["xall", "idn"], writes=["pb3"], tag="ukT", mmkey=(0, 128))
            if q == 3:
                OP("act", lambda e, h=h: e.activation(
                    ukdm[:, (h - 3) * 128:(h + 1) * 128],
                    pb[3][0:64, 0:512], AF.Identity),
                   reads=["pb3"], writes=["ktokov"], tag="ukTe")
        if "m1" in cp:
          for w in range(2):
            for i in range(8):
                h = w * 8 + i
                OP("te", lambda e, h=h, i=i: e.matmul(
                    pb[2][:, i * 64:(i + 1) * 64],
                    ukdm[:, h * 128:(h + 1) * 128],
                    sb16[0:64, h * 64:(h + 1) * 64],
                    start=True, stop=True),
                   reads=["ktokov", "sb16"], writes=["pb2"], tag="m1", mmkey=(0, 64))
            vsl = xall[:].rearrange("p (h w) -> p h w", h=H)[:, w * 8:(w + 1) * 8, 0:64]
            OP("dve", lambda e, w=w, vsl=vsl: e.tensor_tensor(
                m1sb[:, w * 512:(w + 1) * 512].rearrange("p (h v) -> p h v", h=8),
                vsl, pb[2][:, 0:512].rearrange("p (h v) -> p h v", h=8), ALU.subtract),
               reads=["pb2", "xall"], writes=["m1sb"], tag="m1e")
          bb = beta_tokb[:, ci * 16:(ci + 1) * 16].rearrange(
              "p (h x) -> p h x", x=1).broadcast_to([128, 16, 64])
          OP("dve", lambda e, bb=bb: e.tensor_tensor(
              m1b[:].rearrange("p (h v) -> p h v", h=H),
              m1sb[:].rearrange("p (h v) -> p h v", h=H), bb, ALU.mult),
             reads=["m1sb", "beta_tokb"], writes=["m1b"], tag="m1b")
        if "o" not in cp:
            return
        for w in range(2):
            for par in range(2):
                b0 = 64 * par
                for pr in range(4):
                    h = w * 8 + 2 * pr + par
                    OP("te", lambda e, h=h, b0=b0, pr=pr, dc=h // 2: e.matmul(
                        pb[3][b0:b0 + 64, pr * 128:(pr + 1) * 128],
                        sb16[b0:b0 + 64, h * 64:(h + 1) * 64],
                        qdm[b0:b0 + 64, dc * st + tc: dc * st + tc + CH],
                        start=True, stop=True, tile_position=(b0, b0)),
                       reads=["sb16", "qdm"], writes=["pb3"], tag="oq",
                       mmkey=(b0, 64))
            for i in range(8):
                h = w * 8 + i
                b0 = 64 * (h % 2)
                pr = i // 2
                OP("te", lambda e, h=h, b0=b0, pr=pr: e.matmul(
                    pb[2][b0:b0 + 64, pr * 128:(pr + 1) * 128],
                    m1sb[:, h * 64:(h + 1) * 64],
                    ltc[:, h * 128:(h + 1) * 128],
                    start=True, stop=True, tile_position=(0, b0)),
                   reads=["m1sb", ltkey], writes=["pb2"], tag="ol", mmkey=(0, 128))
            OP("act", lambda e: e.activation(oasb[:], pb[3][:, 0:512], AF.Identity),
               reads=["pb3"], writes=["oasb"], tag="oae")
            oout = odm[:].rearrange("p (c t) -> p c t", c=NDC)[
                :, w * 4:(w + 1) * 4, tc:tc + CH]
            OP("dve", lambda e, oout=oout: e.tensor_tensor(
                oout, pb[2][:, 0:512].rearrange("p (q t) -> p q t", q=4),
                oasb[:].rearrange("p (q t) -> p q t", q=4), ALU.add),
               reads=["pb2", "oasb"], writes=["kdmov"], tag="oe")
        if "ds" in cp:
          for w in range(2):
            for i in range(8):
                h = w * 8 + i
                for dup in range(2):
                    OP("te", lambda e, h=h, i=i, dup=dup: e.matmul(
                        pb[3][64 * dup:64 * dup + 64, i * 64:(i + 1) * 64],
                        khtok[:, ci * D + h * 64: ci * D + (h + 1) * 64],
                        m1b[:, h * 64:(h + 1) * 64],
                        start=True, stop=True, tile_position=(0, 64 * dup)),
                       reads=["khtok", "m1b"], writes=["pb3"], tag="ds",
                       mmkey=(0, 128))
            OP("dve", lambda e, w=w: e.tensor_tensor(
                s32[:, w * 512:(w + 1) * 512], s32[:, w * 512:(w + 1) * 512],
                pb[3][:, 0:512], ALU.add),
               reads=["pb3", "s32"], writes=["s32"], tag="sadd")
            OP("act", lambda e, w=w: e.activation(
                sb16[:, w * 512:(w + 1) * 512], s32[:, w * 512:(w + 1) * 512],
                AF.Identity),
               reads=["s32"], writes=["sb16"], tag="sbf")

    def emit_l0f():
        arena.mark()
        W8 = arena.alloc(NDC * D, FP8)
        nkc = NDC
        OP("gp", lambda g: g.dma_start(
            out=W8[:].rearrange("p (c o) -> p c o", c=nkc),
            in_=wf8.rearrange("(c p) o -> p c o", p=128)),
           writes=["W8"], dma=True, tag="ld_W8")
        st = STF
        xq = arena.alloc(NDC * st, FP8)
        sq = arena.alloc(NDC * st, BF16)
        lntmp = arena.alloc(NDC * st, BF16)
        stt = mkstats(st)
        for s in range(T // st):
            t0 = s * st
            xk = gkeys(t0, st)
            emit_stats(t0, st, lambda dc=None, t0=t0: xres3(t0, st, dc), xk, sq, stt, 0)
            emit_apply(st, lambda t0=t0: xres3(t0, st), xk, stt,
                       xq[:].rearrange("p (c t) -> p c t", c=NDC), "xq",
                       lntmp[:].rearrange("p (c t) -> p c t", c=NDC))
            for oc in range(NDC):
                bk = 2 + (oc % 2)
                for kc in range(NDC):
                    OP("te", lambda e, oc=oc, kc=kc, bk=bk: e.matmul(
                        pb[bk][:, 0:st],
                        W8[:, kc * D + oc * 128: kc * D + (oc + 1) * 128],
                        xq[:, kc * st:(kc + 1) * st],
                        start=(kc == 0), stop=(kc == NDC - 1)),
                       reads=["W8", "xq"], writes=[f"pb{bk}"], tag="f8",
                       mmkey=(0, 128, "f8"))
                OP("dve", lambda e, oc=oc, bk=bk, t0=t0: e.scalar_tensor_tensor(
                    xres[:, oc * T + t0: oc * T + t0 + st],
                    pb[bk][:, 0:st], pf["bf0"][:, oc:oc + 1],
                    xres[:, oc * T + t0: oc * T + t0 + st], ALU.add, ALU.add),
                   reads=[f"pb{bk}", "pf_bf0"] + xk, writes=xk, tag="res1")
        arena.pop()

    def _lp(fn):
        def g(e):
            with nc.allow_low_precision(reason="bf16 reduce (prototype-validated)"):
                return fn(e)
        return g

    def _stat_math(st, stt, eps, l1a_eps=None):
        mu, va, tmp = stt["mu"], stt["va"], stt["tmp"]
        arow, srow = stt["arow"], stt["srow"]
        arow_bf, srow_bf = stt["arow_bf"], stt["srow_bf"]
        KMU = stt.get("k_mu", "murow")
        KVA = stt.get("k_va", "varow")
        KTMP = stt.get("k_tmp", "tmprow")
        KSR = stt.get("k_srow", "srow")
        KABF = stt.get("k_abf", "arow_bf")
        KSBF = stt.get("k_sbf", "srow_bf")
        OP("dve", lambda e: e.tensor_scalar_mul(mu[0:1, 0:st], pb[0][0:1, 0:st], 1.0 / D),
           reads=["pb0"], writes=[KMU], tag="st_mu")
        OP("dve", lambda e: e.tensor_scalar_mul(va[0:1, 0:st], pb[1][0:1, 0:st], 1.0 / D),
           reads=["pb1"], writes=[KVA], tag="st_ex2")
        OP("dve", lambda e: e.scalar_tensor_tensor(
            tmp[0:1, 0:st], mu[0:1, 0:st], -1.0, mu[0:1, 0:st], ALU.mult, ALU.mult),
           reads=[KMU], writes=[KTMP], tag="st_msq")
        OP("dve", lambda e: e.tensor_add(va[0:1, 0:st], va[0:1, 0:st], tmp[0:1, 0:st]),
           reads=[KVA, KTMP], writes=[KVA], tag="st_var")
        OP("act", lambda e: e.activation(tmp[0:1, 0:st], va[0:1, 0:st], AF.Ln,
                                         bias=cvals[0:1, eps:eps + 1]),
           reads=[KVA, "cvals"], writes=[KTMP], tag="st_ln")
        OP("act", lambda e: e.activation(arow[0:1, 0:st], tmp[0:1, 0:st], AF.Exp,
                                         scale=-0.5),
           reads=[KTMP], writes=["arow"], tag="st_rstd")
        if l1a_eps is not None:
            OP("dve", lambda e: e.scalar_tensor_tensor(
                tmp[0:1, 0:st], arow[0:1, 0:st], 1.0, arow[0:1, 0:st],
                ALU.mult, ALU.mult),
               reads=["arow"], writes=[KTMP], tag="dln_r2")
            OP("dve", lambda e: e.tensor_mul(tmp[0:1, 0:st], tmp[0:1, 0:st],
                                             va[0:1, 0:st]),
               reads=[KTMP, KVA], writes=[KTMP], tag="dln_w")
            OP("act", lambda e: e.activation(tmp[0:1, 0:st], tmp[0:1, 0:st], AF.Ln,
                                             bias=cvals[0:1, l1a_eps:l1a_eps + 1]),
               reads=[KTMP, "cvals"], writes=[KTMP], tag="dln_ln")
            OP("act", lambda e: e.activation(tmp[0:1, 0:st], tmp[0:1, 0:st], AF.Exp,
                                             scale=-0.5),
               reads=[KTMP], writes=[KTMP], tag="dln_u")
            OP("dve", lambda e: e.tensor_mul(arow[0:1, 0:st], arow[0:1, 0:st],
                                             tmp[0:1, 0:st]),
               reads=["arow", KTMP], writes=["arow"], tag="dln_a")
        OP("dve", lambda e: e.scalar_tensor_tensor(
            srow[0:1, 0:st], mu[0:1, 0:st], -1.0, arow[0:1, 0:st], ALU.mult, ALU.mult),
           reads=[KMU, "arow"], writes=[KSR], tag="st_srow")
        OP("act", lambda e: e.activation(arow_bf[0:1, 0:st], arow[0:1, 0:st],
                                         AF.Identity),
           reads=["arow"], writes=[KABF], tag="st_abf")
        OP("act", lambda e: e.activation(srow_bf[0:1, 0:st], srow[0:1, 0:st],
                                         AF.Identity),
           reads=[KSR], writes=[KSBF], tag="st_sbf")

    def _bcg(st, stt, gname, bname, pitch=None):
        if pitch is None:
            pitch = st
        arow_bf, srow_bf = stt["arow_bf"], stt["srow_bf"]
        KABF = stt.get("k_abf", "arow_bf")
        KSBF = stt.get("k_sbf", "srow_bf")
        absc, sbsc = stt["absc"], stt["sbsc"]
        grow, brow = prow[gname], prow[bname]
        for dc in range(NDC):
            ba, bs = (0, 1) if dc % 2 == 0 else (2, 3)
            OP("te", lambda e, dc=dc, grow=grow, ba=ba: e.matmul(
                pb[ba][:, 0:st], grow[0:1, dc * 128:(dc + 1) * 128],
                arow_bf[0:1, 0:st], start=True, stop=True),
               reads=[KABF, "pr_" + gname], writes=[f"pb{ba}"], tag="bcg_a",
               mmkey=(0, 1))
            OP("act", lambda e, dc=dc, ba=ba: e.activation(
                absc[:, dc * pitch:dc * pitch + st], pb[ba][:, 0:st], AF.Identity),
               reads=[f"pb{ba}"], writes=["absc"], tag="bcg_ae")
            OP("te", lambda e, dc=dc, grow=grow, bs=bs: e.matmul(
                pb[bs][:, 0:st], grow[0:1, dc * 128:(dc + 1) * 128],
                srow_bf[0:1, 0:st], start=True, stop=False),
               reads=[KSBF, "pr_" + gname], writes=[f"pb{bs}"], tag="bcg_s",
               mmkey=(0, 1))
            OP("te", lambda e, dc=dc, brow=brow, bs=bs: e.matmul(
                pb[bs][:, 0:st], brow[0:1, dc * 128:(dc + 1) * 128],
                onesrow[0:1, 0:st], start=False, stop=True),
               reads=["onesrow", "pr_" + bname], writes=[f"pb{bs}"], tag="bcg_b",
               mmkey=(0, 1))
            OP("act", lambda e, dc=dc, bs=bs: e.activation(
                sbsc[:, dc * pitch:dc * pitch + st], pb[bs][:, 0:st], AF.Identity),
               reads=[f"pb{bs}"], writes=["sbsc"], tag="bcg_se")

    def emit_l1a():
        arena.mark()
        WQ = arena.alloc(NDC * D, BF16); load_weight(WQ, wq1, "WQ1")
        WK = arena.alloc(NDC * D, BF16); load_weight(WK, wk1, "WK1")
        WV = arena.alloc(NDC * D, BF16); load_weight(WV, wv1, "WV1")
        WFB = arena.alloc(2 * NDC * D, BF16); load_weight(WFB, wfb, "WFB", kin=2 * D)
        WO = arena.alloc(NDC * D, BF16); load_weight(WO, wo1, "WO1")
        for nm in ("g1", "b1", "g2", "b2"):
            prow[nm] = arena.alloc(D, BF16)
            OP("gp", lambda g, nm=nm: g.dma_start(
                out=prow[nm][0:1, :], in_=pfsrc[nm].rearrange("(x d) -> x d", x=1)),
               writes=["pr_" + nm], dma=True)
        hmapR = arena.alloc(NDC * 128, BF16)
        OP("dve", lambda e: e.memset(hmapR[:], 0.0), writes=["hmapR"], tag="c_hmapR0")
        OP("dve", lambda e: e.memset(hmapR[64:80, :], 1.0), writes=["hmapR"],
           tag="c_hmapR1")
        for dc in range(NDC):
            for half in range(2):
                sl = hmapR[64:80, dc * 128 + half * 64: dc * 128 + (half + 1) * 64]
                OP("gp", lambda g, sl=sl, dc=dc, half=half: g.affine_select(
                    out=sl, in_=sl, compare_op=ALU.is_equal, fill=0.0,
                    base=-(2 * dc + half), pattern=[[0, 64]], channel_multiplier=1),
                   reads=["hmapR"], writes=["hmapR"], tag="c_hmapR")
        st = ST1
        W = st + 4
        xw = arena.alloc(NDC * W, BF16)
        vw = arena.alloc(NDC * W, BF16)
        sq = vw
        lntmp = vw
        qw = arena.alloc(NDC * W, BF16)
        kw = arena.alloc(NDC * W, BF16)
        outf, outr, zt = qw, kw, qw
        scsb = arena.alloc(3 * st, F32)
        tmpc = scsb
        stt = {}
        stt["mu"] = arena.alloc(W, F32)
        stt["va"] = arena.alloc(W, F32)
        stt["tmp"] = arena.alloc(W, F32)
        stt["arow"] = arena.alloc(W, F32)
        stt["srow"] = stt["va"]
        mubits = stt["mu"][:].bitcast(BF16)
        stt["arow_bf"] = mubits[:, 0:W]
        stt["srow_bf"] = mubits[:, W:2 * W]
        stt["k_srow"] = "varow"
        stt["k_abf"] = "murow"
        stt["k_sbf"] = "murow"
        stt["absc"] = arena.alloc(NDC * W, BF16)
        stt["sbsc"] = arena.alloc(NDC * W, BF16)
        attnb = stt["sbsc"][:, 0:3 * st]
        mx = stt["tmp"]
        prods4 = stt["arow"][:, 0:st].bitcast(BF16)

        KXW = "vw"
        KQ = "qw"
        KK_ = "kw"
        KSC = "scsb"
        KMX = "tmprow"

        def xw3():
            return xw[:].rearrange("p (c t) -> p c t", c=NDC)

        for s in range(T // st):
            t0 = s * st
            first, last = (s == 0), (s == T // st - 1)
            rd0 = max(t0 - 2, 0)
            rd1 = min(t0 + st + 2, T)
            dst0 = rd0 - (t0 - 2)
            rlen = rd1 - rd0
            xk = gkeys(rd0, rlen)

            def srcw(dc=None, rd0=rd0, rlen=rlen):
                if dc is None:
                    return xres[:].rearrange("p (c t) -> p c t", c=NDC)[
                        :, :, rd0:rd0 + rlen]
                return xres[:, dc * T + rd0:dc * T + rd0 + rlen]

            if first:
                OP("dve", lambda e: e.memset(xw3()[:, :, 0:2], 0.0),
                   writes=["xw"], tag="xwpadL")
            if last:
                OP("dve", lambda e: e.memset(xw3()[:, :, W - 2:W], 0.0),
                   writes=["xw"], tag="xwpadR")
            OP("act", lambda e, rlen=rlen, srcw=srcw: e.activation(
                sq[:].rearrange("p (c t) -> p c t", c=NDC)[:, :, 0:rlen],
                srcw(), AF.Square),
               reads=xk, writes=[KXW], tag="l1a_sq")
            for dc in range(NDC):
                OP("te", lambda e, dc=dc, rlen=rlen, srcw=srcw: e.matmul(
                    pb[0][0:1, 0:rlen], onesc128[:], srcw(dc),
                    start=(dc == 0), stop=(dc == NDC - 1)),
                   reads=xk + ["onesc128"], writes=["pb0"], tag="l1a_s1",
                   mmkey=(0, 128))
            for dc in range(NDC):
                OP("te", lambda e, dc=dc, rlen=rlen: e.matmul(
                    pb[1][0:1, 0:rlen], onesc128[:], sq[:, dc * W:dc * W + rlen],
                    start=(dc == 0), stop=(dc == NDC - 1)),
                   reads=[KXW, "onesc128"], writes=["pb1"], tag="l1a_s2",
                   mmkey=(0, 128))
            _stat_math(rlen, stt, 0, l1a_eps=2)
            _bcg(rlen, stt, "g1", "b1", pitch=W)
            a_b = stt["absc"][:].rearrange("p (c t) -> p c t", c=NDC)[:, :, 0:rlen]
            s_b = stt["sbsc"][:].rearrange("p (c t) -> p c t", c=NDC)[:, :, 0:rlen]
            OP("dve", lambda e, rlen=rlen, srcw=srcw, a_b=a_b: e.tensor_tensor(
                lntmp[:].rearrange("p (c t) -> p c t", c=NDC)[:, :, 0:rlen],
                srcw(), a_b, ALU.mult),
               reads=xk + ["absc"], writes=[KXW], tag="l1a_ap1")
            OP("dve", lambda e, rlen=rlen, dst0=dst0, s_b=s_b: e.tensor_tensor(
                xw3()[:, :, dst0:dst0 + rlen],
                lntmp[:].rearrange("p (c t) -> p c t", c=NDC)[:, :, 0:rlen],
                s_b, ALU.add),
               reads=[KXW, "sbsc"], writes=["xw"], tag="l1a_ap2")

            def proj_w(wsb, wkey, out, okey, bias):
                for dc in range(NDC):
                    bank = 1 + (dc % 2)
                    for kc in range(NDC):
                        OP("te", lambda e, kc=kc, dc=dc, wsb=wsb, bank=bank: e.matmul(
                            pb[bank][:, 0:W],
                            wsb[:, kc * D + dc * 128: kc * D + (dc + 1) * 128],
                            xw[:, kc * W:(kc + 1) * W],
                            start=(kc == 0), stop=(kc == NDC - 1)),
                           reads=[wkey, "xw"], writes=[f"pb{bank}"], tag="l1a_p",
                           mmkey=(0, 128))
                    OP("act", lambda e, dc=dc, out=out, bias=bias, bank=bank:
                       e.activation(
                        out[:, dc * W:(dc + 1) * W], pb[bank][:, 0:W], AF.Identity,
                        bias=pf[bias][:, dc:dc + 1]),
                       reads=[f"pb{bank}", "pf_" + bias], writes=[okey], tag="l1a_pe")

            proj_w(WQ, "WQ1", qw, KQ, "bq1")
            proj_w(WK, "WK1", kw, KK_, "bk1")
            for tname, tkey in ((qw, KQ), (kw, KK_)):
                if first:
                    OP("dve", lambda e, tname=tname: e.memset(
                        tname[:].rearrange("p (c t) -> p c t", c=NDC)[:, :, 0:2], 0.0),
                       reads=[tkey], writes=[tkey], tag="padL")
                if last:
                    OP("dve", lambda e, tname=tname: e.memset(
                        tname[:].rearrange("p (c t) -> p c t", c=NDC)[:, :, W - 2:W],
                        0.0),
                       reads=[tkey], writes=[tkey], tag="padR")

            pidx = 0
            for dc in range(NDC):
                for di, sgn in ((0, 1), (1, -1)):
                    for j in range(3):
                        qs = qw[:, dc * W + 2 - sgn:dc * W + 2 - sgn + st]
                        ks = kw[:, dc * W + 2 - sgn * j:dc * W + 2 - sgn * j + st]
                        sl = pidx % 2
                        pidx += 1
                        psl = prods4[:, sl * st:(sl + 1) * st]
                        OP("dve", lambda e, qs=qs, ks=ks, psl=psl: e.tensor_tensor(
                            psl, qs, ks, ALU.mult),
                           reads=[KQ, KK_], writes=[f"pr{sl}"], tag="l1a_prod")
                        bank = (2 + j) if di == 0 else (5 + j)
                        tp = (0, 0) if di == 0 else (0, 64)
                        r0 = 0 if di == 0 else 64
                        OP("te", lambda e, dc=dc, bank=bank, tp=tp, r0=r0,
                           psl=psl: e.matmul(
                            pb[bank][r0:r0 + 16, 0:st],
                            pmap[:, dc * 16:(dc + 1) * 16], psl,
                            start=(dc == 0), stop=(dc == NDC - 1),
                            tile_position=tp),
                           reads=[f"pr{sl}", "pmap"], writes=[f"pb{bank}"],
                           tag="l1a_sc", mmkey=(0, 128))
            SC = 1.0 / 8.0
            if s == 0:
                OP("dve", lambda e: e.memset(scsb[0:64, :], 0.0),
                   writes=[KSC], tag="sc_z")
            for j in range(3):
                OP("act", lambda e, j=j: e.activation(
                    scsb[0:16, j * st:(j + 1) * st], pb[2 + j][0:16, 0:st],
                    AF.Identity, scale=SC),
                   reads=[f"pb{2 + j}"], writes=[KSC], tag="sc_eF")
                OP("act", lambda e, j=j: e.activation(
                    scsb[64:80, j * st:(j + 1) * st], pb[5 + j][64:80, 0:st],
                    AF.Identity, scale=SC),
                   reads=[f"pb{5 + j}"], writes=[KSC], tag="sc_eR")
            for dc in range(NDC):
                bank = dc % 2
                for kc in range(NDC):
                    OP("te", lambda e, kc=kc, dc=dc, bank=bank: e.matmul(
                        pb[bank][:, 0:W],
                        WV[:, kc * D + dc * 128: kc * D + (dc + 1) * 128],
                        xw[:, kc * W:(kc + 1) * W],
                        start=(kc == 0), stop=(kc == NDC - 1)),
                       reads=["WV1", "xw"], writes=[f"pb{bank}"], tag="l1a_pv",
                       mmkey=(0, 128))
                OP("act", lambda e, dc=dc, bank=bank: e.activation(
                    vw[:, dc * W:(dc + 1) * W], pb[bank][:, 0:W], AF.Identity,
                    bias=pf["bv1"][:, dc:dc + 1]),
                   reads=[f"pb{bank}", "pf_bv1"], writes=[KXW], tag="l1a_ev")
            if first:
                OP("dve", lambda e: e.memset(
                    vw[:].rearrange("p (c t) -> p c t", c=NDC)[:, :, 0:2], 0.0),
                   reads=[KXW], writes=[KXW], tag="vpadL")
            if last:
                OP("dve", lambda e: e.memset(
                    vw[:].rearrange("p (c t) -> p c t", c=NDC)[:, :, W - 2:W], 0.0),
                   reads=[KXW], writes=[KXW], tag="vpadR")

            sc3 = scsb[0:80, 0:3 * st].rearrange("p (j t) -> p t j", j=3)
            OP("dve", lambda e, sc3=sc3: e.tensor_reduce(
                mx[0:80, 0:st], sc3, mybir.AxisListType.X, ALU.max),
               reads=[KSC], writes=[KMX], tag="sm_mx")
            mxb2 = mx[0:80, 0:st].rearrange("p (t x) -> p t x", x=1).broadcast_to(
                [80, st, 3])
            OP("dve", lambda e, sc3=sc3, mxb2=mxb2: e.tensor_tensor(
                sc3, sc3, mxb2, ALU.subtract),
               reads=[KSC, KMX], writes=[KSC], tag="sm_sub")
            OP("act", lambda e: e.activation(scsb[0:80, 0:3 * st],
                                             scsb[0:80, 0:3 * st], AF.Exp),
               reads=[KSC], writes=[KSC], tag="sm_exp")
            OP("dve", lambda e, sc3=sc3: e.tensor_reduce(
                mx[0:80, 0:st], sc3, mybir.AxisListType.X, ALU.add),
               reads=[KSC], writes=[KMX], tag="sm_den")
            OP("act", lambda e: e.activation(mx[0:80, 0:st], mx[0:80, 0:st], AF.Ln),
               reads=[KMX], writes=[KMX], tag="sm_ln")
            OP("act", lambda e: e.activation(mx[0:80, 0:st], mx[0:80, 0:st],
                                             AF.Exp, scale=-1.0),
               reads=[KMX], writes=[KMX], tag="sm_rcp")
            OP("dve", lambda e, sc3=sc3, mxb2=mxb2: e.tensor_tensor(
                attnb[0:80, 0:3 * st].rearrange("p (j t) -> p t j", j=3),
                sc3, mxb2, ALU.mult),
               reads=[KSC, KMX], writes=["sbsc"], tag="sm_attn")

            for di, sgn, hm, hkey, r0, outt, okey in (
                    (0, 1, hmap, "hmap", 0, outf, KQ),
                    (1, -1, hmapR, "hmapR", 64, outr, KK_)):
                for dc in range(NDC):
                    w0 = 2 if dc % 2 == 0 else 3
                    for j in range(3):
                        bank = 2 * w0 + (0 if j < 2 else 1)
                        col = (j % 2) * st
                        OP("te", lambda e, dc=dc, j=j, hm=hm, r0=r0, bank=bank,
                           col=col: e.matmul(
                            pb[bank][:, col:col + st],
                            hm[r0:r0 + 16, dc * 128:(dc + 1) * 128],
                            attnb[r0:r0 + 16, j * st:(j + 1) * st],
                            start=True, stop=True),
                           reads=["sbsc", hkey], writes=[f"pb{bank}"],
                           tag="l1a_abc", mmkey=(r0, 16))
                    vv = vw[:, dc * W:(dc + 1) * W]
                    tsl = tmpc[:, 0:3 * st]
                    OP("dve", lambda e, vv=vv, sgn=sgn, w0=w0, tsl=tsl:
                       e.tensor_tensor(
                        tsl.rearrange("p (t j) -> p t j", j=3),
                        pbw[w0][:, 0:3 * st].rearrange("p (j t) -> p t j", j=3),
                        _shiftn(vv, sgn, 0, 3), ALU.mult),
                       reads=[f"pb{2 * w0}", f"pb{2 * w0 + 1}", KXW],
                       writes=[KSC], tag="l1a_cm")
                    OP("dve", _lp(lambda e, dc=dc, outt=outt, tsl=tsl:
                       e.tensor_reduce(
                        outt[:, dc * W + 2:dc * W + 2 + st],
                        tsl.rearrange("p (t j) -> p t j", j=3),
                        mybir.AxisListType.X, ALU.add)),
                       reads=[KSC], writes=[okey], tag="l1a_cred")

            for oc in range(NDC):
                bank = 1 + (oc % 2)
                for kc in range(2 * NDC):
                    srct = outf if kc < NDC else outr
                    srk = KQ if kc < NDC else KK_
                    kcl = kc % NDC
                    OP("te", lambda e, oc=oc, kc=kc, kcl=kcl, srct=srct, bank=bank:
                       e.matmul(
                        pb[bank][:, 0:st],
                        WFB[:, kc * D + oc * 128: kc * D + (oc + 1) * 128],
                        srct[:, kcl * W + 2:kcl * W + 2 + st],
                        start=(kc == 0), stop=(kc == 2 * NDC - 1)),
                       reads=["WFB", srk], writes=[f"pb{bank}"], tag="l1a_fb",
                       mmkey=(0, 128))
                OP("dve", lambda e, oc=oc, bank=bank: e.scalar_tensor_tensor(
                    xw[:, oc * W + 2:oc * W + 2 + st], pb[bank][:, 0:st],
                    pf["bfb"][:, oc:oc + 1],
                    xw[:, oc * W + 2:oc * W + 2 + st], ALU.add, ALU.add),
                   reads=[f"pb{bank}", "pf_bfb", "xw"], writes=["xw"], tag="l1a_mid")

            midv = xw3()[:, :, 2:2 + st]
            OP("act", lambda e, midv=midv: e.activation(
                sq[:].rearrange("p (c t) -> p c t", c=NDC)[:, :, 0:st],
                midv, AF.Square),
               reads=["xw"], writes=[KXW], tag="m_sq")
            for dc in range(NDC):
                OP("te", lambda e, dc=dc: e.matmul(
                    pb[0][0:1, 0:st], onesc128[:],
                    xw[:, dc * W + 2:dc * W + 2 + st],
                    start=(dc == 0), stop=(dc == NDC - 1)),
                   reads=["xw", "onesc128"], writes=["pb0"], tag="m_s1",
                   mmkey=(0, 128))
            for dc in range(NDC):
                OP("te", lambda e, dc=dc: e.matmul(
                    pb[1][0:1, 0:st], onesc128[:], sq[:, dc * W:dc * W + st],
                    start=(dc == 0), stop=(dc == NDC - 1)),
                   reads=[KXW, "onesc128"], writes=["pb1"], tag="m_s2",
                   mmkey=(0, 128))
            _stat_math(st, stt, 2)
            _bcg(st, stt, "g2", "b2", pitch=W)
            a_b2 = stt["absc"][:].rearrange("p (c t) -> p c t", c=NDC)[:, :, 0:st]
            s_b2 = stt["sbsc"][:].rearrange("p (c t) -> p c t", c=NDC)[:, :, 0:st]
            OP("dve", lambda e, a_b2=a_b2, midv=midv: e.tensor_tensor(
                lntmp[:].rearrange("p (c t) -> p c t", c=NDC)[:, :, 0:st],
                midv, a_b2, ALU.mult),
               reads=["xw", "absc"], writes=[KXW], tag="z_ap1")
            OP("dve", lambda e, s_b2=s_b2: e.tensor_tensor(
                zt[:].rearrange("p (c t) -> p c t", c=NDC)[:, :, 0:st],
                lntmp[:].rearrange("p (c t) -> p c t", c=NDC)[:, :, 0:st],
                s_b2, ALU.add),
               reads=[KXW, "sbsc"], writes=[KQ], tag="z_ap2")

            OP("act", lambda e: e.add_instruction(mybir.InstLoadActFuncSet(
                name=nc.get_next_instruction_name(), act_func_set_id=10,
                ins=[], outs=[])), tag="tbl10")
            for oc in range(NDC):
                bank = 1 + (oc % 2)
                for kc in range(NDC):
                    OP("te", lambda e, oc=oc, kc=kc, bank=bank: e.matmul(
                        pb[bank][:, 0:st],
                        WO[:, kc * D + oc * 128: kc * D + (oc + 1) * 128],
                        zt[:, kc * W:kc * W + st],
                        start=(kc == 0), stop=(kc == NDC - 1)),
                       reads=["WO1", KQ], writes=[f"pb{bank}"], tag="l1a_wo",
                       mmkey=(0, 128))
                OP("act", lambda e, oc=oc, bank=bank: e.activation(
                    lntmp[:, oc * W:oc * W + st], pb[bank][:, 0:st], AF.Gelu,
                    bias=pf["bo1"][:, oc:oc + 1]),
                   reads=[f"pb{bank}", "pf_bo1"], writes=[KXW], tag="l1a_gel")
            OP("act", lambda e: e.add_instruction(mybir.InstLoadActFuncSet(
                name=nc.get_next_instruction_name(), act_func_set_id=6,
                ins=[], outs=[])), tag="tbl6b")
            for oc in range(NDC):
                OP("dve", lambda e, oc=oc, t0=t0: e.tensor_tensor(
                    xres[:, oc * T + t0: oc * T + t0 + st],
                    xres[:, oc * T + t0: oc * T + t0 + st],
                    lntmp[:, oc * W:oc * W + st], ALU.add),
                   reads=[KXW] + xk, writes=xk, tag="l1a_res")
        arena.pop()

    def _shiftn(vv, sgn, j0, nj):
        base = vv[:, 2 - sgn * j0:2 - sgn * j0 + ST1]
        return bass.AP(base.tensor, base.offset,
                       [list(base.ap[0]), [1, ST1], [-sgn, nj]])

    def emit_l1f():
        arena.mark()
        WR = arena.alloc(NDC * D, BF16); load_weight(WR, wr2, "WR2")
        WK2 = arena.alloc(NDC * D, BF16); load_weight(WK2, wk2, "WK2")
        WV2 = arena.alloc(NDC * D, BF16); load_weight(WV2, wv2, "WV2")
        st = STF
        Wd = st + 1
        xh = arena.alloc(NDC * Wd, BF16)
        sq = arena.alloc(NDC * st, BF16)
        lntmp = arena.alloc(NDC * st, BF16)
        xr = arena.alloc(NDC * st, BF16)
        xkk = arena.alloc(NDC * st, BF16)
        rt = arena.alloc(NDC * st, BF16)
        kact = arena.alloc(NDC * st, BF16)
        ytile = arena.alloc(NDC * st, F32)
        stt = mkstats(st)
        OP("dve", lambda e: e.memset(
            xh[:].rearrange("p (c t) -> p c t", c=NDC)[:, :, 0:1], 0.0),
           writes=["xh"], tag="xh0")
        for s in range(T // st):
            t0 = s * st
            xk = gkeys(t0, st)
            emit_stats(t0, st, lambda dc=None, t0=t0: xres3(t0, st, dc), xk, sq, stt, 0)
            a_b = stt["absc"][:, 0:st].rearrange("p (c t) -> p c t", c=1).broadcast_to(
                [128, NDC, st])
            s_b = stt["sbsc"][:, 0:st].rearrange("p (c t) -> p c t", c=1).broadcast_to(
                [128, NDC, st])
            OP("dve", lambda e, t0=t0: e.tensor_tensor(
                lntmp[:].rearrange("p (c t) -> p c t", c=NDC),
                xres3(t0, st), a_b, ALU.mult),
               reads=xk + ["absc"], writes=["lntmp"], tag="f_ap1")
            OP("dve", lambda e: e.tensor_tensor(
                xh[:].rearrange("p (c t) -> p c t", c=NDC)[:, :, 1:Wd],
                lntmp[:].rearrange("p (c t) -> p c t", c=NDC), s_b, ALU.add),
               reads=["lntmp", "sbsc"], writes=["xh"], tag="f_ap2")
            mur3 = pf["mur"][:].rearrange("p (c x) -> p c x", x=1).broadcast_to(
                [128, NDC, st])
            muk3 = pf["muk"][:].rearrange("p (c x) -> p c x", x=1).broadcast_to(
                [128, NDC, st])
            xcur = xh[:].rearrange("p (c t) -> p c t", c=NDC)[:, :, 1:Wd]
            xprv = xh[:].rearrange("p (c t) -> p c t", c=NDC)[:, :, 0:Wd - 1]
            OP("dve", lambda e, mur3=mur3: e.tensor_tensor(
                lntmp[:].rearrange("p (c t) -> p c t", c=NDC), xcur, mur3, ALU.mult),
               reads=["xh", "pf_mur"], writes=["lntmp"], tag="f_xr1")
            OP("dve", lambda e, mur3=mur3: e.scalar_tensor_tensor(
                xr[:].rearrange("p (c t) -> p c t", c=NDC), xprv, 1.0,
                lntmp[:].rearrange("p (c t) -> p c t", c=NDC), ALU.mult, ALU.add)
               if False else
               e.tensor_tensor(sq[:].rearrange("p (c t) -> p c t", c=NDC),
                               xprv, mur3, ALU.mult),
               reads=["xh", "pf_mur"], writes=["sq"], tag="f_xr2")
            OP("dve", lambda e: e.tensor_tensor(
                xr[:].rearrange("p (c t) -> p c t", c=NDC),
                lntmp[:].rearrange("p (c t) -> p c t", c=NDC),
                sq[:].rearrange("p (c t) -> p c t", c=NDC), ALU.subtract)
               if False else None, tag="noop") if False else None
            OP("dve", lambda e: e.tensor_tensor(
                xr[:].rearrange("p (c t) -> p c t", c=NDC),
                lntmp[:].rearrange("p (c t) -> p c t", c=NDC), xprv, ALU.add),
               reads=["lntmp", "xh"], writes=["xr"], tag="f_xr3")
            OP("dve", lambda e: e.tensor_tensor(
                xr[:].rearrange("p (c t) -> p c t", c=NDC),
                xr[:].rearrange("p (c t) -> p c t", c=NDC),
                sq[:].rearrange("p (c t) -> p c t", c=NDC), ALU.subtract),
               reads=["xr", "sq"], writes=["xr"], tag="f_xr4")
            OP("dve", lambda e, muk3=muk3: e.tensor_tensor(
                lntmp[:].rearrange("p (c t) -> p c t", c=NDC), xcur, muk3, ALU.mult),
               reads=["xh", "pf_muk"], writes=["lntmp"], tag="f_xk1")
            OP("dve", lambda e, muk3=muk3: e.tensor_tensor(
                sq[:].rearrange("p (c t) -> p c t", c=NDC), xprv, muk3, ALU.mult),
               reads=["xh", "pf_muk"], writes=["sq"], tag="f_xk2")
            OP("dve", lambda e: e.tensor_tensor(
                xkk[:].rearrange("p (c t) -> p c t", c=NDC),
                lntmp[:].rearrange("p (c t) -> p c t", c=NDC), xprv, ALU.add),
               reads=["lntmp", "xh"], writes=["xkk"], tag="f_xk3")
            OP("dve", lambda e: e.tensor_tensor(
                xkk[:].rearrange("p (c t) -> p c t", c=NDC),
                xkk[:].rearrange("p (c t) -> p c t", c=NDC),
                sq[:].rearrange("p (c t) -> p c t", c=NDC), ALU.subtract),
               reads=["xkk", "sq"], writes=["xkk"], tag="f_xk4")
            OP("dve", lambda e: e.tensor_copy(
                xh[:].rearrange("p (c t) -> p c t", c=NDC)[:, :, 0:1],
                xh[:].rearrange("p (c t) -> p c t", c=NDC)[:, :, Wd - 1:Wd]),
               reads=["xh"], writes=["xh"], tag="f_carry")
            for oc in range(NDC):
                bank = 1 + (oc % 2)
                for kc in range(NDC):
                    OP("te", lambda e, oc=oc, kc=kc, bank=bank: e.matmul(
                        pb[bank][:, 0:st],
                        WR[:, kc * D + oc * 128: kc * D + (oc + 1) * 128],
                        xr[:, kc * st:(kc + 1) * st],
                        start=(kc == 0), stop=(kc == NDC - 1)),
                       reads=["WR2", "xr"], writes=[f"pb{bank}"], tag="f_r",
                       mmkey=(0, 128))
                OP("act", lambda e, oc=oc, bank=bank: e.activation(
                    rt[:, oc * st:(oc + 1) * st], pb[bank][:, 0:st], AF.Exp,
                    scale=-1.0),
                   reads=[f"pb{bank}"], writes=["rt"], tag="f_re")
            OP("dve", lambda e: e.tensor_scalar_add(rt[:], rt[:], 1.0),
               reads=["rt"], writes=["rt"], tag="f_r1")
            OP("act", lambda e: e.activation(rt[:], rt[:], AF.Ln),
               reads=["rt"], writes=["rt"], tag="f_rl")
            OP("act", lambda e: e.activation(rt[:], rt[:], AF.Exp, scale=-1.0),
               reads=["rt"], writes=["rt"], tag="f_rx")
            for oc in range(NDC):
                bank = 1 + (oc % 2)
                for kc in range(NDC):
                    OP("te", lambda e, oc=oc, kc=kc, bank=bank: e.matmul(
                        pb[bank][:, 0:st],
                        WK2[:, kc * D + oc * 128: kc * D + (oc + 1) * 128],
                        xkk[:, kc * st:(kc + 1) * st],
                        start=(kc == 0), stop=(kc == NDC - 1)),
                       reads=["WK2", "xkk"], writes=[f"pb{bank}"], tag="f_k",
                       mmkey=(0, 128))
                OP("dve", lambda e, oc=oc, bank=bank: e.tensor_scalar_max(
                    sq[:, oc * st:(oc + 1) * st], pb[bank][:, 0:st], 0.0),
                   reads=[f"pb{bank}"], writes=["sq"], tag="f_km")
            OP("act", lambda e: e.activation(kact[:], sq[:, 0:NDC * st], AF.Square),
               reads=["sq"], writes=["kact"], tag="f_k2")
            for oc in range(NDC):
                bank = 1 + (oc % 2)
                for kc in range(NDC):
                    OP("te", lambda e, oc=oc, kc=kc, bank=bank: e.matmul(
                        pb[bank][:, 0:st],
                        WV2[:, kc * D + oc * 128: kc * D + (oc + 1) * 128],
                        kact[:, kc * st:(kc + 1) * st],
                        start=(kc == 0), stop=(kc == NDC - 1)),
                       reads=["WV2", "kact"], writes=[f"pb{bank}"], tag="f_v",
                       mmkey=(0, 128))
                OP("dve", lambda e, oc=oc, bank=bank: e.tensor_tensor(
                    lntmp[:, oc * st:(oc + 1) * st], pb[bank][:, 0:st],
                    rt[:, oc * st:(oc + 1) * st], ALU.mult),
                   reads=[f"pb{bank}", "rt"], writes=["lntmp"], tag="f_vm")
                OP("dve", lambda e, oc=oc, t0=t0: e.tensor_tensor(
                    ytile[:, oc * st:(oc + 1) * st],
                    xres[:, oc * T + t0:oc * T + t0 + st],
                    lntmp[:, oc * st:(oc + 1) * st], ALU.add),
                   reads=["lntmp"] + xk, writes=["ytile"], tag="f_y")
            OP("sy", lambda e, t0=t0: e.dma_start(
                out=y_fm.rearrange("(c p) t -> p c t", p=128)[:, :, t0:t0 + st],
                in_=ytile[:].rearrange("p (c t) -> p c t", c=NDC)),
               reads=["ytile"], writes=[f"y:{t0 // 512}"], dma=True, tag="f_yd")
        arena.pop()

    def emit_out():
        arena.mark()
        ytile = arena.alloc(NDC * 512, F32)
        for s in range(8):
            t0 = s * 512
            OP("act", lambda e, t0=t0: e.activation(
                ytile[:].rearrange("p (c t) -> p c t", c=NDC),
                xres[:].rearrange("p (c t) -> p c t", c=NDC)[:, :, t0:t0 + 512],
                AF.Identity),
               reads=[f"xres:{t0 // 512}"], writes=["ytile"], tag="yt")
            OP("sy", lambda e, t0=t0: e.dma_start(
                out=y_fm.rearrange("(c p) t -> p c t", p=128)[:, :, t0:t0 + 512],
                in_=ytile[:].rearrange("p (c t) -> p c t", c=NDC)),
               reads=["ytile"], writes=[f"y:{t0 // 512}"], dma=True, tag="yd")
        arena.pop()

    OP("act", lambda e: e.add_instruction(mybir.InstLoadActFuncSet(
        name=nc.get_next_instruction_name(), act_func_set_id=6, ins=[], outs=[])),
       tag="tbl6")
    emit_consts()
    emit_param_loads()
    if "l0a" in phases:
        emit_l0a()
        sch.barrier()
    if "l0f" in phases:
        emit_l0f()
        sch.barrier()
    if "l1a" in phases:
        emit_l1a()
        sch.barrier()
    if "l1f" in phases:
        emit_l1f()
        sch.barrier()
    if "l1f" not in phases:
        emit_out()
    OP("sy", lambda e: e.nop(), reads=[f"y:{s}" for s in range(8)], tag="fin")

    with nc.Block() as blk:
        sch.materialize(blk, sems)
    for cm in reversed(ctxs):
        cm.__exit__(None, None, None)
    return nc


def host_inputs(inputs):
    shared = dict(
        wq0=np.ascontiguousarray(np.asarray(inputs["L0_attn_Wq"], np.float32)),
        wk0=np.ascontiguousarray(np.asarray(inputs["L0_attn_Wk"], np.float32)),
        wv0=np.ascontiguousarray(np.asarray(inputs["L0_attn_Wv"], np.float32)),
        wb0=np.ascontiguousarray(np.asarray(inputs["L0_attn_Wbeta"], np.float32)),
        bb0v=np.ascontiguousarray(np.asarray(inputs["L0_attn_b_beta"], np.float32)),
        wo0=np.ascontiguousarray(np.asarray(inputs["L0_attn_Wout"], np.float32)),
        wf8=np.ascontiguousarray(np.asarray(inputs["L0_ffn_weight"], np.float32).T),
        bf0=np.ascontiguousarray(np.asarray(inputs["L0_ffn_bias"], np.float32)),
        wq1=np.ascontiguousarray(np.asarray(inputs["L1_attn_Wq"], np.float32)),
        bq1=np.ascontiguousarray(np.asarray(inputs["L1_attn_bq"], np.float32)),
        wk1=np.ascontiguousarray(np.asarray(inputs["L1_attn_Wk"], np.float32)),
        bk1=np.ascontiguousarray(np.asarray(inputs["L1_attn_bk"], np.float32)),
        wv1=np.ascontiguousarray(np.asarray(inputs["L1_attn_Wv"], np.float32)),
        bv1=np.ascontiguousarray(np.asarray(inputs["L1_attn_bv"], np.float32)),
        wfb=np.ascontiguousarray(np.asarray(inputs["L1_attn_Wfb"], np.float32)),
        bfb=np.ascontiguousarray(np.asarray(inputs["L1_attn_bfb"], np.float32)),
        wo1=np.ascontiguousarray(np.asarray(inputs["L1_attn_Wo"], np.float32)),
        bo1=np.ascontiguousarray(np.asarray(inputs["L1_attn_bo"], np.float32)),
        g1=np.ascontiguousarray(np.asarray(inputs["L1_attn_g1"], np.float32)),
        b1=np.ascontiguousarray(np.asarray(inputs["L1_attn_b1"], np.float32)),
        g2=np.ascontiguousarray(np.asarray(inputs["L1_attn_g2"], np.float32)),
        b2=np.ascontiguousarray(np.asarray(inputs["L1_attn_b2"], np.float32)),
        wr2=np.ascontiguousarray(np.asarray(inputs["L1_ffn_W_r"], np.float32).T),
        wk2=np.ascontiguousarray(np.asarray(inputs["L1_ffn_W_k"], np.float32).T),
        wv2=np.ascontiguousarray(np.asarray(inputs["L1_ffn_W_v"], np.float32).T),
        mur=np.ascontiguousarray(np.asarray(inputs["L1_ffn_mu_r"], np.float32)),
        muk=np.ascontiguousarray(np.asarray(inputs["L1_ffn_mu_k"], np.float32)),
    )
    xs = np.asarray(inputs["L0_attn_x"], np.float32)
    in_maps = []
    for b in range(8):
        m = dict(shared)
        m["x_fm"] = np.ascontiguousarray(xs[b].T)
        in_maps.append(m)
    return in_maps


def kernel(**inputs):
    in_maps = host_inputs(inputs)
    nc = build()
    res = run_bass_kernel_spmd(nc, in_maps, core_ids=list(range(8)))
    out = np.stack([np.ascontiguousarray(res.results[b]["y_fm"].T) for b in range(8)])
    return out.astype(np.float32)


# version 09
# speedup vs baseline: 1.0682x; 1.0098x over previous
import numpy as np
import concourse.bass as bass
import concourse.mybir as mybir
from concourse.bass_utils import run_bass_kernel_spmd

F32 = mybir.dt.float32
BF16 = mybir.dt.bfloat16
FP8 = mybir.dt.float8e4
AF = mybir.ActivationFunctionType
ALU = mybir.AluOpType

T = 4096
D = 1024
NDC = 8
H = 16
CH = 128
ST0 = 256
NS0 = T // ST0
STF = 512
ST1 = 256
NLEV = 7

DT_SZ = {F32: 4, BF16: 2, FP8: 1}


class Op:
    __slots__ = ("eng", "fn", "reads", "writes", "dma", "idx", "deps", "inc",
                 "cum", "waits", "tag", "lane", "force")

    def __init__(self, eng, fn, reads, writes, dma, idx, tag):
        self.eng, self.fn = eng, fn
        self.reads, self.writes = list(reads), list(writes)
        self.dma, self.idx, self.tag = dma, idx, tag
        self.deps = []
        self.inc = False
        self.cum = 0
        self.waits = []
        self.lane = 0
        self.force = False


class Sched:
    ENGS = ("te", "dve", "act", "gp", "sy")

    def __init__(self):
        self.ops = []
        self.res = {}
        self.last_on = {e: None for e in self.ENGS}
        self.bar = {e: None for e in self.ENGS}
        self.te_key = None
        self.n_drains = 0

    def drain(self, eng):
        last = self.last_on[eng]
        if last is None:
            return
        o = Op(eng, lambda e: None, [], [], False, len(self.ops), "drain")
        o.deps = [last]
        o.force = True
        self.ops.append(o)
        self.n_drains += 1

    def op(self, eng, fn, reads=(), writes=(), dma=False, tag="", mmkey=None):
        if eng == "te" and mmkey is not None:
            if self.te_key is not None and self.te_key != mmkey:
                self.drain("te")
            self.te_key = mmkey
        idx = len(self.ops)
        o = Op(eng, fn, reads, writes, dma, idx, tag)
        deps = set()
        for r in o.reads:
            e = self.res.get(r)
            if e and e["w"] is not None:
                deps.add(e["w"])
        for w in o.writes:
            e = self.res.get(w)
            if e:
                if e["w"] is not None:
                    deps.add(e["w"])
                deps.update(e["r"])
        for e2, i2 in self.bar.items():
            if i2 is not None and e2 != eng:
                deps.add(i2)
        deps.discard(idx)
        o.deps = sorted(deps)
        for r in o.reads:
            self.res.setdefault(r, {"w": None, "r": []})["r"].append(idx)
        for w in o.writes:
            self.res[w] = {"w": idx, "r": []}
        self.ops.append(o)
        self.last_on[eng] = idx
        return idx

    def barrier(self):
        self.drain("te")
        self.te_key = None
        self.bar = dict(self.last_on)

    NLANES = 4

    @staticmethod
    def _semkey(op):
        if op.dma:
            return (op.eng, "d", op.lane)
        return (op.eng, "c")

    def materialize(self, blk, sems):
        ops = self.ops
        def exempt(dp, o):
            if o.force:
                return False
            if dp.eng != o.eng:
                return False
            if dp.dma and o.dma:
                return True
            if dp.dma or o.dma:
                return o.eng != "gp" and not dp.dma
            return o.eng != "gp"
        for o in ops:
            for d in o.deps:
                dp = ops[d]
                if exempt(dp, o):
                    continue
                dp.inc = True
        counters = {}
        qcount = {}
        for o in ops:
            if o.dma:
                qi = qcount.get(o.eng, 0)
                qcount[o.eng] = qi + 1
                o.lane = qi % self.NLANES
                k = self._semkey(o)
                counters[k] = counters.get(k, 0) + 16
                o.cum = counters[k]
                o.inc = True
            elif o.inc:
                k = self._semkey(o)
                counters[k] = counters.get(k, 0) + 1
                o.cum = counters[k]
        marks = {e: {} for e in self.ENGS}
        for o in ops:
            need = {}
            for d in o.deps:
                dp = ops[d]
                if exempt(dp, o):
                    continue
                k = self._semkey(dp)
                if dp.cum > need.get(k, 0):
                    need[k] = dp.cum
            if o.dma and o.cum > 16:
                k = self._semkey(o)
                if o.cum - 16 > need.get(k, 0):
                    need[k] = o.cum - 16
            wm = marks[o.eng]
            for k, v in need.items():
                if v > wm.get(k, 0):
                    o.waits.append((k, v))
                    wm[k] = v

        def run_engine(ename):
            def body(eng):
                for o in ops:
                    if o.eng != ename:
                        continue
                    for (k, v) in o.waits:
                        eng.wait_ge(sems[k], v)
                    if o.tag == "drain":
                        continue
                    inst = o.fn(eng)
                    if o.inc:
                        if inst is None:
                            raise RuntimeError(f"op {o.idx} {o.tag}: no inst for inc")
                        inst.then_inc(sems[self._semkey(o)], 16 if o.dma else 1)
            return body
        blk.tensor(run_engine("te"))
        blk.vector(run_engine("dve"))
        blk.scalar(run_engine("act"))
        blk.gpsimd(run_engine("gp"))
        blk.sync(run_engine("sy"))


class Arena:
    def __init__(self, ap, total_bytes):
        self.ap = ap
        self.total = total_bytes
        self.off = 0
        self.marks = []

    def alloc(self, cols, dt):
        nbytes = cols * DT_SZ[dt]
        pad = (nbytes + 3) & ~3
        assert self.off + pad <= self.total, f"arena overflow {self.off}+{pad}"
        a = self.off
        self.off += pad
        view = self.ap[:, a // 2:(a + nbytes + 1) // 2]
        if dt != BF16:
            view = view.bitcast(dt)
        return view

    def mark(self):
        self.marks.append(self.off)

    def pop(self):
        self.off = self.marks.pop()


def build(phases=("l0a", "l0f", "l1a", "l1f"), ns0=NS0, l0a_parts=("stats", "proj", "beta", "norms", "chunks", "wout"), nchunks=99, chunk_parts=("xinit", "grams", "solve", "uk", "m1", "o", "ds")):
    nc = bass.Bass("TRN2", target_bir_lowering=False, detect_race_conditions=False)
    dram = {}

    def din(name, shape):
        dram[name] = nc.dram_tensor(name, list(shape), F32, kind="ExternalInput").ap()
        return dram[name]

    x_fm = din("x_fm", [D, T])
    wq0 = din("wq0", [D, D]); wk0 = din("wk0", [D, D]); wv0 = din("wv0", [D, D])
    wb0 = din("wb0", [D, H]); wo0 = din("wo0", [D, D])
    wf8 = din("wf8", [D, D])
    wq1 = din("wq1", [D, D]); wk1 = din("wk1", [D, D]); wv1 = din("wv1", [D, D])
    wfb = din("wfb", [2 * D, D]); wo1 = din("wo1", [D, D])
    wr2 = din("wr2", [D, D]); wk2 = din("wk2", [D, D]); wv2 = din("wv2", [D, D])
    pfsrc = {}
    for nm in ("bf0", "bq1", "bk1", "bv1", "bfb", "bo1", "g1", "b1", "g2", "b2",
               "mur", "muk"):
        pfsrc[nm] = din(nm, [D])
    bb0_h = din("bb0v", [H])
    y_fm = nc.dram_tensor("y_fm", [D, T], F32, kind="ExternalOutput").ap()

    sch = Sched()
    OP = sch.op
    ctxs = []

    def sb(name, shape, dt):
        cm = nc.sbuf_tensor(name, list(shape), dt)
        t = cm.__enter__()
        ctxs.append(cm)
        return t

    def psm(name, shape, dt):
        cm = nc.psum_tensor(name, list(shape), dt)
        t = cm.__enter__()
        ctxs.append(cm)
        return t

    xres = sb("xres", [128, NDC * T], BF16)
    idn = sb("idn", [128, 128], BF16)
    maskSUL = sb("maskSUL", [128, 256], BF16)
    maskSU = maskSUL[:, 0:128]
    maskSL = maskSUL[:, 128:256]
    maskIU = sb("maskIU", [128, 128], BF16)
    onesc128 = sb("onesc128", [128, 1], BF16)
    onesc1 = sb("onesc1", [1, 128], BF16)
    onesrow = sb("onesrow", [1, 512], BF16)
    hmap = sb("hmap", [16, NDC * 128], BF16)
    pmap = sb("pmap", [128, NDC * 16], BF16)
    bb0r = sb("bb0r", [1, H], BF16)
    bb0c = sb("bb0c", [H, 2], F32)
    cvals = sb("cvals", [128, 4], F32)
    pf = {nm: sb("pf_" + nm, [128, NDC], F32) for nm in pfsrc}
    prow = {}
    hmapRbox = [None]
    arena_ap = sb("arena", [128, 70500], BF16)
    arena = Arena(arena_ap, 141000)
    pbw = [psm(f"pbw{j}", [128, 1024], F32) for j in range(4)]
    pb = [pbw[i // 2][:, (i % 2) * 512:(i % 2) * 512 + 512] for i in range(8)]

    sems = {}
    semkeys = [("te", "c"), ("dve", "c"), ("act", "c"), ("gp", "c"), ("sy", "c")]
    for q in ("sy", "gp", "act"):
        for ln in range(Sched.NLANES):
            semkeys.append((q, "d", ln))
    for k in semkeys:
        cm = nc.semaphore("sem_" + "_".join(str(x) for x in k))
        sems[k] = cm.__enter__()
        ctxs.append(cm)

    def emit_consts():
        OP("dve", lambda e: e.memset(idn[:], 0.0), writes=["idn"], tag="c_idn0")
        OP("gp", lambda g: g.affine_select(out=idn[:], in_=idn[:],
                                           compare_op=ALU.not_equal, fill=1.0, base=0,
                                           pattern=[[-1, 128]], channel_multiplier=1),
           reads=["idn"], writes=["idn"], tag="c_idn")

        OP("dve", lambda e: e.memset(maskSU[:], -1.0), writes=["maskSU"], tag="c_msu0")
        OP("gp", lambda g: g.affine_select(out=maskSU[:], in_=maskSU[:],
                                           compare_op=ALU.is_gt, fill=0.0, base=0,
                                           pattern=[[1, 128]], channel_multiplier=-1),
           reads=["maskSU"], writes=["maskSU"], tag="c_msu")

        OP("dve", lambda e: e.memset(maskIU[:], 1.0), writes=["maskIU"], tag="c_miu0")
        OP("gp", lambda g: g.affine_select(out=maskIU[:], in_=maskIU[:],
                                           compare_op=ALU.is_ge, fill=0.0, base=0,
                                           pattern=[[1, 128]], channel_multiplier=-1),
           reads=["maskIU"], writes=["maskIU"], tag="c_miu")
        OP("dve", lambda e: e.memset(maskSL[:], -1.0), writes=["maskSL"], tag="c_msl0")
        OP("gp", lambda g: g.affine_select(out=maskSL[:], in_=maskSL[:],
                                           compare_op=ALU.is_gt, fill=0.0, base=0,
                                           pattern=[[-1, 128]], channel_multiplier=1),
           reads=["maskSL"], writes=["maskSL"], tag="c_msl")

        OP("dve", lambda e: e.memset(onesc128[:], 1.0), writes=["onesc128"])
        OP("dve", lambda e: e.memset(onesc1[:], 1.0), writes=["onesc1"])
        OP("dve", lambda e: e.memset(onesrow[:], 1.0), writes=["onesrow"])
        def c_cv(e):
            e.memset(cvals[:, 0:1], 1e-5)
            e.memset(cvals[:, 1:2], 1e-24)
            return e.memset(cvals[:, 2:3], 1e-6)
        OP("dve", c_cv, writes=["cvals"], tag="c_cv")

        OP("dve", lambda e: e.memset(hmap[:], 1.0), writes=["hmap"], tag="c_hmap0")
        for dc in range(NDC):
            for half in range(2):
                sl = hmap[:, dc * 128 + half * 64: dc * 128 + (half + 1) * 64]
                OP("gp", lambda g, sl=sl, dc=dc, half=half: g.affine_select(
                    out=sl, in_=sl, compare_op=ALU.is_equal, fill=0.0,
                    base=-(2 * dc + half), pattern=[[0, 64]], channel_multiplier=1),
                   reads=["hmap"], writes=["hmap"], tag="c_hmap")

        OP("dve", lambda e: e.memset(pmap[:], 1.0), writes=["pmap"], tag="c_pmap0")
        for dc in range(NDC):
            for half in range(2):
                sl = pmap[64 * half:64 * (half + 1), dc * 16:(dc + 1) * 16]
                OP("gp", lambda g, sl=sl, dc=dc, half=half: g.affine_select(
                    out=sl, in_=sl, compare_op=ALU.is_equal, fill=0.0,
                    base=-(2 * dc + half), pattern=[[1, 16]], channel_multiplier=0),
                   reads=["pmap"], writes=["pmap"], tag="c_pmap")

    def emit_param_loads():
        OP("gp", lambda g: g.dma_start(out=xres[:].rearrange("p (c t) -> p c t", c=NDC),
                                       in_=x_fm.rearrange("(c p) t -> p c t", p=128)),
           writes=[f"xres:{gg}" for gg in range(8)], dma=True, tag="ld_x")
        OP("gp", lambda g: g.dma_start(out=bb0r[:],
                                       in_=bb0_h.rearrange("(x h) -> x h", x=1)),
           writes=["bb0r"], dma=True)
        OP("gp", lambda g: g.dma_start(out=bb0c[:, 0:1],
                                       in_=bb0_h.rearrange("(h x) -> h x", x=1),
                                       allow_slow_non_contiguous=True),
           writes=["bb0c"], dma=True)
        OP("dve", lambda e: e.tensor_scalar_mul(bb0c[:, 1:2], bb0c[:, 0:1], -1.0),
           reads=["bb0c"], writes=["bb0c"], tag="bb0n")
        for nm, src in pfsrc.items():
            OP("sy", lambda g, src=src, nm=nm: g.dma_start(
                out=pf[nm][:], in_=src.rearrange("(c p) -> p c", p=128),
                allow_slow_non_contiguous=True),
               writes=["pf_" + nm], dma=True)


    def load_weight(wsb, src, name, kin=D):
        nkc = kin // 128
        OP("gp", lambda g: g.dma_start(
            out=wsb[:].rearrange("p (c o) -> p c o", c=nkc),
            in_=src.rearrange("(c p) o -> p c o", p=128)),
           writes=[name], dma=True, tag="ld_" + name)

    def xres3(t0, st, dc=None):
        if dc is None:
            return xres[:].rearrange("p (c t) -> p c t", c=NDC)[:, :, t0:t0 + st]
        return xres[:, dc * T + t0:dc * T + t0 + st]

    def gkeys(t0, st):
        g0, g1 = t0 // 512, (t0 + st - 1) // 512
        return [f"xres:{g}" for g in range(g0, g1 + 1)]

    def emit_stats(t0, st, src_fn, src_keys, sq, stt, eps, l1a_eps=None, gb=None):
        OP("act", lambda e: e.activation(
            sq[:].rearrange("p (c t) -> p c t", c=NDC)[:, :, 0:st],
            src_fn(), AF.Square),
           reads=src_keys, writes=["sq"], tag="st_sq")
        for dc in range(NDC):
            OP("te", lambda e, dc=dc: e.matmul(
                pb[0][0:1, 0:st], onesc128[:], src_fn(dc),
                start=(dc == 0), stop=(dc == NDC - 1)),
               reads=src_keys + ["onesc128"], writes=["pb0"], tag="st_s1",
               mmkey=(0, 128))
        for dc in range(NDC):
            OP("te", lambda e, dc=dc: e.matmul(
                pb[1][0:1, 0:st], onesc128[:],
                sq[:, dc * (sq.shape[1] // NDC): dc * (sq.shape[1] // NDC) + st],
                start=(dc == 0), stop=(dc == NDC - 1)),
               reads=["sq", "onesc128"], writes=["pb1"], tag="st_s2",
               mmkey=(0, 128))
        mu, va, tmp = stt["mu"], stt["va"], stt["tmp"]
        arow, srow = stt["arow"], stt["srow"]
        arow_bf, srow_bf = stt["arow_bf"], stt["srow_bf"]
        KMU = stt.get("k_mu", "murow")
        KVA = stt.get("k_va", "varow")
        KTMP = stt.get("k_tmp", "tmprow")
        KSR = stt.get("k_srow", "srow")
        KABF = stt.get("k_abf", "arow_bf")
        KSBF = stt.get("k_sbf", "srow_bf")
        OP("dve", lambda e: e.tensor_scalar_mul(mu[0:1, 0:st], pb[0][0:1, 0:st], 1.0 / D),
           reads=["pb0"], writes=[KMU], tag="st_mu")
        OP("dve", lambda e: e.tensor_scalar_mul(va[0:1, 0:st], pb[1][0:1, 0:st], 1.0 / D),
           reads=["pb1"], writes=[KVA], tag="st_ex2")
        OP("dve", lambda e: e.scalar_tensor_tensor(
            tmp[0:1, 0:st], mu[0:1, 0:st], -1.0, mu[0:1, 0:st], ALU.mult, ALU.mult),
           reads=[KMU], writes=[KTMP], tag="st_msq")
        OP("dve", lambda e: e.tensor_add(va[0:1, 0:st], va[0:1, 0:st], tmp[0:1, 0:st]),
           reads=[KVA, KTMP], writes=[KVA], tag="st_var")
        OP("act", lambda e: e.activation(tmp[0:1, 0:st], va[0:1, 0:st], AF.Ln,
                                         bias=cvals[0:1, eps:eps + 1]),
           reads=[KVA, "cvals"], writes=[KTMP], tag="st_ln")
        OP("act", lambda e: e.activation(arow[0:1, 0:st], tmp[0:1, 0:st], AF.Exp, scale=-0.5),
           reads=[KTMP], writes=["arow"], tag="st_rstd")
        if l1a_eps is not None:
            OP("dve", lambda e: e.scalar_tensor_tensor(
                tmp[0:1, 0:st], arow[0:1, 0:st], 1.0, arow[0:1, 0:st], ALU.mult, ALU.mult),
               reads=["arow"], writes=[KTMP], tag="dln_r2")
            OP("dve", lambda e: e.tensor_mul(tmp[0:1, 0:st], tmp[0:1, 0:st], va[0:1, 0:st]),
               reads=[KTMP, KVA], writes=[KTMP], tag="dln_w")
            OP("act", lambda e: e.activation(tmp[0:1, 0:st], tmp[0:1, 0:st], AF.Ln,
                                             bias=cvals[0:1, l1a_eps:l1a_eps + 1]),
               reads=[KTMP, "cvals"], writes=[KTMP], tag="dln_ln")
            OP("act", lambda e: e.activation(tmp[0:1, 0:st], tmp[0:1, 0:st], AF.Exp,
                                             scale=-0.5),
               reads=[KTMP], writes=[KTMP], tag="dln_u")
            OP("dve", lambda e: e.tensor_mul(arow[0:1, 0:st], arow[0:1, 0:st],
                                             tmp[0:1, 0:st]),
               reads=["arow", KTMP], writes=["arow"], tag="dln_a")
        OP("dve", lambda e: e.scalar_tensor_tensor(
            srow[0:1, 0:st], mu[0:1, 0:st], -1.0, arow[0:1, 0:st], ALU.mult, ALU.mult),
           reads=[KMU, "arow"], writes=[KSR], tag="st_srow")
        OP("act", lambda e: e.activation(arow_bf[0:1, 0:st], arow[0:1, 0:st], AF.Identity),
           reads=["arow"], writes=[KABF], tag="st_abf")
        OP("act", lambda e: e.activation(srow_bf[0:1, 0:st], srow[0:1, 0:st], AF.Identity),
           reads=[KSR], writes=[KSBF], tag="st_sbf")
        absc, sbsc = stt["absc"], stt["sbsc"]
        if gb is None:
            OP("te", lambda e: e.matmul(pb[0][:, 0:st], onesc1[:], arow_bf[0:1, 0:st],
                                        start=True, stop=True),
               reads=[KABF, "onesc1"], writes=["pb0"], tag="bc_a", mmkey=(0, 1))
            OP("te", lambda e: e.matmul(pb[1][:, 0:st], onesc1[:], srow_bf[0:1, 0:st],
                                        start=True, stop=True),
               reads=[KSBF, "onesc1"], writes=["pb1"], tag="bc_s", mmkey=(0, 1))
            OP("act", lambda e: e.activation(absc[:, 0:st], pb[0][:, 0:st], AF.Identity),
               reads=["pb0"], writes=["absc"], tag="bc_ae")
            OP("act", lambda e: e.activation(sbsc[:, 0:st], pb[1][:, 0:st], AF.Identity),
               reads=["pb1"], writes=["sbsc"], tag="bc_se")
        else:
            gname, bname = gb
            gcol, bcol = stt["gcolb"], stt["bcolb"]
            for dc in range(NDC):
                OP("te", lambda e, dc=dc: e.matmul(
                    pb[0][:, 0:st], gcol[:, dc:dc + 1], arow_bf[0:1, 0:st],
                    start=True, stop=True),
                   reads=["arow_bf", "gcolb"], writes=["pb0"], tag="bcg_a")
                OP("act", lambda e, dc=dc: e.activation(
                    absc[:, dc * st:(dc + 1) * st], pb[0][:, 0:st], AF.Identity),
                   reads=["pb0"], writes=["absc"], tag="bcg_ae")
                OP("te", lambda e, dc=dc: e.matmul(
                    pb[1][:, 0:st], gcol[:, dc:dc + 1], srow_bf[0:1, 0:st],
                    start=True, stop=False),
                   reads=["srow_bf", "gcolb"], writes=["pb1"], tag="bcg_s")
                OP("te", lambda e, dc=dc: e.matmul(
                    pb[1][:, 0:st], bcol[:, dc:dc + 1], onesrow[0:1, 0:st],
                    start=False, stop=True),
                   reads=["onesrow", "bcolb"], writes=["pb1"], tag="bcg_b")
                OP("act", lambda e, dc=dc: e.activation(
                    sbsc[:, dc * st:(dc + 1) * st], pb[1][:, 0:st], AF.Identity),
                   reads=["pb1"], writes=["sbsc"], tag="bcg_se")

    def emit_apply(st, src_fn, src_keys, stt, out_ap, out_key, lntmp, perdc=False,
                   out_dt_note=None):
        absc, sbsc = stt["absc"], stt["sbsc"]
        if not perdc:
            a_b = absc[:, 0:st].rearrange("p (c t) -> p c t", c=1).broadcast_to([128, NDC, st])
            s_b = sbsc[:, 0:st].rearrange("p (c t) -> p c t", c=1).broadcast_to([128, NDC, st])
        else:
            a_b = absc[:].rearrange("p (c t) -> p c t", c=NDC)
            s_b = sbsc[:].rearrange("p (c t) -> p c t", c=NDC)
        OP("dve", lambda e: e.tensor_tensor(lntmp, src_fn(), a_b, ALU.mult),
           reads=src_keys + ["absc"], writes=["sq"], tag="ap1")
        OP("dve", lambda e: e.tensor_tensor(out_ap, lntmp, s_b, ALU.add),
           reads=["sq", "sbsc"], writes=[out_key], tag="ap2")

    def mkstats(st, big=False):
        d = {}
        d["mu"] = arena.alloc(st, F32)
        d["va"] = arena.alloc(st, F32)
        d["tmp"] = arena.alloc(st, F32)
        d["arow"] = arena.alloc(st, F32)
        d["srow"] = arena.alloc(st, F32)
        d["arow_bf"] = arena.alloc(st, BF16)
        d["srow_bf"] = arena.alloc(st, BF16)
        n = NDC * st if big else st
        d["absc"] = arena.alloc(n, BF16)
        d["sbsc"] = arena.alloc(n, BF16)
        return d

    def emit_l0a():
        arena.mark()
        WQ = arena.alloc(NDC * D, BF16); load_weight(WQ, wq0, "WQ0")
        WK = arena.alloc(NDC * D, BF16); load_weight(WK, wk0, "WK0")
        WV = arena.alloc(NDC * D, BF16); load_weight(WV, wv0, "WV0")
        WO = arena.alloc(NDC * D, BF16); load_weight(WO, wo0, "WO0")
        WB = arena.alloc(NDC * H, BF16); load_weight(WB, wb0, "WB0")
        st = ST0
        xhat = arena.alloc(NDC * st, BF16)
        sq = arena.alloc(NDC * st, BF16)
        qdm = arena.alloc(NDC * st, BF16)
        kdm = arena.alloc(NDC * st, BF16)
        khdm = arena.alloc(NDC * st, BF16)
        kbdm = arena.alloc(NDC * st, BF16)
        ktok = arena.alloc(2 * D, BF16)
        khtok = arena.alloc(2 * D, BF16)
        vtok = arena.alloc(2 * D, BF16)
        odm = kdm
        beta_dm = arena.alloc(st, F32)
        beta_dmb = arena.alloc(st, BF16)
        beta_tok = arena.alloc(2 * 16, F32)
        beta_tokb = arena.alloc(2 * 16, BF16)
        rn_dm = arena.alloc(st, F32)
        rn_dmb = arena.alloc(st, BF16)
        rn_tokb = arena.alloc(2 * 16, BF16)
        brn_dmb = arena.alloc(st, BF16)
        stt = mkstats(st)
        lntmp = sq
        xall = arena.alloc(H * CH, BF16)
        npw = [arena.alloc(1024, BF16) for _ in range(3)] + [sq[:, 0:1024]]
        npow = []
        for c4 in range(4):
            npow += [npw[c4][:, 0:512], npw[c4][:, 512:1024]]
        npkey = [f"npow{i}" for i in range(6)] + ["sq", "sq"]
        ltall = [arena.alloc(H * CH, BF16), None]
        ltall[1] = xhat
        ltkeys = ["ltall0", "xhat"]
        ukdm = ktok[0:64, 0:H * CH]
        m1sb = arena.alloc(H * 64, BF16)
        m1b = arena.alloc(H * 64, BF16)
        oasb = arena.alloc(512, BF16)
        s32 = arena.alloc(H * 64, F32)
        sb16 = arena.alloc(H * 64, BF16)
        OP("dve", lambda e: e.memset(s32[:], 0.0), writes=["s32"], tag="s_init")
        OP("dve", lambda e: e.memset(sb16[:], 0.0), writes=["sb16"], tag="sb_init")

        for s in range(ns0):
            t0 = s * st
            xk = gkeys(t0, st)
            if "stats" not in l0a_parts:
                continue
            emit_stats(t0, st, lambda dc=None, t0=t0: xres3(t0, st, dc), xk, sq, stt, 0)
            emit_apply(st, lambda t0=t0: xres3(t0, st), xk, stt,
                       xhat[:].rearrange("p (c t) -> p c t", c=NDC), "xhat",
                       lntmp[:].rearrange("p (c t) -> p c t", c=NDC))

            def proj_fm(wsb, wkey, out, okey):
                for oc in range(NDC):
                    bk = 2 + (oc % 2)
                    for kc in range(NDC):
                        OP("te", lambda e, oc=oc, kc=kc, bk=bk, wsb=wsb: e.matmul(
                            pb[bk][:, 0:st],
                            wsb[:, kc * D + oc * 128: kc * D + (oc + 1) * 128],
                            xhat[:, kc * st:(kc + 1) * st],
                            start=(kc == 0), stop=(kc == NDC - 1)),
                           reads=[wkey, "xhat"], writes=[f"pb{bk}"], tag=f"pj_{okey}",
                           mmkey=(0, 128))
                    OP("act", lambda e, oc=oc, bk=bk, out=out: e.activation(
                        out[:, oc * st:(oc + 1) * st], pb[bk][:, 0:st], AF.Identity),
                       reads=[f"pb{bk}"], writes=[okey], tag=f"pje_{okey}")
            if "proj" not in l0a_parts:
                continue
            proj_fm(WQ, "WQ0", qdm, "qdm")
            proj_fm(WK, "WK0", kdm, "kdmov")

            def proj_tok(wsb, wkey, out, okey):
                for tt in range(st // 128):
                    for oh in range(2):
                        bk = 2 + ((tt * 2 + oh) % 2)
                        for kc in range(NDC):
                            OP("te", lambda e, tt=tt, oh=oh, kc=kc, bk=bk, wsb=wsb: e.matmul(
                                pb[bk][:, 0:512],
                                xhat[:, kc * st + tt * 128: kc * st + (tt + 1) * 128],
                                wsb[:, kc * D + oh * 512: kc * D + (oh + 1) * 512],
                                start=(kc == 0), stop=(kc == NDC - 1)),
                               reads=[wkey, "xhat"], writes=[f"pb{bk}"], tag=f"pt_{okey}",
                               mmkey=(0, 128))
                        OP("act", lambda e, tt=tt, oh=oh, bk=bk, out=out: e.activation(
                            out[:, tt * D + oh * 512: tt * D + (oh + 1) * 512],
                            pb[bk][:, 0:512], AF.Identity),
                           reads=[f"pb{bk}"], writes=[okey], tag=f"pte_{okey}")
            proj_tok(WK, "WK0", ktok, "ktokov")
            proj_tok(WV, "WV0", vtok, "vtok")

            if "beta" not in l0a_parts:
                continue
            for kc in range(NDC):
                OP("te", lambda e, kc=kc: e.matmul(
                    pb[0][0:H, 0:st], WB[:, kc * H:(kc + 1) * H],
                    xhat[:, kc * st:(kc + 1) * st], start=(kc == 0),
                    stop=(kc == NDC - 1)),
                   reads=["WB0", "xhat"], writes=["pb0"], tag="bdm", mmkey=(0, 128))
            OP("act", lambda e: e.activation(beta_dm[0:H, 0:st], pb[0][0:H, 0:st],
                                             AF.Exp, scale=-1.0, bias=bb0c[0:H, 1:2]),
               reads=["pb0", "bb0c"], writes=["beta_dm"], tag="bdm_e")
            OP("dve", lambda e: e.tensor_scalar_add(beta_dm[0:H, 0:st],
                                                    beta_dm[0:H, 0:st], 1.0),
               reads=["beta_dm"], writes=["beta_dm"], tag="bdm_1")
            OP("act", lambda e: e.activation(beta_dm[0:H, 0:st], beta_dm[0:H, 0:st], AF.Ln),
               reads=["beta_dm"], writes=["beta_dm"], tag="bdm_l")
            OP("act", lambda e: e.activation(beta_dm[0:H, 0:st], beta_dm[0:H, 0:st],
                                             AF.Exp, scale=-1.0),
               reads=["beta_dm"], writes=["beta_dm"], tag="bdm_x")
            OP("act", lambda e: e.activation(beta_dmb[0:H, 0:st], beta_dm[0:H, 0:st],
                                             AF.Identity),
               reads=["beta_dm"], writes=["beta_dmb"], tag="bdm_bf")
            for tt in range(st // 128):
                OP("te", lambda e, tt=tt: e.matmul(
                    pb[1][:, tt * 16:(tt + 1) * 16],
                    beta_dmb[0:H, tt * 128:(tt + 1) * 128], idn[0:H, 0:H],
                    start=True, stop=True),
                   reads=["beta_dmb", "idn"], writes=["pb1"], tag="btok_t",
                   mmkey=(0, 16))
            OP("act", lambda e: e.activation(beta_tok[:, 0:32], pb[1][:, 0:32],
                                             AF.Identity),
               reads=["pb1"], writes=["beta_tok"], tag="btok_e")
            OP("act", lambda e: e.activation(beta_tokb[:, 0:32], pb[1][:, 0:32],
                                             AF.Identity),
               reads=["pb1"], writes=["beta_tokb"], tag="btok_bf")

            if "norms" not in l0a_parts:
                continue
            OP("act", lambda e: e.activation(
                sq[:].rearrange("p (c t) -> p c t", c=NDC),
                kdm[:].rearrange("p (c t) -> p c t", c=NDC), AF.Square),
               reads=["kdmov"], writes=["sq"], tag="kn_sq")
            for dc in range(NDC):
                OP("te", lambda e, dc=dc: e.matmul(
                    pb[0][0:H, 0:st], pmap[:, dc * 16:(dc + 1) * 16],
                    sq[:, dc * st:(dc + 1) * st],
                    start=(dc == 0), stop=(dc == NDC - 1)),
                   reads=["sq", "pmap"], writes=["pb0"], tag="kn_mm", mmkey=(0, 128))
            OP("act", lambda e: e.activation(rn_dm[0:H, 0:st], pb[0][0:H, 0:st],
                                             AF.Ln, bias=cvals[0:H, 1:2]),
               reads=["pb0", "cvals"], writes=["rn_dm"], tag="kn_ln")
            OP("act", lambda e: e.activation(rn_dm[0:H, 0:st], rn_dm[0:H, 0:st],
                                             AF.Exp, scale=-0.5),
               reads=["rn_dm"], writes=["rn_dm"], tag="kn_exp")
            OP("act", lambda e: e.activation(rn_dmb[0:H, 0:st], rn_dm[0:H, 0:st],
                                             AF.Identity),
               reads=["rn_dm"], writes=["rn_dmb"], tag="kn_bf")
            OP("dve", lambda e: e.tensor_mul(brn_dmb[0:H, 0:st], rn_dmb[0:H, 0:st],
                                             beta_dmb[0:H, 0:st]),
               reads=["rn_dmb", "beta_dmb"], writes=["brn_dmb"], tag="kn_brn")
            for tt in range(st // 128):
                OP("te", lambda e, tt=tt: e.matmul(
                    pb[1][:, tt * 16:(tt + 1) * 16],
                    rn_dmb[0:H, tt * 128:(tt + 1) * 128], idn[0:H, 0:H],
                    start=True, stop=True),
                   reads=["rn_dmb", "idn"], writes=["pb1"], tag="rnt_t", mmkey=(0, 16))
            OP("act", lambda e: e.activation(
                rn_tokb[:, 0:32], pb[1][:, 0:32], AF.Identity),
               reads=["pb1"], writes=["rn_tokb"], tag="rnt_e")
            for dc in range(NDC):
                OP("te", lambda e, dc=dc: e.matmul(
                    pb[0][:, 0:st], hmap[:, dc * 128:(dc + 1) * 128], rn_dmb[0:H, 0:st],
                    start=True, stop=True),
                   reads=["rn_dmb", "hmap"], writes=["pb0"], tag="khdm_bc",
                   mmkey=(0, 16))
                OP("dve", lambda e, dc=dc: e.tensor_tensor(
                    khdm[:, dc * st:(dc + 1) * st], kdm[:, dc * st:(dc + 1) * st],
                    pb[0][:, 0:st], ALU.mult),
                   reads=["kdmov", "pb0"], writes=["khdm"], tag="khdm_tt")
                OP("te", lambda e, dc=dc: e.matmul(
                    pb[1][:, 0:st], hmap[:, dc * 128:(dc + 1) * 128], brn_dmb[0:H, 0:st],
                    start=True, stop=True),
                   reads=["brn_dmb", "hmap"], writes=["pb1"], tag="kbdm_bc",
                   mmkey=(0, 16))
                OP("dve", lambda e, dc=dc: e.tensor_tensor(
                    kbdm[:, dc * st:(dc + 1) * st], kdm[:, dc * st:(dc + 1) * st],
                    pb[1][:, 0:st], ALU.mult),
                   reads=["kdmov", "pb1"], writes=["kbdm"], tag="kbdm_tt")
            for tt in range(st // 128):
                rnb = rn_tokb[:, tt * 16:(tt + 1) * 16].rearrange(
                    "p (h x) -> p h x", x=1).broadcast_to([128, 16, 64])
                OP("dve", lambda e, tt=tt, rnb=rnb: e.tensor_tensor(
                    khtok[:, tt * D:(tt + 1) * D].rearrange("p (h d) -> p h d", d=64),
                    ktok[:, tt * D:(tt + 1) * D].rearrange("p (h d) -> p h d", d=64),
                    rnb, ALU.mult),
                   reads=["ktokov", "rn_tokb"], writes=["khtok"], tag="khtok_tt")

            if "chunks" not in l0a_parts:
                continue
            FULLCP = ("xinit", "grams", "solve", "uk", "m1", "o", "ds")
            nch = min(st // CH, nchunks)
            if tuple(chunk_parts) == FULLCP and nch == 2:
                def ec(ci, cp):
                    emit_chunk(s, ci, qdm, kbdm, khdm, khtok, vtok, odm,
                               beta_tokb, xall, npow, npkey, npw, ltall[ci % 2],
                               ltkeys[ci % 2], ukdm, m1sb, m1b, oasb, s32,
                               sb16, st, cp)
                ec(0, ("xinit", "grams", "solve", "uk", "m1"))
                ec(1, ("xinit", "grams", "solve"))
                ec(0, ("o", "ds"))
                ec(1, ("uk", "m1"))
                ec(1, ("o", "ds"))
            else:
                for ci in range(nch):
                    emit_chunk(s, ci, qdm, kbdm, khdm, khtok, vtok, odm,
                               beta_tokb, xall, npow, npkey, npw, ltall[ci % 2],
                               ltkeys[ci % 2], ukdm, m1sb, m1b, oasb, s32,
                               sb16, st, chunk_parts)

            if "wout" not in l0a_parts:
                continue
            for oc in range(NDC):
                bk = 2 + (oc % 2)
                for kc in range(NDC):
                    OP("te", lambda e, oc=oc, kc=kc, bk=bk: e.matmul(
                        pb[bk][:, 0:st],
                        WO[:, kc * D + oc * 128: kc * D + (oc + 1) * 128],
                        odm[:, kc * st:(kc + 1) * st],
                        start=(kc == 0), stop=(kc == NDC - 1)),
                       reads=["WO0", "kdmov"], writes=[f"pb{bk}"], tag="wo",
                       mmkey=(0, 128))
                OP("dve", lambda e, oc=oc, bk=bk, t0=t0: e.tensor_tensor(
                    xres[:, oc * T + t0: oc * T + t0 + st],
                    xres[:, oc * T + t0: oc * T + t0 + st],
                    pb[bk][:, 0:st], ALU.add),
                   reads=[f"pb{bk}"] + xk, writes=xk, tag="res0")
        arena.pop()

    def emit_chunk(s, ci, qdm, kbdm, khdm, khtok, vtok, odm, beta_tokb,
                   xall, npow, npkey, npw, ltc, ltkey, ukdm, m1sb, m1b, oasb, s32, sb16, st,
                   cp=("xinit", "grams", "solve", "uk", "m1", "o", "ds")):
        tc = ci * CH
        if "xinit" in cp:
          OP("act", lambda e: e.activation(
            xall[:].rearrange("p (h w) -> p h w", h=H)[:, :, 0:64],
            vtok[:, ci * D:(ci + 1) * D].rearrange("p (h d) -> p h d", d=64),
            AF.Identity),
             reads=["vtok"], writes=["xall"], tag=f"xiV{s}_{ci}")
          OP("act", lambda e: e.activation(
            xall[:].rearrange("p (h w) -> p h w", h=H)[:, :, 64:128],
            khtok[:, ci * D:(ci + 1) * D].rearrange("p (h d) -> p h d", d=64),
            AF.Identity),
             reads=["khtok"], writes=["xall"], tag=f"xiK{s}_{ci}")
        chains = []
        for par in range(2):
            groups = [[par + 2 * i for i in range(4)],
                      [8 + par + 2 * i for i in range(4)]]
            for gg, hs in enumerate(groups):
                chains.append((2 * par + gg, 64 * par, hs))
        if "grams" in cp:
            for cid, b0, hs in chains:
                for q, h in enumerate(hs):
                    dc = h // 2
                    OP("te", lambda e, q=q, dc=dc, b0=b0: e.matmul(
                        pb[6][:, q * 128:(q + 1) * 128],
                        kbdm[b0:b0 + 64, dc * st + tc: dc * st + tc + CH],
                        khdm[b0:b0 + 64, dc * st + tc: dc * st + tc + CH],
                        start=True, stop=True),
                       reads=["kbdm", "khdm"], writes=["pb6"], tag="grA",
                       mmkey=(b0, 64))
                for q, h in enumerate(hs):
                    dc = h // 2
                    OP("te", lambda e, q=q, dc=dc, b0=b0: e.matmul(
                        pb[7][:, q * 128:(q + 1) * 128],
                        khdm[b0:b0 + 64, dc * st + tc: dc * st + tc + CH],
                        kbdm[b0:b0 + 64, dc * st + tc: dc * st + tc + CH],
                        start=True, stop=True),
                       reads=["kbdm", "khdm"], writes=["pb7"], tag="grAT",
                       mmkey=(b0, 64))
                msul = maskSUL[:].rearrange("p (c u t) -> p c u t", c=2,
                                            u=1).broadcast_to([128, 2, 4, 128])
                OP("dve", lambda e, cid=cid, msul=msul: e.tensor_tensor(
                    npw[cid][:].rearrange("p (c q t) -> p c q t", c=2, q=4),
                    pbw[3][:, 0:1024].rearrange("p (c q t) -> p c q t", c=2, q=4),
                    msul, ALU.mult),
                   reads=["pb6", "pb7", "maskSU", "maskSL"],
                   writes=[npkey[2 * cid], npkey[2 * cid + 1]], tag="grNTe")
                for q, h in enumerate(hs):
                    dc = h // 2
                    OP("te", lambda e, q=q, dc=dc, b0=b0: e.matmul(
                        pb[3][:, q * 128:(q + 1) * 128],
                        kbdm[b0:b0 + 64, dc * st + tc: dc * st + tc + CH],
                        qdm[b0:b0 + 64, dc * st + tc: dc * st + tc + CH],
                        start=True, stop=True),
                       reads=["kbdm", "qdm"], writes=["pb3"], tag="grL",
                       mmkey=(b0, 64))
                mbi = maskIU[:].rearrange("p (c t) -> p c t", c=1).broadcast_to(
                    [128, 4, 128])
                j0, po = hs[0] // 2, (hs[0] % 2) * 128
                OP("dve", lambda e, j0=j0, po=po, mbi=mbi: e.tensor_tensor(
                    ltc[:].rearrange("p (hp x) -> p hp x", x=256)[
                        :, j0:j0 + 4, po:po + 128],
                    pb[3][:, 0:512].rearrange("p (q t) -> p q t", q=4), mbi, ALU.mult),
                   reads=["pb3", "maskIU"], writes=[ltkey], tag="grLe")
        if "solve" in cp:
            for k in range(NLEV):
                for cid, b0, hs in chains:
                    h0 = hs[0]
                    cur, ckey = npow[2 * cid], npkey[2 * cid]
                    curT, ckeyT = npow[2 * cid + 1], npkey[2 * cid + 1]
                    ab = 4 if cid % 2 == 0 else 5
                    nb, tb = (6, 7) if cid % 2 == 0 else (0, 1)
                    sqw = pbw[3] if cid % 2 == 0 else pbw[0]
                    for q, h in enumerate(hs):
                        xsl = xall[:, h * 128:(h + 1) * 128]
                        OP("te", lambda e, q=q, xsl=xsl, cur=cur, ab=ab: e.matmul(
                            pb[ab][:, q * 128:(q + 1) * 128],
                            cur[:, q * 128:(q + 1) * 128], xsl,
                            start=True, stop=True),
                           reads=["xall", ckey], writes=[f"pb{ab}"], tag="apN",
                           mmkey=(0, 128))
                    j0, po = h0 // 2, (h0 % 2) * 128
                    xsl4 = xall[:].rearrange("p (hp x) -> p hp x", x=256)[
                        :, j0:j0 + 4, po:po + 128]
                    OP("dve", lambda e, xsl4=xsl4, ab=ab: e.tensor_tensor(
                        xsl4, xsl4,
                        pb[ab][:, 0:512].rearrange("p (q t) -> p q t", q=4), ALU.add),
                       reads=[f"pb{ab}", "xall"], writes=["xall"], tag="apE")
                    if k < NLEV - 1:
                        lastsq = (k == NLEV - 2)
                        for q in range(4):
                            OP("te", lambda e, q=q, cur=cur, curT=curT, nb=nb:
                               e.matmul(
                                pb[nb][:, q * 128:(q + 1) * 128],
                                curT[:, q * 128:(q + 1) * 128],
                                cur[:, q * 128:(q + 1) * 128], start=True, stop=True),
                               reads=[ckey, ckeyT], writes=[f"pb{nb}"], tag="sqN",
                               mmkey=(0, 128))
                            if not lastsq:
                                OP("te", lambda e, q=q, cur=cur, curT=curT, tb=tb:
                                   e.matmul(
                                    pb[tb][:, q * 128:(q + 1) * 128],
                                    cur[:, q * 128:(q + 1) * 128],
                                    curT[:, q * 128:(q + 1) * 128],
                                    start=True, stop=True),
                                   reads=[ckey, ckeyT], writes=[f"pb{tb}"],
                                   tag="sqT", mmkey=(0, 128))
                        eng = "dve" if cid == 0 else "act"
                        nq = 4 if lastsq else 8
                        dst = (npow[2 * cid] if lastsq else npw[cid])
                        wk = [ckey] if lastsq else [ckey, ckeyT]
                        rk = [f"pb{nb}"] if lastsq else [f"pb{nb}", f"pb{tb}"]
                        src = (pb[nb] if lastsq else sqw[:, 0:1024])
                        if eng == "dve":
                            OP("dve", lambda e, dst=dst, src=src, nq=nq:
                               e.tensor_copy(
                                dst[:].rearrange("p (q t) -> p q t", q=nq),
                                src[:].rearrange("p (q t) -> p q t", q=nq)),
                               reads=rk, writes=wk, tag="sqNTe")
                        else:
                            OP("act", lambda e, dst=dst, src=src, nq=nq:
                               e.activation(
                                dst[:].rearrange("p (q t) -> p q t", q=nq),
                                src[:].rearrange("p (q t) -> p q t", q=nq),
                                AF.Identity),
                               reads=rk, writes=wk, tag="sqNTe")
        if "uk" in cp:
          for h in range(H):
            q = h % 4
            OP("te", lambda e, h=h, q=q: e.matmul(
                pb[3][0:64, q * 128:(q + 1) * 128],
                xall[:, h * 128 + 64:(h + 1) * 128], idn[:],
                start=True, stop=True),
               reads=["xall", "idn"], writes=["pb3"], tag="ukT", mmkey=(0, 128))
            if q == 3:
                OP("act", lambda e, h=h: e.activation(
                    ukdm[:, (h - 3) * 128:(h + 1) * 128],
                    pb[3][0:64, 0:512], AF.Identity),
                   reads=["pb3"], writes=["ktokov"], tag="ukTe")
        if "m1" in cp:
          for w in range(2):
            for i in range(8):
                h = w * 8 + i
                OP("te", lambda e, h=h, i=i: e.matmul(
                    pb[2][:, i * 64:(i + 1) * 64],
                    ukdm[:, h * 128:(h + 1) * 128],
                    sb16[0:64, h * 64:(h + 1) * 64],
                    start=True, stop=True),
                   reads=["ktokov", "sb16"], writes=["pb2"], tag="m1", mmkey=(0, 64))
            vsl = xall[:].rearrange("p (h w) -> p h w", h=H)[:, w * 8:(w + 1) * 8, 0:64]
            OP("dve", lambda e, w=w, vsl=vsl: e.tensor_tensor(
                m1sb[:, w * 512:(w + 1) * 512].rearrange("p (h v) -> p h v", h=8),
                vsl, pb[2][:, 0:512].rearrange("p (h v) -> p h v", h=8), ALU.subtract),
               reads=["pb2", "xall"], writes=["m1sb"], tag="m1e")
          bb = beta_tokb[:, ci * 16:(ci + 1) * 16].rearrange(
              "p (h x) -> p h x", x=1).broadcast_to([128, 16, 64])
          OP("dve", lambda e, bb=bb: e.tensor_tensor(
              m1b[:].rearrange("p (h v) -> p h v", h=H),
              m1sb[:].rearrange("p (h v) -> p h v", h=H), bb, ALU.mult),
             reads=["m1sb", "beta_tokb"], writes=["m1b"], tag="m1b")
        if "o" not in cp:
            return
        for w in range(2):
            for par in range(2):
                b0 = 64 * par
                for pr in range(4):
                    h = w * 8 + 2 * pr + par
                    OP("te", lambda e, h=h, b0=b0, pr=pr, dc=h // 2: e.matmul(
                        pb[3][b0:b0 + 64, pr * 128:(pr + 1) * 128],
                        sb16[b0:b0 + 64, h * 64:(h + 1) * 64],
                        qdm[b0:b0 + 64, dc * st + tc: dc * st + tc + CH],
                        start=True, stop=True, tile_position=(b0, b0)),
                       reads=["sb16", "qdm"], writes=["pb3"], tag="oq",
                       mmkey=(b0, 64))
            for i in range(8):
                h = w * 8 + i
                b0 = 64 * (h % 2)
                pr = i // 2
                OP("te", lambda e, h=h, b0=b0, pr=pr: e.matmul(
                    pb[2][b0:b0 + 64, pr * 128:(pr + 1) * 128],
                    m1sb[:, h * 64:(h + 1) * 64],
                    ltc[:, h * 128:(h + 1) * 128],
                    start=True, stop=True, tile_position=(0, b0)),
                   reads=["m1sb", ltkey], writes=["pb2"], tag="ol", mmkey=(0, 128))
            OP("act", lambda e: e.activation(oasb[:], pb[3][:, 0:512], AF.Identity),
               reads=["pb3"], writes=["oasb"], tag="oae")
            oout = odm[:].rearrange("p (c t) -> p c t", c=NDC)[
                :, w * 4:(w + 1) * 4, tc:tc + CH]
            OP("dve", lambda e, oout=oout: e.tensor_tensor(
                oout, pb[2][:, 0:512].rearrange("p (q t) -> p q t", q=4),
                oasb[:].rearrange("p (q t) -> p q t", q=4), ALU.add),
               reads=["pb2", "oasb"], writes=["kdmov"], tag="oe")
        if "ds" in cp:
          for w in range(2):
            for i in range(8):
                h = w * 8 + i
                for dup in range(2):
                    OP("te", lambda e, h=h, i=i, dup=dup: e.matmul(
                        pb[3][64 * dup:64 * dup + 64, i * 64:(i + 1) * 64],
                        khtok[:, ci * D + h * 64: ci * D + (h + 1) * 64],
                        m1b[:, h * 64:(h + 1) * 64],
                        start=True, stop=True, tile_position=(0, 64 * dup)),
                       reads=["khtok", "m1b"], writes=["pb3"], tag="ds",
                       mmkey=(0, 128))
            OP("dve", lambda e, w=w: e.tensor_tensor(
                s32[:, w * 512:(w + 1) * 512], s32[:, w * 512:(w + 1) * 512],
                pb[3][:, 0:512], ALU.add),
               reads=["pb3", "s32"], writes=["s32"], tag="sadd")
            OP("act", lambda e, w=w: e.activation(
                sb16[:, w * 512:(w + 1) * 512], s32[:, w * 512:(w + 1) * 512],
                AF.Identity),
               reads=["s32"], writes=["sb16"], tag="sbf")

    def emit_l0f():
        arena.mark()
        W8 = arena.alloc(NDC * D, FP8)
        nkc = NDC
        OP("gp", lambda g: g.dma_start(
            out=W8[:].rearrange("p (c o) -> p c o", c=nkc),
            in_=wf8.rearrange("(c p) o -> p c o", p=128)),
           writes=["W8"], dma=True, tag="ld_W8")
        st = STF
        xq = arena.alloc(NDC * st, FP8)
        sq = arena.alloc(NDC * st, BF16)
        lntmp = arena.alloc(NDC * st, BF16)
        stt = mkstats(st)
        for s in range(T // st):
            t0 = s * st
            xk = gkeys(t0, st)
            emit_stats(t0, st, lambda dc=None, t0=t0: xres3(t0, st, dc), xk, sq, stt, 0)
            emit_apply(st, lambda t0=t0: xres3(t0, st), xk, stt,
                       xq[:].rearrange("p (c t) -> p c t", c=NDC), "xq",
                       lntmp[:].rearrange("p (c t) -> p c t", c=NDC))
            for oc in range(NDC):
                bk = 2 + (oc % 2)
                for kc in range(NDC):
                    OP("te", lambda e, oc=oc, kc=kc, bk=bk: e.matmul(
                        pb[bk][:, 0:st],
                        W8[:, kc * D + oc * 128: kc * D + (oc + 1) * 128],
                        xq[:, kc * st:(kc + 1) * st],
                        start=(kc == 0), stop=(kc == NDC - 1)),
                       reads=["W8", "xq"], writes=[f"pb{bk}"], tag="f8",
                       mmkey=(0, 128, "f8"))
                OP("dve", lambda e, oc=oc, bk=bk, t0=t0: e.scalar_tensor_tensor(
                    xres[:, oc * T + t0: oc * T + t0 + st],
                    pb[bk][:, 0:st], pf["bf0"][:, oc:oc + 1],
                    xres[:, oc * T + t0: oc * T + t0 + st], ALU.add, ALU.add),
                   reads=[f"pb{bk}", "pf_bf0"] + xk, writes=xk, tag="res1")
        arena.pop()

    def _lp(fn):
        def g(e):
            with nc.allow_low_precision(reason="bf16 reduce (prototype-validated)"):
                return fn(e)
        return g

    def _stat_math(st, stt, eps, l1a_eps=None):
        mu, va, tmp = stt["mu"], stt["va"], stt["tmp"]
        arow, srow = stt["arow"], stt["srow"]
        arow_bf, srow_bf = stt["arow_bf"], stt["srow_bf"]
        KMU = stt.get("k_mu", "murow")
        KVA = stt.get("k_va", "varow")
        KTMP = stt.get("k_tmp", "tmprow")
        KSR = stt.get("k_srow", "srow")
        KABF = stt.get("k_abf", "arow_bf")
        KSBF = stt.get("k_sbf", "srow_bf")
        OP("dve", lambda e: e.tensor_scalar_mul(mu[0:1, 0:st], pb[0][0:1, 0:st], 1.0 / D),
           reads=["pb0"], writes=[KMU], tag="st_mu")
        OP("dve", lambda e: e.tensor_scalar_mul(va[0:1, 0:st], pb[1][0:1, 0:st], 1.0 / D),
           reads=["pb1"], writes=[KVA], tag="st_ex2")
        OP("dve", lambda e: e.scalar_tensor_tensor(
            tmp[0:1, 0:st], mu[0:1, 0:st], -1.0, mu[0:1, 0:st], ALU.mult, ALU.mult),
           reads=[KMU], writes=[KTMP], tag="st_msq")
        OP("dve", lambda e: e.tensor_add(va[0:1, 0:st], va[0:1, 0:st], tmp[0:1, 0:st]),
           reads=[KVA, KTMP], writes=[KVA], tag="st_var")
        OP("act", lambda e: e.activation(tmp[0:1, 0:st], va[0:1, 0:st], AF.Ln,
                                         bias=cvals[0:1, eps:eps + 1]),
           reads=[KVA, "cvals"], writes=[KTMP], tag="st_ln")
        OP("act", lambda e: e.activation(arow[0:1, 0:st], tmp[0:1, 0:st], AF.Exp,
                                         scale=-0.5),
           reads=[KTMP], writes=["arow"], tag="st_rstd")
        if l1a_eps is not None:
            OP("dve", lambda e: e.scalar_tensor_tensor(
                tmp[0:1, 0:st], arow[0:1, 0:st], 1.0, arow[0:1, 0:st],
                ALU.mult, ALU.mult),
               reads=["arow"], writes=[KTMP], tag="dln_r2")
            OP("dve", lambda e: e.tensor_mul(tmp[0:1, 0:st], tmp[0:1, 0:st],
                                             va[0:1, 0:st]),
               reads=[KTMP, KVA], writes=[KTMP], tag="dln_w")
            OP("act", lambda e: e.activation(tmp[0:1, 0:st], tmp[0:1, 0:st], AF.Ln,
                                             bias=cvals[0:1, l1a_eps:l1a_eps + 1]),
               reads=[KTMP, "cvals"], writes=[KTMP], tag="dln_ln")
            OP("act", lambda e: e.activation(tmp[0:1, 0:st], tmp[0:1, 0:st], AF.Exp,
                                             scale=-0.5),
               reads=[KTMP], writes=[KTMP], tag="dln_u")
            OP("dve", lambda e: e.tensor_mul(arow[0:1, 0:st], arow[0:1, 0:st],
                                             tmp[0:1, 0:st]),
               reads=["arow", KTMP], writes=["arow"], tag="dln_a")
        OP("dve", lambda e: e.scalar_tensor_tensor(
            srow[0:1, 0:st], mu[0:1, 0:st], -1.0, arow[0:1, 0:st], ALU.mult, ALU.mult),
           reads=[KMU, "arow"], writes=[KSR], tag="st_srow")
        OP("act", lambda e: e.activation(arow_bf[0:1, 0:st], arow[0:1, 0:st],
                                         AF.Identity),
           reads=["arow"], writes=[KABF], tag="st_abf")
        OP("act", lambda e: e.activation(srow_bf[0:1, 0:st], srow[0:1, 0:st],
                                         AF.Identity),
           reads=[KSR], writes=[KSBF], tag="st_sbf")

    def _bcg(st, stt, gname, bname, pitch=None):
        if pitch is None:
            pitch = st
        arow_bf, srow_bf = stt["arow_bf"], stt["srow_bf"]
        KABF = stt.get("k_abf", "arow_bf")
        KSBF = stt.get("k_sbf", "srow_bf")
        absc, sbsc = stt["absc"], stt["sbsc"]
        grow, brow = prow[gname], prow[bname]
        for dc in range(NDC):
            ba, bs = (0, 1) if dc % 2 == 0 else (2, 3)
            OP("te", lambda e, dc=dc, grow=grow, ba=ba: e.matmul(
                pb[ba][:, 0:st], grow[0:1, dc * 128:(dc + 1) * 128],
                arow_bf[0:1, 0:st], start=True, stop=True),
               reads=[KABF, "pr_" + gname], writes=[f"pb{ba}"], tag="bcg_a",
               mmkey=(0, 1))
            OP("act", lambda e, dc=dc, ba=ba: e.activation(
                absc[:, dc * pitch:dc * pitch + st], pb[ba][:, 0:st], AF.Identity),
               reads=[f"pb{ba}"], writes=["absc"], tag="bcg_ae")
            OP("te", lambda e, dc=dc, grow=grow, bs=bs: e.matmul(
                pb[bs][:, 0:st], grow[0:1, dc * 128:(dc + 1) * 128],
                srow_bf[0:1, 0:st], start=True, stop=False),
               reads=[KSBF, "pr_" + gname], writes=[f"pb{bs}"], tag="bcg_s",
               mmkey=(0, 1))
            OP("te", lambda e, dc=dc, brow=brow, bs=bs: e.matmul(
                pb[bs][:, 0:st], brow[0:1, dc * 128:(dc + 1) * 128],
                onesrow[0:1, 0:st], start=False, stop=True),
               reads=["onesrow", "pr_" + bname], writes=[f"pb{bs}"], tag="bcg_b",
               mmkey=(0, 1))
            OP("act", lambda e, dc=dc, bs=bs: e.activation(
                sbsc[:, dc * pitch:dc * pitch + st], pb[bs][:, 0:st], AF.Identity),
               reads=[f"pb{bs}"], writes=["sbsc"], tag="bcg_se")

    def emit_l1a():
        arena.mark()
        WQ = arena.alloc(NDC * D, BF16); load_weight(WQ, wq1, "WQ1")
        WK = arena.alloc(NDC * D, BF16); load_weight(WK, wk1, "WK1")
        WV = arena.alloc(NDC * D, BF16); load_weight(WV, wv1, "WV1")
        WFB = arena.alloc(2 * NDC * D, BF16); load_weight(WFB, wfb, "WFB", kin=2 * D)
        WO = arena.alloc(NDC * D, BF16); load_weight(WO, wo1, "WO1")
        for nm in ("g1", "b1", "g2", "b2"):
            prow[nm] = arena.alloc(D, BF16)
            OP("gp", lambda g, nm=nm: g.dma_start(
                out=prow[nm][0:1, :], in_=pfsrc[nm].rearrange("(x d) -> x d", x=1)),
               writes=["pr_" + nm], dma=True)
        hmapR = arena.alloc(NDC * 128, BF16)
        OP("dve", lambda e: e.memset(hmapR[:], 0.0), writes=["hmapR"], tag="c_hmapR0")
        OP("dve", lambda e: e.memset(hmapR[64:80, :], 1.0), writes=["hmapR"],
           tag="c_hmapR1")
        for dc in range(NDC):
            for half in range(2):
                sl = hmapR[64:80, dc * 128 + half * 64: dc * 128 + (half + 1) * 64]
                OP("gp", lambda g, sl=sl, dc=dc, half=half: g.affine_select(
                    out=sl, in_=sl, compare_op=ALU.is_equal, fill=0.0,
                    base=-(2 * dc + half), pattern=[[0, 64]], channel_multiplier=1),
                   reads=["hmapR"], writes=["hmapR"], tag="c_hmapR")
        st = ST1
        W = st + 4
        xw = arena.alloc(NDC * W, BF16)
        vw = arena.alloc(NDC * W, BF16)
        sq = vw
        lntmp = vw
        qw = arena.alloc(NDC * W, BF16)
        kw = arena.alloc(NDC * W, BF16)
        outf, outr, zt = qw, kw, qw
        scsb = arena.alloc(3 * st, F32)
        tmpc = scsb
        stt = {}
        stt["mu"] = arena.alloc(W, F32)
        stt["va"] = arena.alloc(W, F32)
        stt["tmp"] = arena.alloc(W, F32)
        stt["arow"] = arena.alloc(W, F32)
        stt["srow"] = stt["va"]
        mubits = stt["mu"][:].bitcast(BF16)
        stt["arow_bf"] = mubits[:, 0:W]
        stt["srow_bf"] = mubits[:, W:2 * W]
        stt["k_srow"] = "varow"
        stt["k_abf"] = "murow"
        stt["k_sbf"] = "murow"
        stt["absc"] = arena.alloc(NDC * W, BF16)
        stt["sbsc"] = arena.alloc(NDC * W, BF16)
        attnb = stt["sbsc"][:, 0:3 * st]
        mx = stt["tmp"]
        prods4 = stt["arow"][:, 0:st].bitcast(BF16)

        KXW = "vw"
        KQ = "qw"
        KK_ = "kw"
        KSC = "scsb"
        KMX = "tmprow"

        def xw3():
            return xw[:].rearrange("p (c t) -> p c t", c=NDC)

        for s in range(T // st):
            t0 = s * st
            first, last = (s == 0), (s == T // st - 1)
            rd0 = max(t0 - 2, 0)
            rd1 = min(t0 + st + 2, T)
            dst0 = rd0 - (t0 - 2)
            rlen = rd1 - rd0
            xk = gkeys(rd0, rlen)

            def srcw(dc=None, rd0=rd0, rlen=rlen):
                if dc is None:
                    return xres[:].rearrange("p (c t) -> p c t", c=NDC)[
                        :, :, rd0:rd0 + rlen]
                return xres[:, dc * T + rd0:dc * T + rd0 + rlen]

            if first:
                OP("dve", lambda e: e.memset(xw3()[:, :, 0:2], 0.0),
                   writes=["xw"], tag="xwpadL")
            if last:
                OP("dve", lambda e: e.memset(xw3()[:, :, W - 2:W], 0.0),
                   writes=["xw"], tag="xwpadR")
            OP("act", lambda e, rlen=rlen, srcw=srcw: e.activation(
                sq[:].rearrange("p (c t) -> p c t", c=NDC)[:, :, 0:rlen],
                srcw(), AF.Square),
               reads=xk, writes=[KXW], tag="l1a_sq")
            for dc in range(NDC):
                OP("te", lambda e, dc=dc, rlen=rlen, srcw=srcw: e.matmul(
                    pb[0][0:1, 0:rlen], onesc128[:], srcw(dc),
                    start=(dc == 0), stop=(dc == NDC - 1)),
                   reads=xk + ["onesc128"], writes=["pb0"], tag="l1a_s1",
                   mmkey=(0, 128))
            for dc in range(NDC):
                OP("te", lambda e, dc=dc, rlen=rlen: e.matmul(
                    pb[1][0:1, 0:rlen], onesc128[:], sq[:, dc * W:dc * W + rlen],
                    start=(dc == 0), stop=(dc == NDC - 1)),
                   reads=[KXW, "onesc128"], writes=["pb1"], tag="l1a_s2",
                   mmkey=(0, 128))
            _stat_math(rlen, stt, 0, l1a_eps=2)
            _bcg(rlen, stt, "g1", "b1", pitch=W)
            a_b = stt["absc"][:].rearrange("p (c t) -> p c t", c=NDC)[:, :, 0:rlen]
            s_b = stt["sbsc"][:].rearrange("p (c t) -> p c t", c=NDC)[:, :, 0:rlen]
            OP("dve", lambda e, rlen=rlen, srcw=srcw, a_b=a_b: e.tensor_tensor(
                lntmp[:].rearrange("p (c t) -> p c t", c=NDC)[:, :, 0:rlen],
                srcw(), a_b, ALU.mult),
               reads=xk + ["absc"], writes=[KXW], tag="l1a_ap1")
            OP("dve", lambda e, rlen=rlen, dst0=dst0, s_b=s_b: e.tensor_tensor(
                xw3()[:, :, dst0:dst0 + rlen],
                lntmp[:].rearrange("p (c t) -> p c t", c=NDC)[:, :, 0:rlen],
                s_b, ALU.add),
               reads=[KXW, "sbsc"], writes=["xw"], tag="l1a_ap2")

            def proj_w(wsb, wkey, out, okey, bias):
                for dc in range(NDC):
                    bank = 1 + (dc % 2)
                    for kc in range(NDC):
                        OP("te", lambda e, kc=kc, dc=dc, wsb=wsb, bank=bank: e.matmul(
                            pb[bank][:, 0:W],
                            wsb[:, kc * D + dc * 128: kc * D + (dc + 1) * 128],
                            xw[:, kc * W:(kc + 1) * W],
                            start=(kc == 0), stop=(kc == NDC - 1)),
                           reads=[wkey, "xw"], writes=[f"pb{bank}"], tag="l1a_p",
                           mmkey=(0, 128))
                    OP("act", lambda e, dc=dc, out=out, bias=bias, bank=bank:
                       e.activation(
                        out[:, dc * W:(dc + 1) * W], pb[bank][:, 0:W], AF.Identity,
                        bias=pf[bias][:, dc:dc + 1]),
                       reads=[f"pb{bank}", "pf_" + bias], writes=[okey], tag="l1a_pe")

            proj_w(WQ, "WQ1", qw, KQ, "bq1")
            proj_w(WK, "WK1", kw, KK_, "bk1")
            for tname, tkey in ((qw, KQ), (kw, KK_)):
                if first:
                    OP("dve", lambda e, tname=tname: e.memset(
                        tname[:].rearrange("p (c t) -> p c t", c=NDC)[:, :, 0:2], 0.0),
                       reads=[tkey], writes=[tkey], tag="padL")
                if last:
                    OP("dve", lambda e, tname=tname: e.memset(
                        tname[:].rearrange("p (c t) -> p c t", c=NDC)[:, :, W - 2:W],
                        0.0),
                       reads=[tkey], writes=[tkey], tag="padR")

            pidx = 0
            for dc in range(NDC):
                for di, sgn in ((0, 1), (1, -1)):
                    for j in range(3):
                        qs = qw[:, dc * W + 2 - sgn:dc * W + 2 - sgn + st]
                        ks = kw[:, dc * W + 2 - sgn * j:dc * W + 2 - sgn * j + st]
                        sl = pidx % 2
                        pidx += 1
                        psl = prods4[:, sl * st:(sl + 1) * st]
                        OP("dve", lambda e, qs=qs, ks=ks, psl=psl: e.tensor_tensor(
                            psl, qs, ks, ALU.mult),
                           reads=[KQ, KK_], writes=[f"pr{sl}"], tag="l1a_prod")
                        bank = (2 + j) if di == 0 else (5 + j)
                        tp = (0, 0) if di == 0 else (0, 64)
                        r0 = 0 if di == 0 else 64
                        OP("te", lambda e, dc=dc, bank=bank, tp=tp, r0=r0,
                           psl=psl: e.matmul(
                            pb[bank][r0:r0 + 16, 0:st],
                            pmap[:, dc * 16:(dc + 1) * 16], psl,
                            start=(dc == 0), stop=(dc == NDC - 1),
                            tile_position=tp),
                           reads=[f"pr{sl}", "pmap"], writes=[f"pb{bank}"],
                           tag="l1a_sc", mmkey=(0, 128))
            SC = 1.0 / 8.0
            if s == 0:
                OP("dve", lambda e: e.memset(scsb[0:64, :], 0.0),
                   writes=[KSC], tag="sc_z")
            for j in range(3):
                OP("act", lambda e, j=j: e.activation(
                    scsb[0:16, j * st:(j + 1) * st], pb[2 + j][0:16, 0:st],
                    AF.Identity, scale=SC),
                   reads=[f"pb{2 + j}"], writes=[KSC], tag="sc_eF")
                OP("act", lambda e, j=j: e.activation(
                    scsb[64:80, j * st:(j + 1) * st], pb[5 + j][64:80, 0:st],
                    AF.Identity, scale=SC),
                   reads=[f"pb{5 + j}"], writes=[KSC], tag="sc_eR")
            for dc in range(NDC):
                bank = dc % 2
                for kc in range(NDC):
                    OP("te", lambda e, kc=kc, dc=dc, bank=bank: e.matmul(
                        pb[bank][:, 0:W],
                        WV[:, kc * D + dc * 128: kc * D + (dc + 1) * 128],
                        xw[:, kc * W:(kc + 1) * W],
                        start=(kc == 0), stop=(kc == NDC - 1)),
                       reads=["WV1", "xw"], writes=[f"pb{bank}"], tag="l1a_pv",
                       mmkey=(0, 128))
                OP("act", lambda e, dc=dc, bank=bank: e.activation(
                    vw[:, dc * W:(dc + 1) * W], pb[bank][:, 0:W], AF.Identity,
                    bias=pf["bv1"][:, dc:dc + 1]),
                   reads=[f"pb{bank}", "pf_bv1"], writes=[KXW], tag="l1a_ev")
            if first:
                OP("dve", lambda e: e.memset(
                    vw[:].rearrange("p (c t) -> p c t", c=NDC)[:, :, 0:2], 0.0),
                   reads=[KXW], writes=[KXW], tag="vpadL")
            if last:
                OP("dve", lambda e: e.memset(
                    vw[:].rearrange("p (c t) -> p c t", c=NDC)[:, :, W - 2:W], 0.0),
                   reads=[KXW], writes=[KXW], tag="vpadR")

            sc3 = scsb[0:80, 0:3 * st].rearrange("p (j t) -> p t j", j=3)
            OP("dve", lambda e, sc3=sc3: e.tensor_reduce(
                mx[0:80, 0:st], sc3, mybir.AxisListType.X, ALU.max),
               reads=[KSC], writes=[KMX], tag="sm_mx")
            mxb2 = mx[0:80, 0:st].rearrange("p (t x) -> p t x", x=1).broadcast_to(
                [80, st, 3])
            OP("dve", lambda e, sc3=sc3, mxb2=mxb2: e.tensor_tensor(
                sc3, sc3, mxb2, ALU.subtract),
               reads=[KSC, KMX], writes=[KSC], tag="sm_sub")
            OP("act", lambda e: e.activation(scsb[0:80, 0:3 * st],
                                             scsb[0:80, 0:3 * st], AF.Exp),
               reads=[KSC], writes=[KSC], tag="sm_exp")
            OP("dve", lambda e, sc3=sc3: e.tensor_reduce(
                mx[0:80, 0:st], sc3, mybir.AxisListType.X, ALU.add),
               reads=[KSC], writes=[KMX], tag="sm_den")
            OP("act", lambda e: e.activation(mx[0:80, 0:st], mx[0:80, 0:st], AF.Ln),
               reads=[KMX], writes=[KMX], tag="sm_ln")
            OP("act", lambda e: e.activation(mx[0:80, 0:st], mx[0:80, 0:st],
                                             AF.Exp, scale=-1.0),
               reads=[KMX], writes=[KMX], tag="sm_rcp")
            OP("dve", lambda e, sc3=sc3, mxb2=mxb2: e.tensor_tensor(
                attnb[0:80, 0:3 * st].rearrange("p (j t) -> p t j", j=3),
                sc3, mxb2, ALU.mult),
               reads=[KSC, KMX], writes=["sbsc"], tag="sm_attn")

            for di, sgn, hm, hkey, r0, outt, okey in (
                    (0, 1, hmap, "hmap", 0, outf, KQ),
                    (1, -1, hmapR, "hmapR", 64, outr, KK_)):
                for dc in range(NDC):
                    w0 = 2 if dc % 2 == 0 else 3
                    for j in range(3):
                        bank = 2 * w0 + (0 if j < 2 else 1)
                        col = (j % 2) * st
                        OP("te", lambda e, dc=dc, j=j, hm=hm, r0=r0, bank=bank,
                           col=col: e.matmul(
                            pb[bank][:, col:col + st],
                            hm[r0:r0 + 16, dc * 128:(dc + 1) * 128],
                            attnb[r0:r0 + 16, j * st:(j + 1) * st],
                            start=True, stop=True),
                           reads=["sbsc", hkey], writes=[f"pb{bank}"],
                           tag="l1a_abc", mmkey=(r0, 16))
                    vv = vw[:, dc * W:(dc + 1) * W]
                    tsl = tmpc[:, 0:3 * st]
                    OP("dve", lambda e, vv=vv, sgn=sgn, w0=w0, tsl=tsl:
                       e.tensor_tensor(
                        tsl.rearrange("p (t j) -> p t j", j=3),
                        pbw[w0][:, 0:3 * st].rearrange("p (j t) -> p t j", j=3),
                        _shiftn(vv, sgn, 0, 3), ALU.mult),
                       reads=[f"pb{2 * w0}", f"pb{2 * w0 + 1}", KXW],
                       writes=[KSC], tag="l1a_cm")
                    OP("dve", _lp(lambda e, dc=dc, outt=outt, tsl=tsl:
                       e.tensor_reduce(
                        outt[:, dc * W + 2:dc * W + 2 + st],
                        tsl.rearrange("p (t j) -> p t j", j=3),
                        mybir.AxisListType.X, ALU.add)),
                       reads=[KSC], writes=[okey], tag="l1a_cred")

            for oc in range(NDC):
                bank = 1 + (oc % 2)
                for kc in range(2 * NDC):
                    srct = outf if kc < NDC else outr
                    srk = KQ if kc < NDC else KK_
                    kcl = kc % NDC
                    OP("te", lambda e, oc=oc, kc=kc, kcl=kcl, srct=srct, bank=bank:
                       e.matmul(
                        pb[bank][:, 0:st],
                        WFB[:, kc * D + oc * 128: kc * D + (oc + 1) * 128],
                        srct[:, kcl * W + 2:kcl * W + 2 + st],
                        start=(kc == 0), stop=(kc == 2 * NDC - 1)),
                       reads=["WFB", srk], writes=[f"pb{bank}"], tag="l1a_fb",
                       mmkey=(0, 128))
                OP("dve", lambda e, oc=oc, bank=bank: e.scalar_tensor_tensor(
                    xw[:, oc * W + 2:oc * W + 2 + st], pb[bank][:, 0:st],
                    pf["bfb"][:, oc:oc + 1],
                    xw[:, oc * W + 2:oc * W + 2 + st], ALU.add, ALU.add),
                   reads=[f"pb{bank}", "pf_bfb", "xw"], writes=["xw"], tag="l1a_mid")

            midv = xw3()[:, :, 2:2 + st]
            OP("act", lambda e, midv=midv: e.activation(
                sq[:].rearrange("p (c t) -> p c t", c=NDC)[:, :, 0:st],
                midv, AF.Square),
               reads=["xw"], writes=[KXW], tag="m_sq")
            for dc in range(NDC):
                OP("te", lambda e, dc=dc: e.matmul(
                    pb[0][0:1, 0:st], onesc128[:],
                    xw[:, dc * W + 2:dc * W + 2 + st],
                    start=(dc == 0), stop=(dc == NDC - 1)),
                   reads=["xw", "onesc128"], writes=["pb0"], tag="m_s1",
                   mmkey=(0, 128))
            for dc in range(NDC):
                OP("te", lambda e, dc=dc: e.matmul(
                    pb[1][0:1, 0:st], onesc128[:], sq[:, dc * W:dc * W + st],
                    start=(dc == 0), stop=(dc == NDC - 1)),
                   reads=[KXW, "onesc128"], writes=["pb1"], tag="m_s2",
                   mmkey=(0, 128))
            _stat_math(st, stt, 2)
            _bcg(st, stt, "g2", "b2", pitch=W)
            a_b2 = stt["absc"][:].rearrange("p (c t) -> p c t", c=NDC)[:, :, 0:st]
            s_b2 = stt["sbsc"][:].rearrange("p (c t) -> p c t", c=NDC)[:, :, 0:st]
            OP("dve", lambda e, a_b2=a_b2, midv=midv: e.tensor_tensor(
                lntmp[:].rearrange("p (c t) -> p c t", c=NDC)[:, :, 0:st],
                midv, a_b2, ALU.mult),
               reads=["xw", "absc"], writes=[KXW], tag="z_ap1")
            OP("dve", lambda e, s_b2=s_b2: e.tensor_tensor(
                zt[:].rearrange("p (c t) -> p c t", c=NDC)[:, :, 0:st],
                lntmp[:].rearrange("p (c t) -> p c t", c=NDC)[:, :, 0:st],
                s_b2, ALU.add),
               reads=[KXW, "sbsc"], writes=[KQ], tag="z_ap2")

            OP("act", lambda e: e.add_instruction(mybir.InstLoadActFuncSet(
                name=nc.get_next_instruction_name(), act_func_set_id=10,
                ins=[], outs=[])), tag="tbl10")
            for oc in range(NDC):
                bank = 1 + (oc % 2)
                for kc in range(NDC):
                    OP("te", lambda e, oc=oc, kc=kc, bank=bank: e.matmul(
                        pb[bank][:, 0:st],
                        WO[:, kc * D + oc * 128: kc * D + (oc + 1) * 128],
                        zt[:, kc * W:kc * W + st],
                        start=(kc == 0), stop=(kc == NDC - 1)),
                       reads=["WO1", KQ], writes=[f"pb{bank}"], tag="l1a_wo",
                       mmkey=(0, 128))
                OP("act", lambda e, oc=oc, bank=bank: e.activation(
                    lntmp[:, oc * W:oc * W + st], pb[bank][:, 0:st], AF.Gelu,
                    bias=pf["bo1"][:, oc:oc + 1]),
                   reads=[f"pb{bank}", "pf_bo1"], writes=[KXW], tag="l1a_gel")
            OP("act", lambda e: e.add_instruction(mybir.InstLoadActFuncSet(
                name=nc.get_next_instruction_name(), act_func_set_id=6,
                ins=[], outs=[])), tag="tbl6b")
            for oc in range(NDC):
                OP("dve", lambda e, oc=oc, t0=t0: e.tensor_tensor(
                    xres[:, oc * T + t0: oc * T + t0 + st],
                    xres[:, oc * T + t0: oc * T + t0 + st],
                    lntmp[:, oc * W:oc * W + st], ALU.add),
                   reads=[KXW] + xk, writes=xk, tag="l1a_res")
        arena.pop()

    def _shiftn(vv, sgn, j0, nj):
        base = vv[:, 2 - sgn * j0:2 - sgn * j0 + ST1]
        return bass.AP(base.tensor, base.offset,
                       [list(base.ap[0]), [1, ST1], [-sgn, nj]])

    def emit_l1f():
        arena.mark()
        WR = arena.alloc(NDC * D, BF16); load_weight(WR, wr2, "WR2")
        WK2 = arena.alloc(NDC * D, BF16); load_weight(WK2, wk2, "WK2")
        WV2 = arena.alloc(NDC * D, BF16); load_weight(WV2, wv2, "WV2")
        st = STF
        Wd = st + 1
        xh = arena.alloc(NDC * Wd, BF16)
        sq = arena.alloc(NDC * st, BF16)
        lntmp = arena.alloc(NDC * st, BF16)
        xr = arena.alloc(NDC * st, BF16)
        xkk = arena.alloc(NDC * st, BF16)
        rt = arena.alloc(NDC * st, BF16)
        kact = arena.alloc(NDC * st, BF16)
        ytile = arena.alloc(NDC * st, F32)
        stt = mkstats(st)
        OP("dve", lambda e: e.memset(
            xh[:].rearrange("p (c t) -> p c t", c=NDC)[:, :, 0:1], 0.0),
           writes=["xh"], tag="xh0")
        for s in range(T // st):
            t0 = s * st
            xk = gkeys(t0, st)
            emit_stats(t0, st, lambda dc=None, t0=t0: xres3(t0, st, dc), xk, sq, stt, 0)
            a_b = stt["absc"][:, 0:st].rearrange("p (c t) -> p c t", c=1).broadcast_to(
                [128, NDC, st])
            s_b = stt["sbsc"][:, 0:st].rearrange("p (c t) -> p c t", c=1).broadcast_to(
                [128, NDC, st])
            OP("dve", lambda e, t0=t0: e.tensor_tensor(
                lntmp[:].rearrange("p (c t) -> p c t", c=NDC),
                xres3(t0, st), a_b, ALU.mult),
               reads=xk + ["absc"], writes=["lntmp"], tag="f_ap1")
            OP("dve", lambda e: e.tensor_tensor(
                xh[:].rearrange("p (c t) -> p c t", c=NDC)[:, :, 1:Wd],
                lntmp[:].rearrange("p (c t) -> p c t", c=NDC), s_b, ALU.add),
               reads=["lntmp", "sbsc"], writes=["xh"], tag="f_ap2")
            mur3 = pf["mur"][:].rearrange("p (c x) -> p c x", x=1).broadcast_to(
                [128, NDC, st])
            muk3 = pf["muk"][:].rearrange("p (c x) -> p c x", x=1).broadcast_to(
                [128, NDC, st])
            xcur = xh[:].rearrange("p (c t) -> p c t", c=NDC)[:, :, 1:Wd]
            xprv = xh[:].rearrange("p (c t) -> p c t", c=NDC)[:, :, 0:Wd - 1]
            OP("dve", lambda e: e.tensor_tensor(
                lntmp[:].rearrange("p (c t) -> p c t", c=NDC), xcur, xprv,
                ALU.subtract),
               reads=["xh"], writes=["lntmp"], tag="f_xd")
            OP("dve", lambda e, mur3=mur3: e.tensor_tensor(
                sq[:].rearrange("p (c t) -> p c t", c=NDC),
                lntmp[:].rearrange("p (c t) -> p c t", c=NDC), mur3, ALU.mult),
               reads=["lntmp", "pf_mur"], writes=["sq"], tag="f_xr1")
            OP("dve", lambda e: e.tensor_tensor(
                xr[:].rearrange("p (c t) -> p c t", c=NDC),
                sq[:].rearrange("p (c t) -> p c t", c=NDC), xprv, ALU.add),
               reads=["sq", "xh"], writes=["xr"], tag="f_xr2")
            OP("dve", lambda e, muk3=muk3: e.tensor_tensor(
                sq[:].rearrange("p (c t) -> p c t", c=NDC),
                lntmp[:].rearrange("p (c t) -> p c t", c=NDC), muk3, ALU.mult),
               reads=["lntmp", "pf_muk"], writes=["sq"], tag="f_xk1")
            OP("dve", lambda e: e.tensor_tensor(
                xkk[:].rearrange("p (c t) -> p c t", c=NDC),
                sq[:].rearrange("p (c t) -> p c t", c=NDC), xprv, ALU.add),
               reads=["sq", "xh"], writes=["xkk"], tag="f_xk2")
            OP("dve", lambda e: e.tensor_copy(
                xh[:].rearrange("p (c t) -> p c t", c=NDC)[:, :, 0:1],
                xh[:].rearrange("p (c t) -> p c t", c=NDC)[:, :, Wd - 1:Wd]),
               reads=["xh"], writes=["xh"], tag="f_carry")
            for oc in range(NDC):
                bank = 1 + (oc % 2)
                for kc in range(NDC):
                    OP("te", lambda e, oc=oc, kc=kc, bank=bank: e.matmul(
                        pb[bank][:, 0:st],
                        WR[:, kc * D + oc * 128: kc * D + (oc + 1) * 128],
                        xr[:, kc * st:(kc + 1) * st],
                        start=(kc == 0), stop=(kc == NDC - 1)),
                       reads=["WR2", "xr"], writes=[f"pb{bank}"], tag="f_r",
                       mmkey=(0, 128))
                OP("act", lambda e, oc=oc, bank=bank: e.activation(
                    rt[:, oc * st:(oc + 1) * st], pb[bank][:, 0:st], AF.Exp,
                    scale=-1.0),
                   reads=[f"pb{bank}"], writes=[f"rt{oc}"], tag="f_re")
                OP("dve", lambda e, oc=oc: e.tensor_scalar_add(
                    rt[:, oc * st:(oc + 1) * st], rt[:, oc * st:(oc + 1) * st],
                    1.0),
                   reads=[f"rt{oc}"], writes=[f"rt{oc}"], tag="f_r1")
                OP("act", lambda e, oc=oc: e.activation(
                    rt[:, oc * st:(oc + 1) * st], rt[:, oc * st:(oc + 1) * st],
                    AF.Ln),
                   reads=[f"rt{oc}"], writes=[f"rt{oc}"], tag="f_rl")
                OP("act", lambda e, oc=oc: e.activation(
                    rt[:, oc * st:(oc + 1) * st], rt[:, oc * st:(oc + 1) * st],
                    AF.Exp, scale=-1.0),
                   reads=[f"rt{oc}"], writes=[f"rt{oc}"], tag="f_rx")
            for oc in range(NDC):
                bank = 1 + (oc % 2)
                for kc in range(NDC):
                    OP("te", lambda e, oc=oc, kc=kc, bank=bank: e.matmul(
                        pb[bank][:, 0:st],
                        WK2[:, kc * D + oc * 128: kc * D + (oc + 1) * 128],
                        xkk[:, kc * st:(kc + 1) * st],
                        start=(kc == 0), stop=(kc == NDC - 1)),
                       reads=["WK2", "xkk"], writes=[f"pb{bank}"], tag="f_k",
                       mmkey=(0, 128))
                OP("dve", lambda e, oc=oc, bank=bank: e.tensor_scalar_max(
                    sq[:, oc * st:(oc + 1) * st], pb[bank][:, 0:st], 0.0),
                   reads=[f"pb{bank}"], writes=[f"sqo{oc}"], tag="f_km")
                OP("act", lambda e, oc=oc: e.activation(
                    kact[:, oc * st:(oc + 1) * st], sq[:, oc * st:(oc + 1) * st],
                    AF.Square),
                   reads=[f"sqo{oc}"], writes=[f"kact{oc}"], tag="f_k2")
            for oc in range(NDC):
                bank = 1 + (oc % 2)
                for kc in range(NDC):
                    OP("te", lambda e, oc=oc, kc=kc, bank=bank: e.matmul(
                        pb[bank][:, 0:st],
                        WV2[:, kc * D + oc * 128: kc * D + (oc + 1) * 128],
                        kact[:, kc * st:(kc + 1) * st],
                        start=(kc == 0), stop=(kc == NDC - 1)),
                       reads=["WV2", f"kact{kc}"], writes=[f"pb{bank}"], tag="f_v",
                       mmkey=(0, 128))
                OP("dve", lambda e, oc=oc, bank=bank: e.tensor_tensor(
                    lntmp[:, oc * st:(oc + 1) * st], pb[bank][:, 0:st],
                    rt[:, oc * st:(oc + 1) * st], ALU.mult),
                   reads=[f"pb{bank}", f"rt{oc}"], writes=["lntmp"], tag="f_vm")
                OP("dve", lambda e, oc=oc, t0=t0: e.tensor_tensor(
                    ytile[:, oc * st:(oc + 1) * st],
                    xres[:, oc * T + t0:oc * T + t0 + st],
                    lntmp[:, oc * st:(oc + 1) * st], ALU.add),
                   reads=["lntmp"] + xk, writes=["ytile"], tag="f_y")
            OP("sy", lambda e, t0=t0: e.dma_start(
                out=y_fm.rearrange("(c p) t -> p c t", p=128)[:, :, t0:t0 + st],
                in_=ytile[:].rearrange("p (c t) -> p c t", c=NDC)),
               reads=["ytile"], writes=[f"y:{t0 // 512}"], dma=True, tag="f_yd")
        arena.pop()

    def emit_out():
        arena.mark()
        ytile = arena.alloc(NDC * 512, F32)
        for s in range(8):
            t0 = s * 512
            OP("act", lambda e, t0=t0: e.activation(
                ytile[:].rearrange("p (c t) -> p c t", c=NDC),
                xres[:].rearrange("p (c t) -> p c t", c=NDC)[:, :, t0:t0 + 512],
                AF.Identity),
               reads=[f"xres:{t0 // 512}"], writes=["ytile"], tag="yt")
            OP("sy", lambda e, t0=t0: e.dma_start(
                out=y_fm.rearrange("(c p) t -> p c t", p=128)[:, :, t0:t0 + 512],
                in_=ytile[:].rearrange("p (c t) -> p c t", c=NDC)),
               reads=["ytile"], writes=[f"y:{t0 // 512}"], dma=True, tag="yd")
        arena.pop()

    OP("act", lambda e: e.add_instruction(mybir.InstLoadActFuncSet(
        name=nc.get_next_instruction_name(), act_func_set_id=6, ins=[], outs=[])),
       tag="tbl6")
    emit_consts()
    emit_param_loads()
    if "l0a" in phases:
        emit_l0a()
        sch.barrier()
    if "l0f" in phases:
        emit_l0f()
        sch.barrier()
    if "l1a" in phases:
        emit_l1a()
        sch.barrier()
    if "l1f" in phases:
        emit_l1f()
        sch.barrier()
    if "l1f" not in phases:
        emit_out()
    OP("sy", lambda e: e.nop(), reads=[f"y:{s}" for s in range(8)], tag="fin")

    with nc.Block() as blk:
        sch.materialize(blk, sems)
    for cm in reversed(ctxs):
        cm.__exit__(None, None, None)
    return nc


def host_inputs(inputs):
    shared = dict(
        wq0=np.ascontiguousarray(np.asarray(inputs["L0_attn_Wq"], np.float32)),
        wk0=np.ascontiguousarray(np.asarray(inputs["L0_attn_Wk"], np.float32)),
        wv0=np.ascontiguousarray(np.asarray(inputs["L0_attn_Wv"], np.float32)),
        wb0=np.ascontiguousarray(np.asarray(inputs["L0_attn_Wbeta"], np.float32)),
        bb0v=np.ascontiguousarray(np.asarray(inputs["L0_attn_b_beta"], np.float32)),
        wo0=np.ascontiguousarray(np.asarray(inputs["L0_attn_Wout"], np.float32)),
        wf8=np.ascontiguousarray(np.asarray(inputs["L0_ffn_weight"], np.float32).T),
        bf0=np.ascontiguousarray(np.asarray(inputs["L0_ffn_bias"], np.float32)),
        wq1=np.ascontiguousarray(np.asarray(inputs["L1_attn_Wq"], np.float32)),
        bq1=np.ascontiguousarray(np.asarray(inputs["L1_attn_bq"], np.float32)),
        wk1=np.ascontiguousarray(np.asarray(inputs["L1_attn_Wk"], np.float32)),
        bk1=np.ascontiguousarray(np.asarray(inputs["L1_attn_bk"], np.float32)),
        wv1=np.ascontiguousarray(np.asarray(inputs["L1_attn_Wv"], np.float32)),
        bv1=np.ascontiguousarray(np.asarray(inputs["L1_attn_bv"], np.float32)),
        wfb=np.ascontiguousarray(np.asarray(inputs["L1_attn_Wfb"], np.float32)),
        bfb=np.ascontiguousarray(np.asarray(inputs["L1_attn_bfb"], np.float32)),
        wo1=np.ascontiguousarray(np.asarray(inputs["L1_attn_Wo"], np.float32)),
        bo1=np.ascontiguousarray(np.asarray(inputs["L1_attn_bo"], np.float32)),
        g1=np.ascontiguousarray(np.asarray(inputs["L1_attn_g1"], np.float32)),
        b1=np.ascontiguousarray(np.asarray(inputs["L1_attn_b1"], np.float32)),
        g2=np.ascontiguousarray(np.asarray(inputs["L1_attn_g2"], np.float32)),
        b2=np.ascontiguousarray(np.asarray(inputs["L1_attn_b2"], np.float32)),
        wr2=np.ascontiguousarray(np.asarray(inputs["L1_ffn_W_r"], np.float32).T),
        wk2=np.ascontiguousarray(np.asarray(inputs["L1_ffn_W_k"], np.float32).T),
        wv2=np.ascontiguousarray(np.asarray(inputs["L1_ffn_W_v"], np.float32).T),
        mur=np.ascontiguousarray(np.asarray(inputs["L1_ffn_mu_r"], np.float32)),
        muk=np.ascontiguousarray(np.asarray(inputs["L1_ffn_mu_k"], np.float32)),
    )
    xs = np.asarray(inputs["L0_attn_x"], np.float32)
    in_maps = []
    for b in range(8):
        m = dict(shared)
        m["x_fm"] = np.ascontiguousarray(xs[b].T)
        in_maps.append(m)
    return in_maps


def kernel(**inputs):
    in_maps = host_inputs(inputs)
    nc = build()
    res = run_bass_kernel_spmd(nc, in_maps, core_ids=list(range(8)))
    out = np.stack([np.ascontiguousarray(res.results[b]["y_fm"].T) for b in range(8)])
    return out.astype(np.float32)


# version 10
# speedup vs baseline: 1.0739x; 1.0053x over previous
import numpy as np
import concourse.bass as bass
import concourse.mybir as mybir
from concourse.bass_utils import run_bass_kernel_spmd

F32 = mybir.dt.float32
BF16 = mybir.dt.bfloat16
FP8 = mybir.dt.float8e4
AF = mybir.ActivationFunctionType
ALU = mybir.AluOpType

T = 4096
D = 1024
NDC = 8
H = 16
CH = 128
ST0 = 256
NS0 = T // ST0
STF = 512
ST1 = 256
NLEV = 7

DT_SZ = {F32: 4, BF16: 2, FP8: 1}


class Op:
    __slots__ = ("eng", "fn", "reads", "writes", "dma", "idx", "deps", "inc",
                 "cum", "waits", "tag", "lane", "force")

    def __init__(self, eng, fn, reads, writes, dma, idx, tag):
        self.eng, self.fn = eng, fn
        self.reads, self.writes = list(reads), list(writes)
        self.dma, self.idx, self.tag = dma, idx, tag
        self.deps = []
        self.inc = False
        self.cum = 0
        self.waits = []
        self.lane = 0
        self.force = False


class Sched:
    ENGS = ("te", "dve", "act", "gp", "sy")

    def __init__(self):
        self.ops = []
        self.res = {}
        self.last_on = {e: None for e in self.ENGS}
        self.bar = {e: None for e in self.ENGS}
        self.te_key = None
        self.n_drains = 0

    def drain(self, eng):
        last = self.last_on[eng]
        if last is None:
            return
        o = Op(eng, lambda e: None, [], [], False, len(self.ops), "drain")
        o.deps = [last]
        o.force = True
        self.ops.append(o)
        self.n_drains += 1

    def op(self, eng, fn, reads=(), writes=(), dma=False, tag="", mmkey=None):
        if eng == "te" and mmkey is not None:
            if self.te_key is not None and self.te_key != mmkey:
                self.drain("te")
            self.te_key = mmkey
        idx = len(self.ops)
        o = Op(eng, fn, reads, writes, dma, idx, tag)
        deps = set()
        for r in o.reads:
            e = self.res.get(r)
            if e and e["w"] is not None:
                deps.add(e["w"])
        for w in o.writes:
            e = self.res.get(w)
            if e:
                if e["w"] is not None:
                    deps.add(e["w"])
                deps.update(e["r"])
        for e2, i2 in self.bar.items():
            if i2 is not None and e2 != eng:
                deps.add(i2)
        deps.discard(idx)
        o.deps = sorted(deps)
        for r in o.reads:
            self.res.setdefault(r, {"w": None, "r": []})["r"].append(idx)
        for w in o.writes:
            self.res[w] = {"w": idx, "r": []}
        self.ops.append(o)
        self.last_on[eng] = idx
        return idx

    def barrier(self):
        self.drain("te")
        self.te_key = None
        self.bar = dict(self.last_on)

    NLANES = 4

    @staticmethod
    def _semkey(op):
        if op.dma:
            return (op.eng, "d", op.lane)
        return (op.eng, "c")

    def materialize(self, blk, sems):
        ops = self.ops
        def exempt(dp, o):
            if o.force:
                return False
            if dp.eng != o.eng:
                return False
            if dp.dma and o.dma:
                return True
            if dp.dma or o.dma:
                return o.eng != "gp" and not dp.dma
            return o.eng != "gp"
        for o in ops:
            for d in o.deps:
                dp = ops[d]
                if exempt(dp, o):
                    continue
                dp.inc = True
        counters = {}
        qcount = {}
        for o in ops:
            if o.dma:
                qi = qcount.get(o.eng, 0)
                qcount[o.eng] = qi + 1
                o.lane = qi % self.NLANES
                k = self._semkey(o)
                counters[k] = counters.get(k, 0) + 16
                o.cum = counters[k]
                o.inc = True
            elif o.inc:
                k = self._semkey(o)
                counters[k] = counters.get(k, 0) + 1
                o.cum = counters[k]
        marks = {e: {} for e in self.ENGS}
        for o in ops:
            need = {}
            for d in o.deps:
                dp = ops[d]
                if exempt(dp, o):
                    continue
                k = self._semkey(dp)
                if dp.cum > need.get(k, 0):
                    need[k] = dp.cum
            if o.dma and o.cum > 16:
                k = self._semkey(o)
                if o.cum - 16 > need.get(k, 0):
                    need[k] = o.cum - 16
            wm = marks[o.eng]
            for k, v in need.items():
                if v > wm.get(k, 0):
                    o.waits.append((k, v))
                    wm[k] = v

        def run_engine(ename):
            def body(eng):
                for o in ops:
                    if o.eng != ename:
                        continue
                    for (k, v) in o.waits:
                        eng.wait_ge(sems[k], v)
                    if o.tag == "drain":
                        continue
                    inst = o.fn(eng)
                    if o.inc:
                        if inst is None:
                            raise RuntimeError(f"op {o.idx} {o.tag}: no inst for inc")
                        inst.then_inc(sems[self._semkey(o)], 16 if o.dma else 1)
            return body
        blk.tensor(run_engine("te"))
        blk.vector(run_engine("dve"))
        blk.scalar(run_engine("act"))
        blk.gpsimd(run_engine("gp"))
        blk.sync(run_engine("sy"))


class Arena:
    def __init__(self, ap, total_bytes):
        self.ap = ap
        self.total = total_bytes
        self.off = 0
        self.marks = []

    def alloc(self, cols, dt):
        nbytes = cols * DT_SZ[dt]
        pad = (nbytes + 3) & ~3
        assert self.off + pad <= self.total, f"arena overflow {self.off}+{pad}"
        a = self.off
        self.off += pad
        view = self.ap[:, a // 2:(a + nbytes + 1) // 2]
        if dt != BF16:
            view = view.bitcast(dt)
        return view

    def mark(self):
        self.marks.append(self.off)

    def pop(self):
        self.off = self.marks.pop()


def build(phases=("l0a", "l0f", "l1a", "l1f"), ns0=NS0, l0a_parts=("stats", "proj", "beta", "norms", "chunks", "wout"), nchunks=99, chunk_parts=("xinit", "grams", "solve", "uk", "m1", "o", "ds")):
    nc = bass.Bass("TRN2", target_bir_lowering=False, detect_race_conditions=False)
    dram = {}

    def din(name, shape):
        dram[name] = nc.dram_tensor(name, list(shape), F32, kind="ExternalInput").ap()
        return dram[name]

    x_fm = din("x_fm", [D, T])
    wq0 = din("wq0", [D, D]); wk0 = din("wk0", [D, D]); wv0 = din("wv0", [D, D])
    wb0 = din("wb0", [D, H]); wo0 = din("wo0", [D, D])
    wf8 = din("wf8", [D, D])
    wq1 = din("wq1", [D, D]); wk1 = din("wk1", [D, D]); wv1 = din("wv1", [D, D])
    wfb = din("wfb", [2 * D, D]); wo1 = din("wo1", [D, D])
    wr2 = din("wr2", [D, D]); wk2 = din("wk2", [D, D]); wv2 = din("wv2", [D, D])
    pfsrc = {}
    for nm in ("bf0", "bq1", "bk1", "bv1", "bfb", "bo1", "g1", "b1", "g2", "b2",
               "mur", "muk"):
        pfsrc[nm] = din(nm, [D])
    bb0_h = din("bb0v", [H])
    y_fm = nc.dram_tensor("y_fm", [D, T], F32, kind="ExternalOutput").ap()

    sch = Sched()
    OP = sch.op
    ctxs = []

    def sb(name, shape, dt):
        cm = nc.sbuf_tensor(name, list(shape), dt)
        t = cm.__enter__()
        ctxs.append(cm)
        return t

    def psm(name, shape, dt):
        cm = nc.psum_tensor(name, list(shape), dt)
        t = cm.__enter__()
        ctxs.append(cm)
        return t

    xres = sb("xres", [128, NDC * T], BF16)
    idn = sb("idn", [128, 128], BF16)
    maskSUL = sb("maskSUL", [128, 256], BF16)
    maskSU = maskSUL[:, 0:128]
    maskSL = maskSUL[:, 128:256]
    maskIU = sb("maskIU", [128, 128], BF16)
    onesc128 = sb("onesc128", [128, 1], BF16)
    onesc1 = sb("onesc1", [1, 128], BF16)
    onesrow = sb("onesrow", [1, 512], BF16)
    hmap = sb("hmap", [16, NDC * 128], BF16)
    pmap = sb("pmap", [128, NDC * 16], BF16)
    bb0r = sb("bb0r", [1, H], BF16)
    bb0c = sb("bb0c", [H, 2], F32)
    cvals = sb("cvals", [128, 4], F32)
    pf = {nm: sb("pf_" + nm, [128, NDC], F32) for nm in pfsrc}
    prow = {}
    hmapRbox = [None]
    arena_ap = sb("arena", [128, 70500], BF16)
    arena = Arena(arena_ap, 141000)
    pbw = [psm(f"pbw{j}", [128, 1024], F32) for j in range(4)]
    pb = [pbw[i // 2][:, (i % 2) * 512:(i % 2) * 512 + 512] for i in range(8)]

    sems = {}
    semkeys = [("te", "c"), ("dve", "c"), ("act", "c"), ("gp", "c"), ("sy", "c")]
    for q in ("sy", "gp", "act"):
        for ln in range(Sched.NLANES):
            semkeys.append((q, "d", ln))
    for k in semkeys:
        cm = nc.semaphore("sem_" + "_".join(str(x) for x in k))
        sems[k] = cm.__enter__()
        ctxs.append(cm)

    def emit_consts():
        OP("dve", lambda e: e.memset(idn[:], 0.0), writes=["idn"], tag="c_idn0")
        OP("gp", lambda g: g.affine_select(out=idn[:], in_=idn[:],
                                           compare_op=ALU.not_equal, fill=1.0, base=0,
                                           pattern=[[-1, 128]], channel_multiplier=1),
           reads=["idn"], writes=["idn"], tag="c_idn")

        OP("dve", lambda e: e.memset(maskSU[:], -1.0), writes=["maskSU"], tag="c_msu0")
        OP("gp", lambda g: g.affine_select(out=maskSU[:], in_=maskSU[:],
                                           compare_op=ALU.is_gt, fill=0.0, base=0,
                                           pattern=[[1, 128]], channel_multiplier=-1),
           reads=["maskSU"], writes=["maskSU"], tag="c_msu")

        OP("dve", lambda e: e.memset(maskIU[:], 1.0), writes=["maskIU"], tag="c_miu0")
        OP("gp", lambda g: g.affine_select(out=maskIU[:], in_=maskIU[:],
                                           compare_op=ALU.is_ge, fill=0.0, base=0,
                                           pattern=[[1, 128]], channel_multiplier=-1),
           reads=["maskIU"], writes=["maskIU"], tag="c_miu")
        OP("dve", lambda e: e.memset(maskSL[:], -1.0), writes=["maskSL"], tag="c_msl0")
        OP("gp", lambda g: g.affine_select(out=maskSL[:], in_=maskSL[:],
                                           compare_op=ALU.is_gt, fill=0.0, base=0,
                                           pattern=[[-1, 128]], channel_multiplier=1),
           reads=["maskSL"], writes=["maskSL"], tag="c_msl")

        OP("dve", lambda e: e.memset(onesc128[:], 1.0), writes=["onesc128"])
        OP("dve", lambda e: e.memset(onesc1[:], 1.0), writes=["onesc1"])
        OP("dve", lambda e: e.memset(onesrow[:], 1.0), writes=["onesrow"])
        def c_cv(e):
            e.memset(cvals[:, 0:1], 1e-5)
            e.memset(cvals[:, 1:2], 1e-24)
            return e.memset(cvals[:, 2:3], 1e-6)
        OP("dve", c_cv, writes=["cvals"], tag="c_cv")

        OP("dve", lambda e: e.memset(hmap[:], 1.0), writes=["hmap"], tag="c_hmap0")
        for dc in range(NDC):
            for half in range(2):
                sl = hmap[:, dc * 128 + half * 64: dc * 128 + (half + 1) * 64]
                OP("gp", lambda g, sl=sl, dc=dc, half=half: g.affine_select(
                    out=sl, in_=sl, compare_op=ALU.is_equal, fill=0.0,
                    base=-(2 * dc + half), pattern=[[0, 64]], channel_multiplier=1),
                   reads=["hmap"], writes=["hmap"], tag="c_hmap")

        OP("dve", lambda e: e.memset(pmap[:], 1.0), writes=["pmap"], tag="c_pmap0")
        for dc in range(NDC):
            for half in range(2):
                sl = pmap[64 * half:64 * (half + 1), dc * 16:(dc + 1) * 16]
                OP("gp", lambda g, sl=sl, dc=dc, half=half: g.affine_select(
                    out=sl, in_=sl, compare_op=ALU.is_equal, fill=0.0,
                    base=-(2 * dc + half), pattern=[[1, 16]], channel_multiplier=0),
                   reads=["pmap"], writes=["pmap"], tag="c_pmap")

    def emit_param_loads():
        OP("gp", lambda g: g.dma_start(out=xres[:].rearrange("p (c t) -> p c t", c=NDC),
                                       in_=x_fm.rearrange("(c p) t -> p c t", p=128)),
           writes=[f"xres:{gg}" for gg in range(8)], dma=True, tag="ld_x")
        OP("gp", lambda g: g.dma_start(out=bb0r[:],
                                       in_=bb0_h.rearrange("(x h) -> x h", x=1)),
           writes=["bb0r"], dma=True)
        OP("gp", lambda g: g.dma_start(out=bb0c[:, 0:1],
                                       in_=bb0_h.rearrange("(h x) -> h x", x=1),
                                       allow_slow_non_contiguous=True),
           writes=["bb0c"], dma=True)
        OP("dve", lambda e: e.tensor_scalar_mul(bb0c[:, 1:2], bb0c[:, 0:1], -1.0),
           reads=["bb0c"], writes=["bb0c"], tag="bb0n")
        for nm, src in pfsrc.items():
            OP("sy", lambda g, src=src, nm=nm: g.dma_start(
                out=pf[nm][:], in_=src.rearrange("(c p) -> p c", p=128),
                allow_slow_non_contiguous=True),
               writes=["pf_" + nm], dma=True)


    def load_weight(wsb, src, name, kin=D):
        nkc = kin // 128
        OP("gp", lambda g: g.dma_start(
            out=wsb[:].rearrange("p (c o) -> p c o", c=nkc),
            in_=src.rearrange("(c p) o -> p c o", p=128)),
           writes=[name], dma=True, tag="ld_" + name)

    def xres3(t0, st, dc=None):
        if dc is None:
            return xres[:].rearrange("p (c t) -> p c t", c=NDC)[:, :, t0:t0 + st]
        return xres[:, dc * T + t0:dc * T + t0 + st]

    def gkeys(t0, st):
        g0, g1 = t0 // 512, (t0 + st - 1) // 512
        return [f"xres:{g}" for g in range(g0, g1 + 1)]

    def emit_stats(t0, st, src_fn, src_keys, sq, stt, eps, l1a_eps=None, gb=None):
        OP("act", lambda e: e.activation(
            sq[:].rearrange("p (c t) -> p c t", c=NDC)[:, :, 0:st],
            src_fn(), AF.Square),
           reads=src_keys, writes=["sq"], tag="st_sq")
        for dc in range(NDC):
            OP("te", lambda e, dc=dc: e.matmul(
                pb[0][0:1, 0:st], onesc128[:], src_fn(dc),
                start=(dc == 0), stop=(dc == NDC - 1)),
               reads=src_keys + ["onesc128"], writes=["pb0"], tag="st_s1",
               mmkey=(0, 128))
        for dc in range(NDC):
            OP("te", lambda e, dc=dc: e.matmul(
                pb[1][0:1, 0:st], onesc128[:],
                sq[:, dc * (sq.shape[1] // NDC): dc * (sq.shape[1] // NDC) + st],
                start=(dc == 0), stop=(dc == NDC - 1)),
               reads=["sq", "onesc128"], writes=["pb1"], tag="st_s2",
               mmkey=(0, 128))
        mu, va, tmp = stt["mu"], stt["va"], stt["tmp"]
        arow, srow = stt["arow"], stt["srow"]
        arow_bf, srow_bf = stt["arow_bf"], stt["srow_bf"]
        KMU = stt.get("k_mu", "murow")
        KVA = stt.get("k_va", "varow")
        KTMP = stt.get("k_tmp", "tmprow")
        KSR = stt.get("k_srow", "srow")
        KABF = stt.get("k_abf", "arow_bf")
        KSBF = stt.get("k_sbf", "srow_bf")
        OP("dve", lambda e: e.tensor_scalar_mul(mu[0:1, 0:st], pb[0][0:1, 0:st], 1.0 / D),
           reads=["pb0"], writes=[KMU], tag="st_mu")
        OP("dve", lambda e: e.tensor_scalar_mul(va[0:1, 0:st], pb[1][0:1, 0:st], 1.0 / D),
           reads=["pb1"], writes=[KVA], tag="st_ex2")
        OP("dve", lambda e: e.scalar_tensor_tensor(
            tmp[0:1, 0:st], mu[0:1, 0:st], -1.0, mu[0:1, 0:st], ALU.mult, ALU.mult),
           reads=[KMU], writes=[KTMP], tag="st_msq")
        OP("dve", lambda e: e.tensor_add(va[0:1, 0:st], va[0:1, 0:st], tmp[0:1, 0:st]),
           reads=[KVA, KTMP], writes=[KVA], tag="st_var")
        OP("act", lambda e: e.activation(tmp[0:1, 0:st], va[0:1, 0:st], AF.Ln,
                                         bias=cvals[0:1, eps:eps + 1]),
           reads=[KVA, "cvals"], writes=[KTMP], tag="st_ln")
        OP("act", lambda e: e.activation(arow[0:1, 0:st], tmp[0:1, 0:st], AF.Exp, scale=-0.5),
           reads=[KTMP], writes=["arow"], tag="st_rstd")
        if l1a_eps is not None:
            OP("dve", lambda e: e.scalar_tensor_tensor(
                tmp[0:1, 0:st], arow[0:1, 0:st], 1.0, arow[0:1, 0:st], ALU.mult, ALU.mult),
               reads=["arow"], writes=[KTMP], tag="dln_r2")
            OP("dve", lambda e: e.tensor_mul(tmp[0:1, 0:st], tmp[0:1, 0:st], va[0:1, 0:st]),
               reads=[KTMP, KVA], writes=[KTMP], tag="dln_w")
            OP("act", lambda e: e.activation(tmp[0:1, 0:st], tmp[0:1, 0:st], AF.Ln,
                                             bias=cvals[0:1, l1a_eps:l1a_eps + 1]),
               reads=[KTMP, "cvals"], writes=[KTMP], tag="dln_ln")
            OP("act", lambda e: e.activation(tmp[0:1, 0:st], tmp[0:1, 0:st], AF.Exp,
                                             scale=-0.5),
               reads=[KTMP], writes=[KTMP], tag="dln_u")
            OP("dve", lambda e: e.tensor_mul(arow[0:1, 0:st], arow[0:1, 0:st],
                                             tmp[0:1, 0:st]),
               reads=["arow", KTMP], writes=["arow"], tag="dln_a")
        OP("dve", lambda e: e.scalar_tensor_tensor(
            srow[0:1, 0:st], mu[0:1, 0:st], -1.0, arow[0:1, 0:st], ALU.mult, ALU.mult),
           reads=[KMU, "arow"], writes=[KSR], tag="st_srow")
        OP("act", lambda e: e.activation(arow_bf[0:1, 0:st], arow[0:1, 0:st], AF.Identity),
           reads=["arow"], writes=[KABF], tag="st_abf")
        OP("act", lambda e: e.activation(srow_bf[0:1, 0:st], srow[0:1, 0:st], AF.Identity),
           reads=[KSR], writes=[KSBF], tag="st_sbf")
        absc, sbsc = stt["absc"], stt["sbsc"]
        if gb is None:
            OP("te", lambda e: e.matmul(pb[0][:, 0:st], onesc1[:], arow_bf[0:1, 0:st],
                                        start=True, stop=True),
               reads=[KABF, "onesc1"], writes=["pb0"], tag="bc_a", mmkey=(0, 1))
            OP("te", lambda e: e.matmul(pb[1][:, 0:st], onesc1[:], srow_bf[0:1, 0:st],
                                        start=True, stop=True),
               reads=[KSBF, "onesc1"], writes=["pb1"], tag="bc_s", mmkey=(0, 1))
            OP("act", lambda e: e.activation(absc[:, 0:st], pb[0][:, 0:st], AF.Identity),
               reads=["pb0"], writes=["absc"], tag="bc_ae")
            OP("act", lambda e: e.activation(sbsc[:, 0:st], pb[1][:, 0:st], AF.Identity),
               reads=["pb1"], writes=["sbsc"], tag="bc_se")
        else:
            gname, bname = gb
            gcol, bcol = stt["gcolb"], stt["bcolb"]
            for dc in range(NDC):
                OP("te", lambda e, dc=dc: e.matmul(
                    pb[0][:, 0:st], gcol[:, dc:dc + 1], arow_bf[0:1, 0:st],
                    start=True, stop=True),
                   reads=["arow_bf", "gcolb"], writes=["pb0"], tag="bcg_a")
                OP("act", lambda e, dc=dc: e.activation(
                    absc[:, dc * st:(dc + 1) * st], pb[0][:, 0:st], AF.Identity),
                   reads=["pb0"], writes=["absc"], tag="bcg_ae")
                OP("te", lambda e, dc=dc: e.matmul(
                    pb[1][:, 0:st], gcol[:, dc:dc + 1], srow_bf[0:1, 0:st],
                    start=True, stop=False),
                   reads=["srow_bf", "gcolb"], writes=["pb1"], tag="bcg_s")
                OP("te", lambda e, dc=dc: e.matmul(
                    pb[1][:, 0:st], bcol[:, dc:dc + 1], onesrow[0:1, 0:st],
                    start=False, stop=True),
                   reads=["onesrow", "bcolb"], writes=["pb1"], tag="bcg_b")
                OP("act", lambda e, dc=dc: e.activation(
                    sbsc[:, dc * st:(dc + 1) * st], pb[1][:, 0:st], AF.Identity),
                   reads=["pb1"], writes=["sbsc"], tag="bcg_se")

    def emit_apply(st, src_fn, src_keys, stt, out_ap, out_key, lntmp, perdc=False,
                   out_dt_note=None, split=False):
        absc, sbsc = stt["absc"], stt["sbsc"]
        if not perdc:
            a_b = absc[:, 0:st].rearrange("p (c t) -> p c t", c=1).broadcast_to([128, NDC, st])
            s_b = sbsc[:, 0:st].rearrange("p (c t) -> p c t", c=1).broadcast_to([128, NDC, st])
        else:
            a_b = absc[:].rearrange("p (c t) -> p c t", c=NDC)
            s_b = sbsc[:].rearrange("p (c t) -> p c t", c=NDC)
        if not split:
            OP("dve", lambda e: e.tensor_tensor(lntmp, src_fn(), a_b, ALU.mult),
               reads=src_keys + ["absc"], writes=["sq"], tag="ap1")
            OP("dve", lambda e: e.tensor_tensor(out_ap, lntmp, s_b, ALU.add),
               reads=["sq", "sbsc"], writes=[out_key], tag="ap2")
            return
        hn = NDC // 2
        for h in (0, 1):
            OP("dve", lambda e, h=h: e.tensor_tensor(
                lntmp[:, h * hn:(h + 1) * hn], src_fn()[:, h * hn:(h + 1) * hn],
                a_b[:, h * hn:(h + 1) * hn], ALU.mult),
               reads=src_keys + ["absc"], writes=["sq"], tag="ap1")
            OP("dve", lambda e, h=h: e.tensor_tensor(
                out_ap[:, h * hn:(h + 1) * hn], lntmp[:, h * hn:(h + 1) * hn],
                s_b[:, h * hn:(h + 1) * hn], ALU.add),
               reads=["sq", "sbsc"], writes=[f"{out_key}h{h}"], tag="ap2")

    def mkstats(st, big=False):
        d = {}
        d["mu"] = arena.alloc(st, F32)
        d["va"] = arena.alloc(st, F32)
        d["tmp"] = arena.alloc(st, F32)
        d["arow"] = arena.alloc(st, F32)
        d["srow"] = arena.alloc(st, F32)
        d["arow_bf"] = arena.alloc(st, BF16)
        d["srow_bf"] = arena.alloc(st, BF16)
        n = NDC * st if big else st
        d["absc"] = arena.alloc(n, BF16)
        d["sbsc"] = arena.alloc(n, BF16)
        return d

    def emit_l0a():
        arena.mark()
        WQ = arena.alloc(NDC * D, BF16); load_weight(WQ, wq0, "WQ0")
        WK = arena.alloc(NDC * D, BF16); load_weight(WK, wk0, "WK0")
        WV = arena.alloc(NDC * D, BF16); load_weight(WV, wv0, "WV0")
        WO = arena.alloc(NDC * D, BF16); load_weight(WO, wo0, "WO0")
        WB = arena.alloc(NDC * H, BF16); load_weight(WB, wb0, "WB0")
        st = ST0
        xhat = arena.alloc(NDC * st, BF16)
        sq = arena.alloc(NDC * st, BF16)
        qdm = arena.alloc(NDC * st, BF16)
        kdm = arena.alloc(NDC * st, BF16)
        khdm = arena.alloc(NDC * st, BF16)
        kbdm = arena.alloc(NDC * st, BF16)
        ktok = arena.alloc(2 * D, BF16)
        khtok = arena.alloc(2 * D, BF16)
        vtok = arena.alloc(2 * D, BF16)
        odm = kdm
        beta_dm = arena.alloc(st, F32)
        beta_dmb = arena.alloc(st, BF16)
        beta_tok = arena.alloc(2 * 16, F32)
        beta_tokb = arena.alloc(2 * 16, BF16)
        rn_dm = arena.alloc(st, F32)
        rn_dmb = arena.alloc(st, BF16)
        rn_tokb = arena.alloc(2 * 16, BF16)
        brn_dmb = arena.alloc(st, BF16)
        stt = mkstats(st)
        lntmp = sq
        xall = arena.alloc(H * CH, BF16)
        npw = [arena.alloc(1024, BF16) for _ in range(3)] + [sq[:, 0:1024]]
        npow = []
        for c4 in range(4):
            npow += [npw[c4][:, 0:512], npw[c4][:, 512:1024]]
        npkey = [f"npow{i}" for i in range(6)] + ["sq", "sq"]
        ltall = [arena.alloc(H * CH, BF16), None]
        ltall[1] = xhat
        ltkeys = ["ltall0", ("xhath0", "xhath1")]
        ukdm = ktok[0:64, 0:H * CH]
        m1sb = arena.alloc(H * 64, BF16)
        m1b = arena.alloc(H * 64, BF16)
        oasb = arena.alloc(512, BF16)
        s32 = arena.alloc(H * 64, F32)
        sb16 = arena.alloc(H * 64, BF16)
        OP("dve", lambda e: e.memset(s32[:], 0.0), writes=["s32"], tag="s_init")
        OP("dve", lambda e: e.memset(sb16[:], 0.0), writes=["sb16"], tag="sb_init")

        for s in range(ns0):
            t0 = s * st
            xk = gkeys(t0, st)
            if "stats" not in l0a_parts:
                continue
            emit_stats(t0, st, lambda dc=None, t0=t0: xres3(t0, st, dc), xk, sq, stt, 0)
            emit_apply(st, lambda t0=t0: xres3(t0, st), xk, stt,
                       xhat[:].rearrange("p (c t) -> p c t", c=NDC), "xhat",
                       lntmp[:].rearrange("p (c t) -> p c t", c=NDC), split=True)

            def proj_fm(wsb, wkey, out, okey):
                for oc in range(NDC):
                    bk = 2 + (oc % 2)
                    for kc in range(NDC):
                        OP("te", lambda e, oc=oc, kc=kc, bk=bk, wsb=wsb: e.matmul(
                            pb[bk][:, 0:st],
                            wsb[:, kc * D + oc * 128: kc * D + (oc + 1) * 128],
                            xhat[:, kc * st:(kc + 1) * st],
                            start=(kc == 0), stop=(kc == NDC - 1)),
                           reads=[wkey, f"xhath{kc // (NDC // 2)}"],
                           writes=[f"pb{bk}"], tag=f"pj_{okey}",
                           mmkey=(0, 128))
                    OP("act", lambda e, oc=oc, bk=bk, out=out: e.activation(
                        out[:, oc * st:(oc + 1) * st], pb[bk][:, 0:st], AF.Identity),
                       reads=[f"pb{bk}"], writes=[okey], tag=f"pje_{okey}")
            if "proj" not in l0a_parts:
                continue
            proj_fm(WQ, "WQ0", qdm, "qdm")
            proj_fm(WK, "WK0", kdm, "kdmov")

            def proj_tok(wsb, wkey, out, okey):
                for tt in range(st // 128):
                    for oh in range(2):
                        bk = 2 + ((tt * 2 + oh) % 2)
                        for kc in range(NDC):
                            OP("te", lambda e, tt=tt, oh=oh, kc=kc, bk=bk, wsb=wsb: e.matmul(
                                pb[bk][:, 0:512],
                                xhat[:, kc * st + tt * 128: kc * st + (tt + 1) * 128],
                                wsb[:, kc * D + oh * 512: kc * D + (oh + 1) * 512],
                                start=(kc == 0), stop=(kc == NDC - 1)),
                               reads=[wkey, f"xhath{kc // (NDC // 2)}"],
                               writes=[f"pb{bk}"], tag=f"pt_{okey}",
                               mmkey=(0, 128))
                        OP("act", lambda e, tt=tt, oh=oh, bk=bk, out=out: e.activation(
                            out[:, tt * D + oh * 512: tt * D + (oh + 1) * 512],
                            pb[bk][:, 0:512], AF.Identity),
                           reads=[f"pb{bk}"], writes=[okey], tag=f"pte_{okey}")
            proj_tok(WK, "WK0", ktok, "ktokov")
            proj_tok(WV, "WV0", vtok, "vtok")

            if "beta" not in l0a_parts:
                continue
            for kc in range(NDC):
                OP("te", lambda e, kc=kc: e.matmul(
                    pb[0][0:H, 0:st], WB[:, kc * H:(kc + 1) * H],
                    xhat[:, kc * st:(kc + 1) * st], start=(kc == 0),
                    stop=(kc == NDC - 1)),
                   reads=["WB0", f"xhath{kc // (NDC // 2)}"], writes=["pb0"],
                   tag="bdm", mmkey=(0, 128))
            OP("act", lambda e: e.activation(beta_dm[0:H, 0:st], pb[0][0:H, 0:st],
                                             AF.Exp, scale=-1.0, bias=bb0c[0:H, 1:2]),
               reads=["pb0", "bb0c"], writes=["beta_dm"], tag="bdm_e")
            OP("dve", lambda e: e.tensor_scalar_add(beta_dm[0:H, 0:st],
                                                    beta_dm[0:H, 0:st], 1.0),
               reads=["beta_dm"], writes=["beta_dm"], tag="bdm_1")
            OP("act", lambda e: e.activation(beta_dm[0:H, 0:st], beta_dm[0:H, 0:st], AF.Ln),
               reads=["beta_dm"], writes=["beta_dm"], tag="bdm_l")
            OP("act", lambda e: e.activation(beta_dm[0:H, 0:st], beta_dm[0:H, 0:st],
                                             AF.Exp, scale=-1.0),
               reads=["beta_dm"], writes=["beta_dm"], tag="bdm_x")
            OP("act", lambda e: e.activation(beta_dmb[0:H, 0:st], beta_dm[0:H, 0:st],
                                             AF.Identity),
               reads=["beta_dm"], writes=["beta_dmb"], tag="bdm_bf")
            for tt in range(st // 128):
                OP("te", lambda e, tt=tt: e.matmul(
                    pb[1][:, tt * 16:(tt + 1) * 16],
                    beta_dmb[0:H, tt * 128:(tt + 1) * 128], idn[0:H, 0:H],
                    start=True, stop=True),
                   reads=["beta_dmb", "idn"], writes=["pb1"], tag="btok_t",
                   mmkey=(0, 16))
            OP("act", lambda e: e.activation(beta_tok[:, 0:32], pb[1][:, 0:32],
                                             AF.Identity),
               reads=["pb1"], writes=["beta_tok"], tag="btok_e")
            OP("act", lambda e: e.activation(beta_tokb[:, 0:32], pb[1][:, 0:32],
                                             AF.Identity),
               reads=["pb1"], writes=["beta_tokb"], tag="btok_bf")

            if "norms" not in l0a_parts:
                continue
            OP("act", lambda e: e.activation(
                sq[:].rearrange("p (c t) -> p c t", c=NDC),
                kdm[:].rearrange("p (c t) -> p c t", c=NDC), AF.Square),
               reads=["kdmov"], writes=["sq"], tag="kn_sq")
            for dc in range(NDC):
                OP("te", lambda e, dc=dc: e.matmul(
                    pb[0][0:H, 0:st], pmap[:, dc * 16:(dc + 1) * 16],
                    sq[:, dc * st:(dc + 1) * st],
                    start=(dc == 0), stop=(dc == NDC - 1)),
                   reads=["sq", "pmap"], writes=["pb0"], tag="kn_mm", mmkey=(0, 128))
            OP("act", lambda e: e.activation(rn_dm[0:H, 0:st], pb[0][0:H, 0:st],
                                             AF.Ln, bias=cvals[0:H, 1:2]),
               reads=["pb0", "cvals"], writes=["rn_dm"], tag="kn_ln")
            OP("act", lambda e: e.activation(rn_dm[0:H, 0:st], rn_dm[0:H, 0:st],
                                             AF.Exp, scale=-0.5),
               reads=["rn_dm"], writes=["rn_dm"], tag="kn_exp")
            OP("act", lambda e: e.activation(rn_dmb[0:H, 0:st], rn_dm[0:H, 0:st],
                                             AF.Identity),
               reads=["rn_dm"], writes=["rn_dmb"], tag="kn_bf")
            OP("dve", lambda e: e.tensor_mul(brn_dmb[0:H, 0:st], rn_dmb[0:H, 0:st],
                                             beta_dmb[0:H, 0:st]),
               reads=["rn_dmb", "beta_dmb"], writes=["brn_dmb"], tag="kn_brn")
            for tt in range(st // 128):
                OP("te", lambda e, tt=tt: e.matmul(
                    pb[1][:, tt * 16:(tt + 1) * 16],
                    rn_dmb[0:H, tt * 128:(tt + 1) * 128], idn[0:H, 0:H],
                    start=True, stop=True),
                   reads=["rn_dmb", "idn"], writes=["pb1"], tag="rnt_t", mmkey=(0, 16))
            OP("act", lambda e: e.activation(
                rn_tokb[:, 0:32], pb[1][:, 0:32], AF.Identity),
               reads=["pb1"], writes=["rn_tokb"], tag="rnt_e")
            for dc in range(NDC):
                OP("te", lambda e, dc=dc: e.matmul(
                    pb[0][:, 0:st], hmap[:, dc * 128:(dc + 1) * 128], rn_dmb[0:H, 0:st],
                    start=True, stop=True),
                   reads=["rn_dmb", "hmap"], writes=["pb0"], tag="khdm_bc",
                   mmkey=(0, 16))
                OP("dve", lambda e, dc=dc: e.tensor_tensor(
                    khdm[:, dc * st:(dc + 1) * st], kdm[:, dc * st:(dc + 1) * st],
                    pb[0][:, 0:st], ALU.mult),
                   reads=["kdmov", "pb0"], writes=["khdm"], tag="khdm_tt")
                OP("te", lambda e, dc=dc: e.matmul(
                    pb[1][:, 0:st], hmap[:, dc * 128:(dc + 1) * 128], brn_dmb[0:H, 0:st],
                    start=True, stop=True),
                   reads=["brn_dmb", "hmap"], writes=["pb1"], tag="kbdm_bc",
                   mmkey=(0, 16))
                OP("dve", lambda e, dc=dc: e.tensor_tensor(
                    kbdm[:, dc * st:(dc + 1) * st], kdm[:, dc * st:(dc + 1) * st],
                    pb[1][:, 0:st], ALU.mult),
                   reads=["kdmov", "pb1"], writes=["kbdm"], tag="kbdm_tt")
            for tt in range(st // 128):
                rnb = rn_tokb[:, tt * 16:(tt + 1) * 16].rearrange(
                    "p (h x) -> p h x", x=1).broadcast_to([128, 16, 64])
                OP("dve", lambda e, tt=tt, rnb=rnb: e.tensor_tensor(
                    khtok[:, tt * D:(tt + 1) * D].rearrange("p (h d) -> p h d", d=64),
                    ktok[:, tt * D:(tt + 1) * D].rearrange("p (h d) -> p h d", d=64),
                    rnb, ALU.mult),
                   reads=["ktokov", "rn_tokb"], writes=["khtok"], tag="khtok_tt")

            if "chunks" not in l0a_parts:
                continue
            FULLCP = ("xinit", "grams", "solve", "uk", "m1", "o", "ds")
            nch = min(st // CH, nchunks)
            if tuple(chunk_parts) == FULLCP and nch == 2:
                def ec(ci, cp):
                    emit_chunk(s, ci, qdm, kbdm, khdm, khtok, vtok, odm,
                               beta_tokb, xall, npow, npkey, npw, ltall[ci % 2],
                               ltkeys[ci % 2], ukdm, m1sb, m1b, oasb, s32,
                               sb16, st, cp)
                ec(0, ("xinit", "grams", "solve", "uk", "m1"))
                ec(1, ("xinit", "grams", "solve"))
                ec(0, ("o", "ds"))
                ec(1, ("uk", "m1"))
                ec(1, ("o", "ds"))
            else:
                for ci in range(nch):
                    emit_chunk(s, ci, qdm, kbdm, khdm, khtok, vtok, odm,
                               beta_tokb, xall, npow, npkey, npw, ltall[ci % 2],
                               ltkeys[ci % 2], ukdm, m1sb, m1b, oasb, s32,
                               sb16, st, chunk_parts)

            if "wout" not in l0a_parts:
                continue
            for oc in range(NDC):
                bk = 2 + (oc % 2)
                for kc in range(NDC):
                    OP("te", lambda e, oc=oc, kc=kc, bk=bk: e.matmul(
                        pb[bk][:, 0:st],
                        WO[:, kc * D + oc * 128: kc * D + (oc + 1) * 128],
                        odm[:, kc * st:(kc + 1) * st],
                        start=(kc == 0), stop=(kc == NDC - 1)),
                       reads=["WO0", "kdmov"], writes=[f"pb{bk}"], tag="wo",
                       mmkey=(0, 128))
                OP("dve", lambda e, oc=oc, bk=bk, t0=t0: e.tensor_tensor(
                    xres[:, oc * T + t0: oc * T + t0 + st],
                    xres[:, oc * T + t0: oc * T + t0 + st],
                    pb[bk][:, 0:st], ALU.add),
                   reads=[f"pb{bk}"] + xk, writes=xk, tag="res0")
        arena.pop()

    def emit_chunk(s, ci, qdm, kbdm, khdm, khtok, vtok, odm, beta_tokb,
                   xall, npow, npkey, npw, ltc, ltkey, ukdm, m1sb, m1b, oasb, s32, sb16, st,
                   cp=("xinit", "grams", "solve", "uk", "m1", "o", "ds")):
        tc = ci * CH
        if "xinit" in cp:
          OP("act", lambda e: e.activation(
            xall[:].rearrange("p (h w) -> p h w", h=H)[:, :, 0:64],
            vtok[:, ci * D:(ci + 1) * D].rearrange("p (h d) -> p h d", d=64),
            AF.Identity),
             reads=["vtok"], writes=["xall"], tag=f"xiV{s}_{ci}")
          OP("act", lambda e: e.activation(
            xall[:].rearrange("p (h w) -> p h w", h=H)[:, :, 64:128],
            khtok[:, ci * D:(ci + 1) * D].rearrange("p (h d) -> p h d", d=64),
            AF.Identity),
             reads=["khtok"], writes=["xall"], tag=f"xiK{s}_{ci}")
        chains = []
        for par in range(2):
            groups = [[par + 2 * i for i in range(4)],
                      [8 + par + 2 * i for i in range(4)]]
            for gg, hs in enumerate(groups):
                chains.append((2 * par + gg, 64 * par, hs))
        if "grams" in cp:
            for cid, b0, hs in chains:
                for q, h in enumerate(hs):
                    dc = h // 2
                    OP("te", lambda e, q=q, dc=dc, b0=b0: e.matmul(
                        pb[6][:, q * 128:(q + 1) * 128],
                        kbdm[b0:b0 + 64, dc * st + tc: dc * st + tc + CH],
                        khdm[b0:b0 + 64, dc * st + tc: dc * st + tc + CH],
                        start=True, stop=True),
                       reads=["kbdm", "khdm"], writes=["pb6"], tag="grA",
                       mmkey=(b0, 64))
                for q, h in enumerate(hs):
                    dc = h // 2
                    OP("te", lambda e, q=q, dc=dc, b0=b0: e.matmul(
                        pb[7][:, q * 128:(q + 1) * 128],
                        khdm[b0:b0 + 64, dc * st + tc: dc * st + tc + CH],
                        kbdm[b0:b0 + 64, dc * st + tc: dc * st + tc + CH],
                        start=True, stop=True),
                       reads=["kbdm", "khdm"], writes=["pb7"], tag="grAT",
                       mmkey=(b0, 64))
                msul = maskSUL[:].rearrange("p (c u t) -> p c u t", c=2,
                                            u=1).broadcast_to([128, 2, 4, 128])
                OP("dve", lambda e, cid=cid, msul=msul: e.tensor_tensor(
                    npw[cid][:].rearrange("p (c q t) -> p c q t", c=2, q=4),
                    pbw[3][:, 0:1024].rearrange("p (c q t) -> p c q t", c=2, q=4),
                    msul, ALU.mult),
                   reads=["pb6", "pb7", "maskSU", "maskSL"],
                   writes=[npkey[2 * cid], npkey[2 * cid + 1]], tag="grNTe")
                for q, h in enumerate(hs):
                    dc = h // 2
                    OP("te", lambda e, q=q, dc=dc, b0=b0: e.matmul(
                        pb[3][:, q * 128:(q + 1) * 128],
                        kbdm[b0:b0 + 64, dc * st + tc: dc * st + tc + CH],
                        qdm[b0:b0 + 64, dc * st + tc: dc * st + tc + CH],
                        start=True, stop=True),
                       reads=["kbdm", "qdm"], writes=["pb3"], tag="grL",
                       mmkey=(b0, 64))
                mbi = maskIU[:].rearrange("p (c t) -> p c t", c=1).broadcast_to(
                    [128, 4, 128])
                j0, po = hs[0] // 2, (hs[0] % 2) * 128
                OP("dve", lambda e, j0=j0, po=po, mbi=mbi: e.tensor_tensor(
                    ltc[:].rearrange("p (hp x) -> p hp x", x=256)[
                        :, j0:j0 + 4, po:po + 128],
                    pb[3][:, 0:512].rearrange("p (q t) -> p q t", q=4), mbi, ALU.mult),
                   reads=["pb3", "maskIU"],
                   writes=list(ltkey) if isinstance(ltkey, tuple) else [ltkey],
                   tag="grLe")
        if "solve" in cp:
            for k in range(NLEV):
                for cid, b0, hs in chains:
                    h0 = hs[0]
                    cur, ckey = npow[2 * cid], npkey[2 * cid]
                    curT, ckeyT = npow[2 * cid + 1], npkey[2 * cid + 1]
                    ab = 4 if cid % 2 == 0 else 5
                    nb, tb = (6, 7) if cid % 2 == 0 else (0, 1)
                    sqw = pbw[3] if cid % 2 == 0 else pbw[0]
                    for q, h in enumerate(hs):
                        xsl = xall[:, h * 128:(h + 1) * 128]
                        OP("te", lambda e, q=q, xsl=xsl, cur=cur, ab=ab: e.matmul(
                            pb[ab][:, q * 128:(q + 1) * 128],
                            cur[:, q * 128:(q + 1) * 128], xsl,
                            start=True, stop=True),
                           reads=["xall", ckey], writes=[f"pb{ab}"], tag="apN",
                           mmkey=(0, 128))
                    j0, po = h0 // 2, (h0 % 2) * 128
                    xsl4 = xall[:].rearrange("p (hp x) -> p hp x", x=256)[
                        :, j0:j0 + 4, po:po + 128]
                    OP("dve", lambda e, xsl4=xsl4, ab=ab: e.tensor_tensor(
                        xsl4, xsl4,
                        pb[ab][:, 0:512].rearrange("p (q t) -> p q t", q=4), ALU.add),
                       reads=[f"pb{ab}", "xall"], writes=["xall"], tag="apE")
                    if k < NLEV - 1:
                        lastsq = (k == NLEV - 2)
                        for q in range(4):
                            OP("te", lambda e, q=q, cur=cur, curT=curT, nb=nb:
                               e.matmul(
                                pb[nb][:, q * 128:(q + 1) * 128],
                                curT[:, q * 128:(q + 1) * 128],
                                cur[:, q * 128:(q + 1) * 128], start=True, stop=True),
                               reads=[ckey, ckeyT], writes=[f"pb{nb}"], tag="sqN",
                               mmkey=(0, 128))
                            if not lastsq:
                                OP("te", lambda e, q=q, cur=cur, curT=curT, tb=tb:
                                   e.matmul(
                                    pb[tb][:, q * 128:(q + 1) * 128],
                                    cur[:, q * 128:(q + 1) * 128],
                                    curT[:, q * 128:(q + 1) * 128],
                                    start=True, stop=True),
                                   reads=[ckey, ckeyT], writes=[f"pb{tb}"],
                                   tag="sqT", mmkey=(0, 128))
                        eng = "dve" if cid == 0 else "act"
                        nq = 4 if lastsq else 8
                        dst = (npow[2 * cid] if lastsq else npw[cid])
                        wk = [ckey] if lastsq else [ckey, ckeyT]
                        rk = [f"pb{nb}"] if lastsq else [f"pb{nb}", f"pb{tb}"]
                        src = (pb[nb] if lastsq else sqw[:, 0:1024])
                        if eng == "dve":
                            OP("dve", lambda e, dst=dst, src=src, nq=nq:
                               e.tensor_copy(
                                dst[:].rearrange("p (q t) -> p q t", q=nq),
                                src[:].rearrange("p (q t) -> p q t", q=nq)),
                               reads=rk, writes=wk, tag="sqNTe")
                        else:
                            OP("act", lambda e, dst=dst, src=src, nq=nq:
                               e.activation(
                                dst[:].rearrange("p (q t) -> p q t", q=nq),
                                src[:].rearrange("p (q t) -> p q t", q=nq),
                                AF.Identity),
                               reads=rk, writes=wk, tag="sqNTe")
        if "uk" in cp:
          for h in range(H):
            q = h % 4
            OP("te", lambda e, h=h, q=q: e.matmul(
                pb[3][0:64, q * 128:(q + 1) * 128],
                xall[:, h * 128 + 64:(h + 1) * 128], idn[:],
                start=True, stop=True),
               reads=["xall", "idn"], writes=["pb3"], tag="ukT", mmkey=(0, 128))
            if q == 3:
                OP("act", lambda e, h=h: e.activation(
                    ukdm[:, (h - 3) * 128:(h + 1) * 128],
                    pb[3][0:64, 0:512], AF.Identity),
                   reads=["pb3"], writes=["ktokov"], tag="ukTe")
        if "m1" in cp:
          for w in range(2):
            for i in range(8):
                h = w * 8 + i
                OP("te", lambda e, h=h, i=i: e.matmul(
                    pb[2][:, i * 64:(i + 1) * 64],
                    ukdm[:, h * 128:(h + 1) * 128],
                    sb16[0:64, h * 64:(h + 1) * 64],
                    start=True, stop=True),
                   reads=["ktokov", "sb16"], writes=["pb2"], tag="m1", mmkey=(0, 64))
            vsl = xall[:].rearrange("p (h w) -> p h w", h=H)[:, w * 8:(w + 1) * 8, 0:64]
            OP("dve", lambda e, w=w, vsl=vsl: e.tensor_tensor(
                m1sb[:, w * 512:(w + 1) * 512].rearrange("p (h v) -> p h v", h=8),
                vsl, pb[2][:, 0:512].rearrange("p (h v) -> p h v", h=8), ALU.subtract),
               reads=["pb2", "xall"], writes=["m1sb"], tag="m1e")
          bb = beta_tokb[:, ci * 16:(ci + 1) * 16].rearrange(
              "p (h x) -> p h x", x=1).broadcast_to([128, 16, 64])
          OP("dve", lambda e, bb=bb: e.tensor_tensor(
              m1b[:].rearrange("p (h v) -> p h v", h=H),
              m1sb[:].rearrange("p (h v) -> p h v", h=H), bb, ALU.mult),
             reads=["m1sb", "beta_tokb"], writes=["m1b"], tag="m1b")
        if "o" not in cp:
            return
        for w in range(2):
            for par in range(2):
                b0 = 64 * par
                for pr in range(4):
                    h = w * 8 + 2 * pr + par
                    OP("te", lambda e, h=h, b0=b0, pr=pr, dc=h // 2: e.matmul(
                        pb[3][b0:b0 + 64, pr * 128:(pr + 1) * 128],
                        sb16[b0:b0 + 64, h * 64:(h + 1) * 64],
                        qdm[b0:b0 + 64, dc * st + tc: dc * st + tc + CH],
                        start=True, stop=True, tile_position=(b0, b0)),
                       reads=["sb16", "qdm"], writes=["pb3"], tag="oq",
                       mmkey=(b0, 64))
            for i in range(8):
                h = w * 8 + i
                b0 = 64 * (h % 2)
                pr = i // 2
                OP("te", lambda e, h=h, b0=b0, pr=pr: e.matmul(
                    pb[2][b0:b0 + 64, pr * 128:(pr + 1) * 128],
                    m1sb[:, h * 64:(h + 1) * 64],
                    ltc[:, h * 128:(h + 1) * 128],
                    start=True, stop=True, tile_position=(0, b0)),
                   reads=["m1sb", ltkey], writes=["pb2"], tag="ol", mmkey=(0, 128))
            OP("act", lambda e: e.activation(oasb[:], pb[3][:, 0:512], AF.Identity),
               reads=["pb3"], writes=["oasb"], tag="oae")
            oout = odm[:].rearrange("p (c t) -> p c t", c=NDC)[
                :, w * 4:(w + 1) * 4, tc:tc + CH]
            OP("dve", lambda e, oout=oout: e.tensor_tensor(
                oout, pb[2][:, 0:512].rearrange("p (q t) -> p q t", q=4),
                oasb[:].rearrange("p (q t) -> p q t", q=4), ALU.add),
               reads=["pb2", "oasb"], writes=["kdmov"], tag="oe")
        if "ds" in cp:
          for w in range(2):
            for i in range(8):
                h = w * 8 + i
                for dup in range(2):
                    OP("te", lambda e, h=h, i=i, dup=dup: e.matmul(
                        pb[3][64 * dup:64 * dup + 64, i * 64:(i + 1) * 64],
                        khtok[:, ci * D + h * 64: ci * D + (h + 1) * 64],
                        m1b[:, h * 64:(h + 1) * 64],
                        start=True, stop=True, tile_position=(0, 64 * dup)),
                       reads=["khtok", "m1b"], writes=["pb3"], tag="ds",
                       mmkey=(0, 128))
            OP("dve", lambda e, w=w: e.tensor_tensor(
                s32[:, w * 512:(w + 1) * 512], s32[:, w * 512:(w + 1) * 512],
                pb[3][:, 0:512], ALU.add),
               reads=["pb3", "s32"], writes=["s32"], tag="sadd")
            OP("act", lambda e, w=w: e.activation(
                sb16[:, w * 512:(w + 1) * 512], s32[:, w * 512:(w + 1) * 512],
                AF.Identity),
               reads=["s32"], writes=["sb16"], tag="sbf")

    def emit_l0f():
        arena.mark()
        W8 = arena.alloc(NDC * D, FP8)
        nkc = NDC
        OP("gp", lambda g: g.dma_start(
            out=W8[:].rearrange("p (c o) -> p c o", c=nkc),
            in_=wf8.rearrange("(c p) o -> p c o", p=128)),
           writes=["W8"], dma=True, tag="ld_W8")
        st = STF
        xq = arena.alloc(NDC * st, FP8)
        sq = arena.alloc(NDC * st, BF16)
        lntmp = arena.alloc(NDC * st, BF16)
        stt = mkstats(st)
        for s in range(T // st):
            t0 = s * st
            xk = gkeys(t0, st)
            emit_stats(t0, st, lambda dc=None, t0=t0: xres3(t0, st, dc), xk, sq, stt, 0)
            emit_apply(st, lambda t0=t0: xres3(t0, st), xk, stt,
                       xq[:].rearrange("p (c t) -> p c t", c=NDC), "xq",
                       lntmp[:].rearrange("p (c t) -> p c t", c=NDC), split=True)
            for oc in range(NDC):
                bk = 2 + (oc % 2)
                for kc in range(NDC):
                    OP("te", lambda e, oc=oc, kc=kc, bk=bk: e.matmul(
                        pb[bk][:, 0:st],
                        W8[:, kc * D + oc * 128: kc * D + (oc + 1) * 128],
                        xq[:, kc * st:(kc + 1) * st],
                        start=(kc == 0), stop=(kc == NDC - 1)),
                       reads=["W8", f"xqh{kc // (NDC // 2)}"],
                       writes=[f"pb{bk}"], tag="f8",
                       mmkey=(0, 128, "f8"))
                OP("dve", lambda e, oc=oc, bk=bk, t0=t0: e.scalar_tensor_tensor(
                    xres[:, oc * T + t0: oc * T + t0 + st],
                    pb[bk][:, 0:st], pf["bf0"][:, oc:oc + 1],
                    xres[:, oc * T + t0: oc * T + t0 + st], ALU.add, ALU.add),
                   reads=[f"pb{bk}", "pf_bf0"] + xk, writes=xk, tag="res1")
        arena.pop()

    def _lp(fn):
        def g(e):
            with nc.allow_low_precision(reason="bf16 reduce (prototype-validated)"):
                return fn(e)
        return g

    def _stat_math(st, stt, eps, l1a_eps=None):
        mu, va, tmp = stt["mu"], stt["va"], stt["tmp"]
        arow, srow = stt["arow"], stt["srow"]
        arow_bf, srow_bf = stt["arow_bf"], stt["srow_bf"]
        KMU = stt.get("k_mu", "murow")
        KVA = stt.get("k_va", "varow")
        KTMP = stt.get("k_tmp", "tmprow")
        KSR = stt.get("k_srow", "srow")
        KABF = stt.get("k_abf", "arow_bf")
        KSBF = stt.get("k_sbf", "srow_bf")
        OP("dve", lambda e: e.tensor_scalar_mul(mu[0:1, 0:st], pb[0][0:1, 0:st], 1.0 / D),
           reads=["pb0"], writes=[KMU], tag="st_mu")
        OP("dve", lambda e: e.tensor_scalar_mul(va[0:1, 0:st], pb[1][0:1, 0:st], 1.0 / D),
           reads=["pb1"], writes=[KVA], tag="st_ex2")
        OP("dve", lambda e: e.scalar_tensor_tensor(
            tmp[0:1, 0:st], mu[0:1, 0:st], -1.0, mu[0:1, 0:st], ALU.mult, ALU.mult),
           reads=[KMU], writes=[KTMP], tag="st_msq")
        OP("dve", lambda e: e.tensor_add(va[0:1, 0:st], va[0:1, 0:st], tmp[0:1, 0:st]),
           reads=[KVA, KTMP], writes=[KVA], tag="st_var")
        OP("act", lambda e: e.activation(tmp[0:1, 0:st], va[0:1, 0:st], AF.Ln,
                                         bias=cvals[0:1, eps:eps + 1]),
           reads=[KVA, "cvals"], writes=[KTMP], tag="st_ln")
        OP("act", lambda e: e.activation(arow[0:1, 0:st], tmp[0:1, 0:st], AF.Exp,
                                         scale=-0.5),
           reads=[KTMP], writes=["arow"], tag="st_rstd")
        if l1a_eps is not None:
            OP("dve", lambda e: e.scalar_tensor_tensor(
                tmp[0:1, 0:st], arow[0:1, 0:st], 1.0, arow[0:1, 0:st],
                ALU.mult, ALU.mult),
               reads=["arow"], writes=[KTMP], tag="dln_r2")
            OP("dve", lambda e: e.tensor_mul(tmp[0:1, 0:st], tmp[0:1, 0:st],
                                             va[0:1, 0:st]),
               reads=[KTMP, KVA], writes=[KTMP], tag="dln_w")
            OP("act", lambda e: e.activation(tmp[0:1, 0:st], tmp[0:1, 0:st], AF.Ln,
                                             bias=cvals[0:1, l1a_eps:l1a_eps + 1]),
               reads=[KTMP, "cvals"], writes=[KTMP], tag="dln_ln")
            OP("act", lambda e: e.activation(tmp[0:1, 0:st], tmp[0:1, 0:st], AF.Exp,
                                             scale=-0.5),
               reads=[KTMP], writes=[KTMP], tag="dln_u")
            OP("dve", lambda e: e.tensor_mul(arow[0:1, 0:st], arow[0:1, 0:st],
                                             tmp[0:1, 0:st]),
               reads=["arow", KTMP], writes=["arow"], tag="dln_a")
        OP("dve", lambda e: e.scalar_tensor_tensor(
            srow[0:1, 0:st], mu[0:1, 0:st], -1.0, arow[0:1, 0:st], ALU.mult, ALU.mult),
           reads=[KMU, "arow"], writes=[KSR], tag="st_srow")
        OP("act", lambda e: e.activation(arow_bf[0:1, 0:st], arow[0:1, 0:st],
                                         AF.Identity),
           reads=["arow"], writes=[KABF], tag="st_abf")
        OP("act", lambda e: e.activation(srow_bf[0:1, 0:st], srow[0:1, 0:st],
                                         AF.Identity),
           reads=[KSR], writes=[KSBF], tag="st_sbf")

    def _bcg(st, stt, gname, bname, pitch=None):
        if pitch is None:
            pitch = st
        arow_bf, srow_bf = stt["arow_bf"], stt["srow_bf"]
        KABF = stt.get("k_abf", "arow_bf")
        KSBF = stt.get("k_sbf", "srow_bf")
        absc, sbsc = stt["absc"], stt["sbsc"]
        grow, brow = prow[gname], prow[bname]
        for dc in range(NDC):
            ba, bs = (0, 1) if dc % 2 == 0 else (2, 3)
            OP("te", lambda e, dc=dc, grow=grow, ba=ba: e.matmul(
                pb[ba][:, 0:st], grow[0:1, dc * 128:(dc + 1) * 128],
                arow_bf[0:1, 0:st], start=True, stop=True),
               reads=[KABF, "pr_" + gname], writes=[f"pb{ba}"], tag="bcg_a",
               mmkey=(0, 1))
            OP("act", lambda e, dc=dc, ba=ba: e.activation(
                absc[:, dc * pitch:dc * pitch + st], pb[ba][:, 0:st], AF.Identity),
               reads=[f"pb{ba}"], writes=["absc"], tag="bcg_ae")
            OP("te", lambda e, dc=dc, grow=grow, bs=bs: e.matmul(
                pb[bs][:, 0:st], grow[0:1, dc * 128:(dc + 1) * 128],
                srow_bf[0:1, 0:st], start=True, stop=False),
               reads=[KSBF, "pr_" + gname], writes=[f"pb{bs}"], tag="bcg_s",
               mmkey=(0, 1))
            OP("te", lambda e, dc=dc, brow=brow, bs=bs: e.matmul(
                pb[bs][:, 0:st], brow[0:1, dc * 128:(dc + 1) * 128],
                onesrow[0:1, 0:st], start=False, stop=True),
               reads=["onesrow", "pr_" + bname], writes=[f"pb{bs}"], tag="bcg_b",
               mmkey=(0, 1))
            OP("act", lambda e, dc=dc, bs=bs: e.activation(
                sbsc[:, dc * pitch:dc * pitch + st], pb[bs][:, 0:st], AF.Identity),
               reads=[f"pb{bs}"], writes=["sbsc"], tag="bcg_se")

    def emit_l1a():
        arena.mark()
        WQ = arena.alloc(NDC * D, BF16); load_weight(WQ, wq1, "WQ1")
        WK = arena.alloc(NDC * D, BF16); load_weight(WK, wk1, "WK1")
        WV = arena.alloc(NDC * D, BF16); load_weight(WV, wv1, "WV1")
        WFB = arena.alloc(2 * NDC * D, BF16); load_weight(WFB, wfb, "WFB", kin=2 * D)
        WO = arena.alloc(NDC * D, BF16); load_weight(WO, wo1, "WO1")
        for nm in ("g1", "b1", "g2", "b2"):
            prow[nm] = arena.alloc(D, BF16)
            OP("gp", lambda g, nm=nm: g.dma_start(
                out=prow[nm][0:1, :], in_=pfsrc[nm].rearrange("(x d) -> x d", x=1)),
               writes=["pr_" + nm], dma=True)
        hmapR = arena.alloc(NDC * 128, BF16)
        OP("dve", lambda e: e.memset(hmapR[:], 0.0), writes=["hmapR"], tag="c_hmapR0")
        OP("dve", lambda e: e.memset(hmapR[64:80, :], 1.0), writes=["hmapR"],
           tag="c_hmapR1")
        for dc in range(NDC):
            for half in range(2):
                sl = hmapR[64:80, dc * 128 + half * 64: dc * 128 + (half + 1) * 64]
                OP("gp", lambda g, sl=sl, dc=dc, half=half: g.affine_select(
                    out=sl, in_=sl, compare_op=ALU.is_equal, fill=0.0,
                    base=-(2 * dc + half), pattern=[[0, 64]], channel_multiplier=1),
                   reads=["hmapR"], writes=["hmapR"], tag="c_hmapR")
        st = ST1
        W = st + 4
        xw = arena.alloc(NDC * W, BF16)
        vw = arena.alloc(NDC * W, BF16)
        sq = vw
        lntmp = vw
        qw = arena.alloc(NDC * W, BF16)
        kw = arena.alloc(NDC * W, BF16)
        outf, outr, zt = qw, kw, qw
        scsb = arena.alloc(3 * st, F32)
        tmpc = scsb
        stt = {}
        stt["mu"] = arena.alloc(W, F32)
        stt["va"] = arena.alloc(W, F32)
        stt["tmp"] = arena.alloc(W, F32)
        stt["arow"] = arena.alloc(W, F32)
        stt["srow"] = stt["va"]
        mubits = stt["mu"][:].bitcast(BF16)
        stt["arow_bf"] = mubits[:, 0:W]
        stt["srow_bf"] = mubits[:, W:2 * W]
        stt["k_srow"] = "varow"
        stt["k_abf"] = "murow"
        stt["k_sbf"] = "murow"
        stt["absc"] = arena.alloc(NDC * W, BF16)
        stt["sbsc"] = arena.alloc(NDC * W, BF16)
        attnb = stt["sbsc"][:, 0:3 * st]
        mx = stt["tmp"]
        prods4 = stt["arow"][:, 0:st].bitcast(BF16)

        KXW = "vw"
        KQ = "qw"
        KK_ = "kw"
        KSC = "scsb"
        KMX = "tmprow"

        def xw3():
            return xw[:].rearrange("p (c t) -> p c t", c=NDC)

        for s in range(T // st):
            t0 = s * st
            first, last = (s == 0), (s == T // st - 1)
            rd0 = max(t0 - 2, 0)
            rd1 = min(t0 + st + 2, T)
            dst0 = rd0 - (t0 - 2)
            rlen = rd1 - rd0
            xk = gkeys(rd0, rlen)

            def srcw(dc=None, rd0=rd0, rlen=rlen):
                if dc is None:
                    return xres[:].rearrange("p (c t) -> p c t", c=NDC)[
                        :, :, rd0:rd0 + rlen]
                return xres[:, dc * T + rd0:dc * T + rd0 + rlen]

            if first:
                OP("dve", lambda e: e.memset(xw3()[:, :, 0:2], 0.0),
                   writes=["xw"], tag="xwpadL")
            if last:
                OP("dve", lambda e: e.memset(xw3()[:, :, W - 2:W], 0.0),
                   writes=["xw"], tag="xwpadR")
            OP("act", lambda e, rlen=rlen, srcw=srcw: e.activation(
                sq[:].rearrange("p (c t) -> p c t", c=NDC)[:, :, 0:rlen],
                srcw(), AF.Square),
               reads=xk, writes=[KXW], tag="l1a_sq")
            for dc in range(NDC):
                OP("te", lambda e, dc=dc, rlen=rlen, srcw=srcw: e.matmul(
                    pb[0][0:1, 0:rlen], onesc128[:], srcw(dc),
                    start=(dc == 0), stop=(dc == NDC - 1)),
                   reads=xk + ["onesc128"], writes=["pb0"], tag="l1a_s1",
                   mmkey=(0, 128))
            for dc in range(NDC):
                OP("te", lambda e, dc=dc, rlen=rlen: e.matmul(
                    pb[1][0:1, 0:rlen], onesc128[:], sq[:, dc * W:dc * W + rlen],
                    start=(dc == 0), stop=(dc == NDC - 1)),
                   reads=[KXW, "onesc128"], writes=["pb1"], tag="l1a_s2",
                   mmkey=(0, 128))
            _stat_math(rlen, stt, 0, l1a_eps=2)
            _bcg(rlen, stt, "g1", "b1", pitch=W)
            a_b = stt["absc"][:].rearrange("p (c t) -> p c t", c=NDC)[:, :, 0:rlen]
            s_b = stt["sbsc"][:].rearrange("p (c t) -> p c t", c=NDC)[:, :, 0:rlen]
            OP("dve", lambda e, rlen=rlen, srcw=srcw, a_b=a_b: e.tensor_tensor(
                lntmp[:].rearrange("p (c t) -> p c t", c=NDC)[:, :, 0:rlen],
                srcw(), a_b, ALU.mult),
               reads=xk + ["absc"], writes=[KXW], tag="l1a_ap1")
            OP("dve", lambda e, rlen=rlen, dst0=dst0, s_b=s_b: e.tensor_tensor(
                xw3()[:, :, dst0:dst0 + rlen],
                lntmp[:].rearrange("p (c t) -> p c t", c=NDC)[:, :, 0:rlen],
                s_b, ALU.add),
               reads=[KXW, "sbsc"], writes=["xw"], tag="l1a_ap2")

            def proj_w(wsb, wkey, out, okey, bias):
                for dc in range(NDC):
                    bank = 1 + (dc % 2)
                    for kc in range(NDC):
                        OP("te", lambda e, kc=kc, dc=dc, wsb=wsb, bank=bank: e.matmul(
                            pb[bank][:, 0:W],
                            wsb[:, kc * D + dc * 128: kc * D + (dc + 1) * 128],
                            xw[:, kc * W:(kc + 1) * W],
                            start=(kc == 0), stop=(kc == NDC - 1)),
                           reads=[wkey, "xw"], writes=[f"pb{bank}"], tag="l1a_p",
                           mmkey=(0, 128))
                    OP("act", lambda e, dc=dc, out=out, bias=bias, bank=bank:
                       e.activation(
                        out[:, dc * W:(dc + 1) * W], pb[bank][:, 0:W], AF.Identity,
                        bias=pf[bias][:, dc:dc + 1]),
                       reads=[f"pb{bank}", "pf_" + bias], writes=[okey], tag="l1a_pe")

            proj_w(WQ, "WQ1", qw, KQ, "bq1")
            proj_w(WK, "WK1", kw, KK_, "bk1")
            for tname, tkey in ((qw, KQ), (kw, KK_)):
                if first:
                    OP("dve", lambda e, tname=tname: e.memset(
                        tname[:].rearrange("p (c t) -> p c t", c=NDC)[:, :, 0:2], 0.0),
                       reads=[tkey], writes=[tkey], tag="padL")
                if last:
                    OP("dve", lambda e, tname=tname: e.memset(
                        tname[:].rearrange("p (c t) -> p c t", c=NDC)[:, :, W - 2:W],
                        0.0),
                       reads=[tkey], writes=[tkey], tag="padR")

            pidx = 0
            for dc in range(NDC):
                for di, sgn in ((0, 1), (1, -1)):
                    for j in range(3):
                        qs = qw[:, dc * W + 2 - sgn:dc * W + 2 - sgn + st]
                        ks = kw[:, dc * W + 2 - sgn * j:dc * W + 2 - sgn * j + st]
                        sl = pidx % 2
                        pidx += 1
                        psl = prods4[:, sl * st:(sl + 1) * st]
                        OP("dve", lambda e, qs=qs, ks=ks, psl=psl: e.tensor_tensor(
                            psl, qs, ks, ALU.mult),
                           reads=[KQ, KK_], writes=[f"pr{sl}"], tag="l1a_prod")
                        bank = (2 + j) if di == 0 else (5 + j)
                        tp = (0, 0) if di == 0 else (0, 64)
                        r0 = 0 if di == 0 else 64
                        OP("te", lambda e, dc=dc, bank=bank, tp=tp, r0=r0,
                           psl=psl: e.matmul(
                            pb[bank][r0:r0 + 16, 0:st],
                            pmap[:, dc * 16:(dc + 1) * 16], psl,
                            start=(dc == 0), stop=(dc == NDC - 1),
                            tile_position=tp),
                           reads=[f"pr{sl}", "pmap"], writes=[f"pb{bank}"],
                           tag="l1a_sc", mmkey=(0, 128))
            SC = 1.0 / 8.0
            if s == 0:
                OP("dve", lambda e: e.memset(scsb[0:64, :], 0.0),
                   writes=[KSC], tag="sc_z")
            for j in range(3):
                OP("act", lambda e, j=j: e.activation(
                    scsb[0:16, j * st:(j + 1) * st], pb[2 + j][0:16, 0:st],
                    AF.Identity, scale=SC),
                   reads=[f"pb{2 + j}"], writes=[KSC], tag="sc_eF")
                OP("act", lambda e, j=j: e.activation(
                    scsb[64:80, j * st:(j + 1) * st], pb[5 + j][64:80, 0:st],
                    AF.Identity, scale=SC),
                   reads=[f"pb{5 + j}"], writes=[KSC], tag="sc_eR")
            for dc in range(NDC):
                bank = dc % 2
                for kc in range(NDC):
                    OP("te", lambda e, kc=kc, dc=dc, bank=bank: e.matmul(
                        pb[bank][:, 0:W],
                        WV[:, kc * D + dc * 128: kc * D + (dc + 1) * 128],
                        xw[:, kc * W:(kc + 1) * W],
                        start=(kc == 0), stop=(kc == NDC - 1)),
                       reads=["WV1", "xw"], writes=[f"pb{bank}"], tag="l1a_pv",
                       mmkey=(0, 128))
                OP("act", lambda e, dc=dc, bank=bank: e.activation(
                    vw[:, dc * W:(dc + 1) * W], pb[bank][:, 0:W], AF.Identity,
                    bias=pf["bv1"][:, dc:dc + 1]),
                   reads=[f"pb{bank}", "pf_bv1"], writes=[KXW], tag="l1a_ev")
            if first:
                OP("dve", lambda e: e.memset(
                    vw[:].rearrange("p (c t) -> p c t", c=NDC)[:, :, 0:2], 0.0),
                   reads=[KXW], writes=[KXW], tag="vpadL")
            if last:
                OP("dve", lambda e: e.memset(
                    vw[:].rearrange("p (c t) -> p c t", c=NDC)[:, :, W - 2:W], 0.0),
                   reads=[KXW], writes=[KXW], tag="vpadR")

            sc3 = scsb[0:80, 0:3 * st].rearrange("p (j t) -> p t j", j=3)
            OP("dve", lambda e, sc3=sc3: e.tensor_reduce(
                mx[0:80, 0:st], sc3, mybir.AxisListType.X, ALU.max),
               reads=[KSC], writes=[KMX], tag="sm_mx")
            mxb2 = mx[0:80, 0:st].rearrange("p (t x) -> p t x", x=1).broadcast_to(
                [80, st, 3])
            OP("dve", lambda e, sc3=sc3, mxb2=mxb2: e.tensor_tensor(
                sc3, sc3, mxb2, ALU.subtract),
               reads=[KSC, KMX], writes=[KSC], tag="sm_sub")
            OP("act", lambda e: e.activation(scsb[0:80, 0:3 * st],
                                             scsb[0:80, 0:3 * st], AF.Exp),
               reads=[KSC], writes=[KSC], tag="sm_exp")
            OP("dve", lambda e, sc3=sc3: e.tensor_reduce(
                mx[0:80, 0:st], sc3, mybir.AxisListType.X, ALU.add),
               reads=[KSC], writes=[KMX], tag="sm_den")
            OP("act", lambda e: e.activation(mx[0:80, 0:st], mx[0:80, 0:st], AF.Ln),
               reads=[KMX], writes=[KMX], tag="sm_ln")
            OP("act", lambda e: e.activation(mx[0:80, 0:st], mx[0:80, 0:st],
                                             AF.Exp, scale=-1.0),
               reads=[KMX], writes=[KMX], tag="sm_rcp")
            OP("dve", lambda e, sc3=sc3, mxb2=mxb2: e.tensor_tensor(
                attnb[0:80, 0:3 * st].rearrange("p (j t) -> p t j", j=3),
                sc3, mxb2, ALU.mult),
               reads=[KSC, KMX], writes=["sbsc"], tag="sm_attn")

            for di, sgn, hm, hkey, r0, outt, okey in (
                    (0, 1, hmap, "hmap", 0, outf, KQ),
                    (1, -1, hmapR, "hmapR", 64, outr, KK_)):
                for dc in range(NDC):
                    w0 = 2 if dc % 2 == 0 else 3
                    for j in range(3):
                        bank = 2 * w0 + (0 if j < 2 else 1)
                        col = (j % 2) * st
                        OP("te", lambda e, dc=dc, j=j, hm=hm, r0=r0, bank=bank,
                           col=col: e.matmul(
                            pb[bank][:, col:col + st],
                            hm[r0:r0 + 16, dc * 128:(dc + 1) * 128],
                            attnb[r0:r0 + 16, j * st:(j + 1) * st],
                            start=True, stop=True),
                           reads=["sbsc", hkey], writes=[f"pb{bank}"],
                           tag="l1a_abc", mmkey=(r0, 16))
                    vv = vw[:, dc * W:(dc + 1) * W]
                    tsl = tmpc[:, 0:3 * st]
                    OP("dve", lambda e, vv=vv, sgn=sgn, w0=w0, tsl=tsl:
                       e.tensor_tensor(
                        tsl.rearrange("p (t j) -> p t j", j=3),
                        pbw[w0][:, 0:3 * st].rearrange("p (j t) -> p t j", j=3),
                        _shiftn(vv, sgn, 0, 3), ALU.mult),
                       reads=[f"pb{2 * w0}", f"pb{2 * w0 + 1}", KXW],
                       writes=[KSC], tag="l1a_cm")
                    OP("dve", _lp(lambda e, dc=dc, outt=outt, tsl=tsl:
                       e.tensor_reduce(
                        outt[:, dc * W + 2:dc * W + 2 + st],
                        tsl.rearrange("p (t j) -> p t j", j=3),
                        mybir.AxisListType.X, ALU.add)),
                       reads=[KSC], writes=[okey], tag="l1a_cred")

            for oc in range(NDC):
                bank = 1 + (oc % 2)
                for kc in range(2 * NDC):
                    srct = outf if kc < NDC else outr
                    srk = KQ if kc < NDC else KK_
                    kcl = kc % NDC
                    OP("te", lambda e, oc=oc, kc=kc, kcl=kcl, srct=srct, bank=bank:
                       e.matmul(
                        pb[bank][:, 0:st],
                        WFB[:, kc * D + oc * 128: kc * D + (oc + 1) * 128],
                        srct[:, kcl * W + 2:kcl * W + 2 + st],
                        start=(kc == 0), stop=(kc == 2 * NDC - 1)),
                       reads=["WFB", srk], writes=[f"pb{bank}"], tag="l1a_fb",
                       mmkey=(0, 128))
                OP("dve", lambda e, oc=oc, bank=bank: e.scalar_tensor_tensor(
                    xw[:, oc * W + 2:oc * W + 2 + st], pb[bank][:, 0:st],
                    pf["bfb"][:, oc:oc + 1],
                    xw[:, oc * W + 2:oc * W + 2 + st], ALU.add, ALU.add),
                   reads=[f"pb{bank}", "pf_bfb", "xw"], writes=["xw"], tag="l1a_mid")

            midv = xw3()[:, :, 2:2 + st]
            OP("act", lambda e, midv=midv: e.activation(
                sq[:].rearrange("p (c t) -> p c t", c=NDC)[:, :, 0:st],
                midv, AF.Square),
               reads=["xw"], writes=[KXW], tag="m_sq")
            for dc in range(NDC):
                OP("te", lambda e, dc=dc: e.matmul(
                    pb[0][0:1, 0:st], onesc128[:],
                    xw[:, dc * W + 2:dc * W + 2 + st],
                    start=(dc == 0), stop=(dc == NDC - 1)),
                   reads=["xw", "onesc128"], writes=["pb0"], tag="m_s1",
                   mmkey=(0, 128))
            for dc in range(NDC):
                OP("te", lambda e, dc=dc: e.matmul(
                    pb[1][0:1, 0:st], onesc128[:], sq[:, dc * W:dc * W + st],
                    start=(dc == 0), stop=(dc == NDC - 1)),
                   reads=[KXW, "onesc128"], writes=["pb1"], tag="m_s2",
                   mmkey=(0, 128))
            _stat_math(st, stt, 2)
            _bcg(st, stt, "g2", "b2", pitch=W)
            a_b2 = stt["absc"][:].rearrange("p (c t) -> p c t", c=NDC)[:, :, 0:st]
            s_b2 = stt["sbsc"][:].rearrange("p (c t) -> p c t", c=NDC)[:, :, 0:st]
            OP("dve", lambda e, a_b2=a_b2, midv=midv: e.tensor_tensor(
                lntmp[:].rearrange("p (c t) -> p c t", c=NDC)[:, :, 0:st],
                midv, a_b2, ALU.mult),
               reads=["xw", "absc"], writes=[KXW], tag="z_ap1")
            OP("dve", lambda e, s_b2=s_b2: e.tensor_tensor(
                zt[:].rearrange("p (c t) -> p c t", c=NDC)[:, :, 0:st],
                lntmp[:].rearrange("p (c t) -> p c t", c=NDC)[:, :, 0:st],
                s_b2, ALU.add),
               reads=[KXW, "sbsc"], writes=[KQ], tag="z_ap2")

            OP("act", lambda e: e.add_instruction(mybir.InstLoadActFuncSet(
                name=nc.get_next_instruction_name(), act_func_set_id=10,
                ins=[], outs=[])), tag="tbl10")
            for oc in range(NDC):
                bank = 1 + (oc % 2)
                for kc in range(NDC):
                    OP("te", lambda e, oc=oc, kc=kc, bank=bank: e.matmul(
                        pb[bank][:, 0:st],
                        WO[:, kc * D + oc * 128: kc * D + (oc + 1) * 128],
                        zt[:, kc * W:kc * W + st],
                        start=(kc == 0), stop=(kc == NDC - 1)),
                       reads=["WO1", KQ], writes=[f"pb{bank}"], tag="l1a_wo",
                       mmkey=(0, 128))
                OP("act", lambda e, oc=oc, bank=bank: e.activation(
                    lntmp[:, oc * W:oc * W + st], pb[bank][:, 0:st], AF.Gelu,
                    bias=pf["bo1"][:, oc:oc + 1]),
                   reads=[f"pb{bank}", "pf_bo1"], writes=[KXW], tag="l1a_gel")
            OP("act", lambda e: e.add_instruction(mybir.InstLoadActFuncSet(
                name=nc.get_next_instruction_name(), act_func_set_id=6,
                ins=[], outs=[])), tag="tbl6b")
            for oc in range(NDC):
                OP("dve", lambda e, oc=oc, t0=t0: e.tensor_tensor(
                    xres[:, oc * T + t0: oc * T + t0 + st],
                    xres[:, oc * T + t0: oc * T + t0 + st],
                    lntmp[:, oc * W:oc * W + st], ALU.add),
                   reads=[KXW] + xk, writes=xk, tag="l1a_res")
        arena.pop()

    def _shiftn(vv, sgn, j0, nj):
        base = vv[:, 2 - sgn * j0:2 - sgn * j0 + ST1]
        return bass.AP(base.tensor, base.offset,
                       [list(base.ap[0]), [1, ST1], [-sgn, nj]])

    def emit_l1f():
        arena.mark()
        WR = arena.alloc(NDC * D, BF16); load_weight(WR, wr2, "WR2")
        WK2 = arena.alloc(NDC * D, BF16); load_weight(WK2, wk2, "WK2")
        WV2 = arena.alloc(NDC * D, BF16); load_weight(WV2, wv2, "WV2")
        st = STF
        Wd = st + 1
        xh = arena.alloc(NDC * Wd, BF16)
        sq = arena.alloc(NDC * st, BF16)
        lntmp = arena.alloc(NDC * st, BF16)
        xr = arena.alloc(NDC * st, BF16)
        xkk = arena.alloc(NDC * st, BF16)
        rt = arena.alloc(NDC * st, BF16)
        kact = arena.alloc(NDC * st, BF16)
        ytile = arena.alloc(NDC * st, F32)
        stt = mkstats(st)
        OP("dve", lambda e: e.memset(
            xh[:].rearrange("p (c t) -> p c t", c=NDC)[:, :, 0:1], 0.0),
           writes=["xh"], tag="xh0")
        for s in range(T // st):
            t0 = s * st
            xk = gkeys(t0, st)
            emit_stats(t0, st, lambda dc=None, t0=t0: xres3(t0, st, dc), xk, sq, stt, 0)
            a_b = stt["absc"][:, 0:st].rearrange("p (c t) -> p c t", c=1).broadcast_to(
                [128, NDC, st])
            s_b = stt["sbsc"][:, 0:st].rearrange("p (c t) -> p c t", c=1).broadcast_to(
                [128, NDC, st])
            OP("dve", lambda e, t0=t0: e.tensor_tensor(
                lntmp[:].rearrange("p (c t) -> p c t", c=NDC),
                xres3(t0, st), a_b, ALU.mult),
               reads=xk + ["absc"], writes=["lntmp"], tag="f_ap1")
            OP("dve", lambda e: e.tensor_tensor(
                xh[:].rearrange("p (c t) -> p c t", c=NDC)[:, :, 1:Wd],
                lntmp[:].rearrange("p (c t) -> p c t", c=NDC), s_b, ALU.add),
               reads=["lntmp", "sbsc"], writes=["xh"], tag="f_ap2")
            mur3 = pf["mur"][:].rearrange("p (c x) -> p c x", x=1).broadcast_to(
                [128, NDC, st])
            muk3 = pf["muk"][:].rearrange("p (c x) -> p c x", x=1).broadcast_to(
                [128, NDC, st])
            xcur = xh[:].rearrange("p (c t) -> p c t", c=NDC)[:, :, 1:Wd]
            xprv = xh[:].rearrange("p (c t) -> p c t", c=NDC)[:, :, 0:Wd - 1]
            OP("dve", lambda e: e.tensor_tensor(
                lntmp[:].rearrange("p (c t) -> p c t", c=NDC), xcur, xprv,
                ALU.subtract),
               reads=["xh"], writes=["lntmp"], tag="f_xd")
            OP("dve", lambda e, mur3=mur3: e.tensor_tensor(
                sq[:].rearrange("p (c t) -> p c t", c=NDC),
                lntmp[:].rearrange("p (c t) -> p c t", c=NDC), mur3, ALU.mult),
               reads=["lntmp", "pf_mur"], writes=["sq"], tag="f_xr1")
            OP("dve", lambda e: e.tensor_tensor(
                xr[:].rearrange("p (c t) -> p c t", c=NDC),
                sq[:].rearrange("p (c t) -> p c t", c=NDC), xprv, ALU.add),
               reads=["sq", "xh"], writes=["xr"], tag="f_xr2")
            OP("dve", lambda e, muk3=muk3: e.tensor_tensor(
                sq[:].rearrange("p (c t) -> p c t", c=NDC),
                lntmp[:].rearrange("p (c t) -> p c t", c=NDC), muk3, ALU.mult),
               reads=["lntmp", "pf_muk"], writes=["sq"], tag="f_xk1")
            OP("dve", lambda e: e.tensor_tensor(
                xkk[:].rearrange("p (c t) -> p c t", c=NDC),
                sq[:].rearrange("p (c t) -> p c t", c=NDC), xprv, ALU.add),
               reads=["sq", "xh"], writes=["xkk"], tag="f_xk2")
            OP("dve", lambda e: e.tensor_copy(
                xh[:].rearrange("p (c t) -> p c t", c=NDC)[:, :, 0:1],
                xh[:].rearrange("p (c t) -> p c t", c=NDC)[:, :, Wd - 1:Wd]),
               reads=["xh"], writes=["xh"], tag="f_carry")
            for oc in range(NDC):
                bank = 1 + (oc % 2)
                for kc in range(NDC):
                    OP("te", lambda e, oc=oc, kc=kc, bank=bank: e.matmul(
                        pb[bank][:, 0:st],
                        WR[:, kc * D + oc * 128: kc * D + (oc + 1) * 128],
                        xr[:, kc * st:(kc + 1) * st],
                        start=(kc == 0), stop=(kc == NDC - 1)),
                       reads=["WR2", "xr"], writes=[f"pb{bank}"], tag="f_r",
                       mmkey=(0, 128))
                OP("act", lambda e, oc=oc, bank=bank: e.activation(
                    rt[:, oc * st:(oc + 1) * st], pb[bank][:, 0:st], AF.Exp,
                    scale=-1.0),
                   reads=[f"pb{bank}"], writes=[f"rt{oc}"], tag="f_re")
                OP("dve", lambda e, oc=oc: e.tensor_scalar_add(
                    rt[:, oc * st:(oc + 1) * st], rt[:, oc * st:(oc + 1) * st],
                    1.0),
                   reads=[f"rt{oc}"], writes=[f"rt{oc}"], tag="f_r1")
                OP("act", lambda e, oc=oc: e.activation(
                    rt[:, oc * st:(oc + 1) * st], rt[:, oc * st:(oc + 1) * st],
                    AF.Ln),
                   reads=[f"rt{oc}"], writes=[f"rt{oc}"], tag="f_rl")
                OP("act", lambda e, oc=oc: e.activation(
                    rt[:, oc * st:(oc + 1) * st], rt[:, oc * st:(oc + 1) * st],
                    AF.Exp, scale=-1.0),
                   reads=[f"rt{oc}"], writes=[f"rt{oc}"], tag="f_rx")
            for oc in range(NDC):
                bank = 1 + (oc % 2)
                for kc in range(NDC):
                    OP("te", lambda e, oc=oc, kc=kc, bank=bank: e.matmul(
                        pb[bank][:, 0:st],
                        WK2[:, kc * D + oc * 128: kc * D + (oc + 1) * 128],
                        xkk[:, kc * st:(kc + 1) * st],
                        start=(kc == 0), stop=(kc == NDC - 1)),
                       reads=["WK2", "xkk"], writes=[f"pb{bank}"], tag="f_k",
                       mmkey=(0, 128))
                OP("dve", lambda e, oc=oc, bank=bank: e.tensor_scalar_max(
                    sq[:, oc * st:(oc + 1) * st], pb[bank][:, 0:st], 0.0),
                   reads=[f"pb{bank}"], writes=[f"sqo{oc}"], tag="f_km")
                OP("act", lambda e, oc=oc: e.activation(
                    kact[:, oc * st:(oc + 1) * st], sq[:, oc * st:(oc + 1) * st],
                    AF.Square),
                   reads=[f"sqo{oc}"], writes=[f"kact{oc}"], tag="f_k2")
            for oc in range(NDC):
                bank = 1 + (oc % 2)
                for kc in range(NDC):
                    OP("te", lambda e, oc=oc, kc=kc, bank=bank: e.matmul(
                        pb[bank][:, 0:st],
                        WV2[:, kc * D + oc * 128: kc * D + (oc + 1) * 128],
                        kact[:, kc * st:(kc + 1) * st],
                        start=(kc == 0), stop=(kc == NDC - 1)),
                       reads=["WV2", f"kact{kc}"], writes=[f"pb{bank}"], tag="f_v",
                       mmkey=(0, 128))
                OP("dve", lambda e, oc=oc, bank=bank: e.tensor_tensor(
                    lntmp[:, oc * st:(oc + 1) * st], pb[bank][:, 0:st],
                    rt[:, oc * st:(oc + 1) * st], ALU.mult),
                   reads=[f"pb{bank}", f"rt{oc}"], writes=["lntmp"], tag="f_vm")
                OP("dve", lambda e, oc=oc, t0=t0: e.tensor_tensor(
                    ytile[:, oc * st:(oc + 1) * st],
                    xres[:, oc * T + t0:oc * T + t0 + st],
                    lntmp[:, oc * st:(oc + 1) * st], ALU.add),
                   reads=["lntmp"] + xk, writes=["ytile"], tag="f_y")
            OP("sy", lambda e, t0=t0: e.dma_start(
                out=y_fm.rearrange("(c p) t -> p c t", p=128)[:, :, t0:t0 + st],
                in_=ytile[:].rearrange("p (c t) -> p c t", c=NDC)),
               reads=["ytile"], writes=[f"y:{t0 // 512}"], dma=True, tag="f_yd")
        arena.pop()

    def emit_out():
        arena.mark()
        ytile = arena.alloc(NDC * 512, F32)
        for s in range(8):
            t0 = s * 512
            OP("act", lambda e, t0=t0: e.activation(
                ytile[:].rearrange("p (c t) -> p c t", c=NDC),
                xres[:].rearrange("p (c t) -> p c t", c=NDC)[:, :, t0:t0 + 512],
                AF.Identity),
               reads=[f"xres:{t0 // 512}"], writes=["ytile"], tag="yt")
            OP("sy", lambda e, t0=t0: e.dma_start(
                out=y_fm.rearrange("(c p) t -> p c t", p=128)[:, :, t0:t0 + 512],
                in_=ytile[:].rearrange("p (c t) -> p c t", c=NDC)),
               reads=["ytile"], writes=[f"y:{t0 // 512}"], dma=True, tag="yd")
        arena.pop()

    OP("act", lambda e: e.add_instruction(mybir.InstLoadActFuncSet(
        name=nc.get_next_instruction_name(), act_func_set_id=6, ins=[], outs=[])),
       tag="tbl6")
    emit_consts()
    emit_param_loads()
    if "l0a" in phases:
        emit_l0a()
        sch.barrier()
    if "l0f" in phases:
        emit_l0f()
        sch.barrier()
    if "l1a" in phases:
        emit_l1a()
        sch.barrier()
    if "l1f" in phases:
        emit_l1f()
        sch.barrier()
    if "l1f" not in phases:
        emit_out()
    OP("sy", lambda e: e.nop(), reads=[f"y:{s}" for s in range(8)], tag="fin")

    with nc.Block() as blk:
        sch.materialize(blk, sems)
    for cm in reversed(ctxs):
        cm.__exit__(None, None, None)
    return nc


def host_inputs(inputs):
    shared = dict(
        wq0=np.ascontiguousarray(np.asarray(inputs["L0_attn_Wq"], np.float32)),
        wk0=np.ascontiguousarray(np.asarray(inputs["L0_attn_Wk"], np.float32)),
        wv0=np.ascontiguousarray(np.asarray(inputs["L0_attn_Wv"], np.float32)),
        wb0=np.ascontiguousarray(np.asarray(inputs["L0_attn_Wbeta"], np.float32)),
        bb0v=np.ascontiguousarray(np.asarray(inputs["L0_attn_b_beta"], np.float32)),
        wo0=np.ascontiguousarray(np.asarray(inputs["L0_attn_Wout"], np.float32)),
        wf8=np.ascontiguousarray(np.asarray(inputs["L0_ffn_weight"], np.float32).T),
        bf0=np.ascontiguousarray(np.asarray(inputs["L0_ffn_bias"], np.float32)),
        wq1=np.ascontiguousarray(np.asarray(inputs["L1_attn_Wq"], np.float32)),
        bq1=np.ascontiguousarray(np.asarray(inputs["L1_attn_bq"], np.float32)),
        wk1=np.ascontiguousarray(np.asarray(inputs["L1_attn_Wk"], np.float32)),
        bk1=np.ascontiguousarray(np.asarray(inputs["L1_attn_bk"], np.float32)),
        wv1=np.ascontiguousarray(np.asarray(inputs["L1_attn_Wv"], np.float32)),
        bv1=np.ascontiguousarray(np.asarray(inputs["L1_attn_bv"], np.float32)),
        wfb=np.ascontiguousarray(np.asarray(inputs["L1_attn_Wfb"], np.float32)),
        bfb=np.ascontiguousarray(np.asarray(inputs["L1_attn_bfb"], np.float32)),
        wo1=np.ascontiguousarray(np.asarray(inputs["L1_attn_Wo"], np.float32)),
        bo1=np.ascontiguousarray(np.asarray(inputs["L1_attn_bo"], np.float32)),
        g1=np.ascontiguousarray(np.asarray(inputs["L1_attn_g1"], np.float32)),
        b1=np.ascontiguousarray(np.asarray(inputs["L1_attn_b1"], np.float32)),
        g2=np.ascontiguousarray(np.asarray(inputs["L1_attn_g2"], np.float32)),
        b2=np.ascontiguousarray(np.asarray(inputs["L1_attn_b2"], np.float32)),
        wr2=np.ascontiguousarray(np.asarray(inputs["L1_ffn_W_r"], np.float32).T),
        wk2=np.ascontiguousarray(np.asarray(inputs["L1_ffn_W_k"], np.float32).T),
        wv2=np.ascontiguousarray(np.asarray(inputs["L1_ffn_W_v"], np.float32).T),
        mur=np.ascontiguousarray(np.asarray(inputs["L1_ffn_mu_r"], np.float32)),
        muk=np.ascontiguousarray(np.asarray(inputs["L1_ffn_mu_k"], np.float32)),
    )
    xs = np.asarray(inputs["L0_attn_x"], np.float32)
    in_maps = []
    for b in range(8):
        m = dict(shared)
        m["x_fm"] = np.ascontiguousarray(xs[b].T)
        in_maps.append(m)
    return in_maps


def kernel(**inputs):
    in_maps = host_inputs(inputs)
    nc = build()
    res = run_bass_kernel_spmd(nc, in_maps, core_ids=list(range(8)))
    out = np.stack([np.ascontiguousarray(res.results[b]["y_fm"].T) for b in range(8)])
    return out.astype(np.float32)
